# version 13
# baseline (speedup 1.0000x reference)
import contextlib
import numpy as np
import concourse.bass as bass
import concourse.mybir as mybir
from concourse.bass_utils import run_bass_kernel_spmd

F32 = mybir.dt.float32
BF16 = mybir.dt.bfloat16
I32 = mybir.dt.int32
AF = mybir.ActivationFunctionType
ALU = mybir.AluOpType
AX = mybir.AxisListType
PI = float(np.pi)
LN1E4 = float(np.log(10000.0))

ENGINES = ("sync", "scalar", "vector", "tensor", "gpsimd")


class Buf:
    __slots__ = ("name", "wtok", "rtoks")

    def __init__(self, name="buf"):
        self.name = name
        self.wtok = None
        self.rtoks = []


class Chan:
    __slots__ = ("name", "sem", "count", "touched", "eng")

    def __init__(self, name, sem, eng):
        self.name = name
        self.sem = sem
        self.count = 0
        self.touched = []
        self.eng = eng


class FW:
    def __init__(self, nc, ctx):
        self.nc = nc
        self.ctx = ctx
        self.streams = {e: [] for e in ENGINES}
        self.esem = {}
        self.epos = {e: 0 for e in ENGINES}
        self._semkey = {}
        self._epoch = -1
        self.wm = {e: {} for e in ENGINES}
        self.chans = []
        self.scope = "x"
        self.new_epoch()

    def new_epoch(self):
        self._epoch += 1
        for e in ENGINES:
            sem = self.ctx.enter_context(self.nc.semaphore(f"prod_{e}_{self._epoch}"))
            self.esem[e] = sem
            self.epos[e] = 0
            self._semkey[id(sem)] = ("e", e, self._epoch)

    def buf(self, name="buf"):
        return Buf(name)

    def chan(self, name, eng):
        sem = self.ctx.enter_context(self.nc.semaphore(f"ch_{name}"))
        c = Chan(name, sem, eng)
        self._semkey[id(sem)] = ("c", name)
        self.chans.append(c)
        return c

    def _wait(self, eng, tok):
        if tok is None:
            return
        sem, val = tok
        key = self._semkey[id(sem)]
        w = self.wm[eng].get(key, 0)
        if w >= val:
            return
        self.wm[eng][key] = val

        def f(h, sem=sem, val=val):
            h.wait_ge(sem, val)

        self.streams[eng].append((self.scope, f))

    def op(self, eng, fn, reads=(), writes=(), dma=None, wtok_only=()):
        toks = []
        for b in reads:
            if b is not None and b.wtok is not None:
                toks.append(b.wtok)
        for b in writes:
            if b is None:
                continue
            if b.wtok is not None:
                toks.append(b.wtok)
            toks.extend(b.rtoks)
        seen = {}
        for tk in toks:
            k = self._semkey[id(tk[0])]
            if k not in seen or seen[k][1] < tk[1]:
                seen[k] = tk
        for tk in seen.values():
            self._wait(eng, tk)
        if dma is not None:
            sem = dma.sem
            dma.count += 16
            val = dma.count
            amt = 16
            still = []
            for b in dma.touched:
                hit = False
                if b.wtok is not None and b.wtok[0] is sem:
                    b.wtok = (sem, val)
                    hit = True
                nr = []
                for tk in b.rtoks:
                    if tk[0] is sem:
                        tk = (sem, val)
                        hit = True
                    nr.append(tk)
                b.rtoks = nr
                if hit:
                    still.append(b)
            dma.touched = still
        else:
            sem = self.esem[eng]
            self.epos[eng] += 1
            val = self.epos[eng]
            amt = 1

        def f(h, fn=fn, sem=sem, amt=amt):
            fn(h).then_inc(sem, amt)

        self.streams[eng].append((self.scope, f))
        tok = (sem, val)
        for b in writes:
            if b is not None:
                b.wtok = tok
                b.rtoks = []
                if dma is not None:
                    dma.touched.append(b)
        for b in wtok_only:
            if b is not None:
                b.wtok = tok
                if dma is not None:
                    dma.touched.append(b)
        for b in reads:
            if b is not None:
                b.rtoks.append(tok)
                if dma is not None:
                    dma.touched.append(b)
        return tok

    def drain_chans(self):
        for c in self.chans:
            if c.count > 0:
                def f(h, sem=c.sem, v=c.count):
                    h.wait_ge(sem, v)
                self.streams[c.eng].append((self.scope, f))

    def flush(self, block):
        nc = self.nc
        for name in ENGINES:
            stream = self.streams[name]
            if not stream:
                continue
            dec = getattr(block, name)

            def body(h, stream=tuple(stream)):
                i = 0
                n = len(stream)
                while i < n:
                    tag = stream[i][0]
                    j = i
                    with nc.named_scope(tag):
                        while j < n and stream[j][0] == tag:
                            stream[j][1](h)
                            j += 1
                    i = j

            dec(body)
        self.streams = {e: [] for e in ENGINES}


def sap(t_ap, offset, dims):
    return bass.AP(t_ap.tensor, offset, dims)


def build_program(S=4096):
    NT = S // 128
    QW = 512
    NJ = QW // 128
    NQ = S // QW
    CH = 2 if S >= 2048 else 1
    HT = S // CH
    HTT = HT // QW
    HNT = HT // 128
    MB = 7
    NB = 28 // MB

    nc = bass.Bass("TRN2", target_bir_lowering=False)
    dr = lambda name, shape, dt, kind: nc.dram_tensor(name, shape, dt, kind=kind).ap()
    X = dr("x", [S, 1024], F32, "ExternalInput")
    shapes = {"wq_w": [1024, 1024], "wq_b": [1024], "wk_w": [1024, 1024],
              "wk_b": [1024], "wv_w": [1024, 1024], "wv_b": [1024],
              "wo_w": [1024, 1024], "wo_b": [1024], "ln1_w": [1024],
              "ln2_w": [1024], "fwq": [1024, 1024], "fwk": [1024, 512],
              "fwv": [1024, 512], "fwo": [1024, 1024], "router_w": [1024, 8],
              "w_gate": [8, 1024, 3584], "w_up": [8, 1024, 3584],
              "w_down": [8, 3584, 1024]}
    W = {n: dr(n, s, F32, "ExternalInput") for n, s in shapes.items()}
    OUT = dr("out", [S, 1024], F32, "ExternalOutput")
    X1S = dr("x1_s", [S, 1024], F32, "Internal")
    RS = dr("r_s", [S, 1024], F32, "Internal")
    H2S = dr("h2_s", [S, 1024], BF16, "Internal")

    ctx = contextlib.ExitStack()
    fw = FW(nc, ctx)
    sbA = lambda n, shp, dt: (ctx.enter_context(nc.sbuf_tensor(n, shp, dt)), fw.buf(n))

    ch_cw = fw.chan("cw", "gpsimd")
    ch_x = [fw.chan(f"x{k}", "sync") for k in range(2)]
    ch_xq = [fw.chan(f"xq{k}", "sync") for k in range(2)]
    ch_wo = [fw.chan(f"wo{k}", "gpsimd") for k in range(2)]
    ch_sx1 = fw.chan("sx1", "gpsimd")
    ch_sr = fw.chan("sr", "gpsimd")
    ch_sh2 = fw.chan("sh2", "gpsimd")
    ch_acc = fw.chan("acc", "sync")
    ch_h2l = [fw.chan(f"h2l{k}", "sync") for k in range(2)]
    ch_out = fw.chan("out", "sync")
    ch_wg = [fw.chan(f"wg{k}", "gpsimd") for k in range(14)]
    ch_wu = [fw.chan(f"wu{k}", "gpsimd") for k in range(14)]
    ch_wdd = [fw.chan(f"wd{k}", "gpsimd") for k in range(14)]

    ident_bf, B_idbf = sbA("ident_bf", [128, 128], BF16)
    ident_f, B_idf = sbA("ident_f", [128, 128], F32)
    mask01, B_mask = sbA("mask01", [128, 128], BF16)
    rowidx, B_rowidx = sbA("rowidx", [128, 1], F32)
    ones_bf, B_onesbf = sbA("ones_bf", [128, 128], BF16)
    ones_f, B_onesf = sbA("ones_f", [1, 128], F32)
    invf, B_invf = sbA("invf", [1, 32], F32)
    MkA, B_MkA = sbA("MkA", [16, 1], F32)
    MkB, B_MkB = sbA("MkB", [16, 1], F32)
    epsc, B_eps = sbA("epsc", [128, 2], F32)
    rw_sb, B_rw = sbA("rw_sb", [128, 64], BF16)

    B_x1s = fw.buf("x1_s")
    B_rs = fw.buf("r_s")
    B_h2s = fw.buf("h2_s")

    v, s, t, g, sy = "vector", "scalar", "tensor", "gpsimd", "sync"

    scM = contextlib.ExitStack()
    sbM = lambda n, shp, dt: (scM.enter_context(nc.sbuf_tensor(n, shp, dt)), fw.buf(n))
    psM = lambda n, shp, dt: (scM.enter_context(nc.psum_tensor(n, shp, dt)), fw.buf(n))

    cos_tm, B_cos = sbM("cos_tm", [128, 32 * NT], F32)
    sin_tm, B_sin = sbM("sin_tm", [128, 32 * NT], F32)
    KT, B_KT = sbM("KT", [128, 8 * S], BF16)
    V_tm, B_V = sbM("V_tm", [128, 1040 * NT], BF16)
    brow_pk, B_brow = sbM("brow_pk", [128, 1024], BF16)
    brow_o, B_browo = sbM("brow_o", [1, 1024], BF16)
    _brows = {"wq_b": (brow_pk, 0), "wk_b": (brow_pk, 32), "wv_b": (brow_pk, 64),
              "wo_b": (brow_o, 0)}
    brow = {n: ((lambda tt=tt, r=r: tt[r:r + 1, :])(),
                B_brow if n != "wo_b" else B_browo, r)
            for n, (tt, r) in _brows.items()}
    ln1b, B_ln1b = sbM("ln1b", [128, 1024], BF16)
    ln2b, B_ln2b = sbM("ln2b", [128, 1024], BF16)

    xin = [sbM(f"xin{k}", [128, 1024], F32) for k in range(2)]
    lnx4 = [sbM(f"lnx{k}", [128, 1024], BF16) for k in range(4)]
    lnxT4 = [sbM(f"lnxT{k}", [128, 1024], BF16) for k in range(4)]
    rot2 = [sbM(f"rot{k}", [128, 1024], BF16) for k in range(2)]
    t12, B_t12 = sbM("t12", [128, 1024], F32)
    QT, B_QT = sbM("QT", [128, 8 * QW], BF16)
    A_sb = [sbM(f"Asb{k}", [128, 1024], BF16) for k in range(2)]
    ctxT, B_ctxT = sbM("ctxT", [128, 8 * QW], BF16)
    linvb, B_linvb = sbM("linvb", [64, 512], F32)
    wos = [sbM(f"wos{k}", [128, 1024], BF16) for k in range(2)]
    cols = [[sbM(f"col{p}_{k}", [128, 8], F32) for k in range(4)] for p in range(2)]
    nrm, B_nrm = sbM("nrm", [128, 16], F32)
    ntmp, B_ntmp = sbM("ntmp", [16, 1], F32)
    mq, B_mq = sbM("mq", [16, 1], F32)
    negrow, B_negrow = sbM("negrow", [1, 16], F32)
    kp, B_kp = sbM("kp", [1, 8], F32)
    prow, B_prow = sbM("prow", [1, 8], F32)
    pairb, B_pairb = sbM("pairb", [128, 8], F32)

    PS = [psM(f"PS{k}", [128, 1024], F32) for k in range(4)]
    psb = [PS[k][0][:, :].bitcast(BF16) for k in range(4)]
    tslots = [(sap(psb[0], 0, [[2048, 128], [1, 128]]), PS[0][1]),
              (sap(psb[1], 0, [[2048, 128], [1, 128]]), PS[1][1]),
              (sap(psb[0], 1024, [[2048, 128], [1, 128]]), PS[0][1]),
              (sap(psb[1], 1024, [[2048, 128], [1, 128]]), PS[1][1])]

    V32 = V_tm[:, :].bitcast(F32)
    y1B = sap(V32, 8320, [[16640, 128], [1, 1024]])
    B_y1 = fw.buf("y1B")
    x2b = sap(V32, 9344, [[16640, 128], [1, 1024]])
    B_x2 = fw.buf("x2b")
    h2b = sap(V_tm[:, :], 20736, [[33280, 128], [1, 1024]])
    B_h2 = fw.buf("h2b")

    def front(ph, gg, jj):
        i = 4 * gg + jj
        p = gg % 2
        col, B_col = cols[p][jj]
        xb, B_xb = xin[jj % 2]
        lx, B_lx = lnx4[jj]
        src = X if ph == "A" else X1S
        B_src = None if ph == "A" else B_x1s
        fw.op(sy, lambda h: h.dma_start(xb[:, :], src[128 * i:128 * i + 128, :]),
              reads=[B_src] if B_src else [], writes=[B_xb], dma=ch_x[jj % 2])
        fw.op(v, lambda h: h.tensor_reduce(col[:, 0:1], xb[:, :], AX.X, ALU.add),
              reads=[B_xb], writes=[B_col])
        fw.op(v, lambda h: h.tensor_scalar(col[:, 1:2], col[:, 0:1], 1.0 / 1024, None, ALU.mult),
              reads=[B_col], writes=[B_col])
        fw.op(s, lambda h: h.activation(lx[:, :], xb[:, :], AF.Square, accum_out=col[:, 2:3]),
              reads=[B_xb], writes=[B_lx, B_col])
        fw.op(v, lambda h: h.tensor_tensor(col[:, 3:4], col[:, 1:2], col[:, 1:2], ALU.mult),
              reads=[B_col], writes=[B_col])
        fw.op(v, lambda h: h.scalar_tensor_tensor(col[:, 3:4], col[:, 2:3], 1.0 / 1024,
                                                  col[:, 3:4], ALU.mult, ALU.subtract),
              reads=[B_col], writes=[B_col])
        fw.op(s, lambda h: h.activation(col[:, 4:5], col[:, 3:4], AF.Sqrt, bias=epsc[:, 0:1]),
              reads=[B_col, B_eps], writes=[B_col])
        fw.op(v, lambda h: h.reciprocal(col[:, 4:5], col[:, 4:5]),
              reads=[B_col], writes=[B_col])
        if ph == "A":
            fw.op(v, lambda h: h.tensor_scalar(lx[:, :], xb[:, :], col[:, 1:2],
                                               col[:, 4:5], ALU.subtract, op1=ALU.mult),
                  reads=[B_xb, B_col], writes=[B_lx])
        else:
            fw.op(v, lambda h: h.tensor_tensor(col[:, 6:7], col[:, 4:5], col[:, 4:5], ALU.mult),
                  reads=[B_col], writes=[B_col])
            fw.op(v, lambda h: h.tensor_tensor(col[:, 6:7], col[:, 3:4], col[:, 6:7], ALU.mult),
                  reads=[B_col], writes=[B_col])
            fw.op(s, lambda h: h.activation(col[:, 6:7], col[:, 6:7], AF.Sqrt, bias=epsc[:, 1:2]),
                  reads=[B_col, B_eps], writes=[B_col])
            fw.op(v, lambda h: h.reciprocal(col[:, 6:7], col[:, 6:7]),
                  reads=[B_col], writes=[B_col])
            fw.op(v, lambda h: h.tensor_tensor(col[:, 7:8], col[:, 4:5], col[:, 6:7], ALU.mult),
                  reads=[B_col], writes=[B_col])
            fw.op(v, lambda h: h.tensor_scalar(t12[:, :], xb[:, :], col[:, 1:2],
                                               col[:, 7:8], ALU.subtract, op1=ALU.mult),
                  reads=[B_xb, B_col], writes=[B_t12])
            fw.op(v, lambda h: h.tensor_tensor(lx[:, :], t12[:, :], ln1b[:, :], ALU.mult),
                  reads=[B_ln1b, B_t12], writes=[B_lx])

    def transposes(src, B_src, nch, emit_out, base=0, slots=None):
        slots = slots or tslots
        for c in range(nch):
            pv, Bp = slots[c % len(slots)]
            fw.op(t, lambda h, c=c, pv=pv: h.transpose(
                pv, src[:, base + 128 * c:base + 128 * c + 128], ident_bf[:, :]),
                reads=[B_src, B_idbf], writes=[Bp])
            emit_out(c, pv, Bp)

    def rope(src_ap, B_srcs, dst, B_dst, i, nh):
        co = 32 * i
        cosap = sap(cos_tm[:, :], co, [[32 * NT, 128], [0, nh], [1, 32]])
        sinap = sap(sin_tm[:, :], co, [[32 * NT, 128], [0, nh], [1, 32]])
        flen = src_ap.ap[0][0]
        off0 = src_ap.offset
        srcf = sap(src_ap, off0 + 0, [[flen, 128], [64, nh], [1, 32]])
        srcs = sap(src_ap, off0 + 32, [[flen, 128], [64, nh], [1, 32]])
        dlen = dst[:, :].ap[0][0]
        dstf = sap(dst[:, :], 0, [[dlen, 128], [64, nh], [1, 32]])
        dsts = sap(dst[:, :], 32, [[dlen, 128], [64, nh], [1, 32]])
        t1v = sap(t12[:, :], 0, [[1024, 128], [32, nh], [1, 32]])
        t2v = sap(t12[:, :], 512, [[1024, 128], [32, nh], [1, 32]])
        fw.op(v, lambda h: h.tensor_tensor(t1v, srcf, cosap, ALU.mult),
              reads=list(B_srcs) + [B_cos], writes=[B_t12])
        fw.op(v, lambda h: h.tensor_tensor(t2v, srcs, sinap, ALU.mult),
              reads=list(B_srcs) + [B_sin], wtok_only=[B_t12])
        fw.op(v, lambda h: h.tensor_tensor(dstf, t1v, t2v, ALU.subtract),
              reads=[B_t12], writes=[B_dst])
        fw.op(v, lambda h: h.tensor_tensor(t1v, srcs, cosap, ALU.mult),
              reads=list(B_srcs) + [B_cos], writes=[B_t12])
        fw.op(v, lambda h: h.tensor_tensor(t2v, srcf, sinap, ALU.mult),
              reads=list(B_srcs) + [B_cos], wtok_only=[B_t12])
        fw.op(v, lambda h: h.tensor_tensor(dsts, t1v, t2v, ALU.add),
              reads=[B_t12], writes=[B_dst])

    def head_norms(nh, Mk, B_Mk, sqf, B_sqf, pnv, B_pn, first, rsrc, B_rsrc):
        fw.op(s, lambda h: h.activation(sqf[:, 0:64 * nh], rsrc[:, 0:64 * nh], AF.Square),
              reads=[B_rsrc], writes=[B_sqf])
        fw.op(v, lambda h: h.tensor_reduce(
            nrm[:, 0:nh], sap(sqf[:, :], 0, [[sqf[:, :].ap[0][0], 128], [64, nh], [1, 64]]),
            AX.X, ALU.add), reads=[B_sqf], writes=[B_nrm])
        fw.op(s, lambda h: h.activation(nrm[:, 0:nh], nrm[:, 0:nh], AF.Sqrt),
              reads=[B_nrm], writes=[B_nrm])
        fw.op(t, lambda h: h.transpose(pnv, nrm[:, 0:nh], ident_f[:, :]),
              reads=[B_nrm, B_idf], writes=[B_pn])
        if first:
            fw.op(v, lambda h: h.reduce_max(Mk[0:nh, 0:1], pnv, AX.X),
                  reads=[B_pn], writes=[B_Mk])
        else:
            fw.op(v, lambda h: h.reduce_max(ntmp[0:nh, 0:1], pnv, AX.X),
                  reads=[B_pn], writes=[B_ntmp])
            fw.op(v, lambda h: h.tensor_max(Mk[0:nh, :], Mk[0:nh, :], ntmp[0:nh, 0:1]),
                  reads=[B_ntmp, B_Mk], writes=[B_Mk])

    fw.scope = "pro"
    _iotaF, B_iotaF = sbM("p_iotaF", [128, 128], F32)
    _wrk, B_wrk = sbM("p_wrk", [128, 32], F32)
    _angw, B_ang = sbM("p_ang", [128, 32], F32)
    _wri, B_wri = sbM("p_wri", [128, 32], I32)
    _invfF, B_invfF = sbM("p_invf", [128, 32], F32)
    _rowcol, B_rowcol = sbM("p_rowcol", [128, 1], F32)
    _ang0, B_ang0 = sbM("p_ang0", [128, 32], F32)
    _posr, B_posr = sbM("p_posr", [1, 128], F32)
    iotaF = _iotaF[:, :]
    wrk = _wrk[:, :]
    angw = _angw[:, :]
    wri = _wri[:, :]
    invf_full = _invfF[:, :]
    rowcol = _rowcol[:, :]
    ang0 = _ang0[:, :]
    posr = _posr[:, :]
    import os
    if os.environ.get("SBUF_DEBUG"):
        print("SBUF remaining at peak:", nc.sbuf_bytes_remaining)

    fw.op(g, lambda h: h.memset(epsc[:, 0:1], 1e-5), writes=[B_eps])
    fw.op(g, lambda h: h.memset(epsc[:, 1:2], 1e-6), writes=[B_eps])
    fw.op(g, lambda h: h.iota(rowidx[:, :], [[0, 1]], channel_multiplier=1,
                              allow_small_or_imprecise_dtypes=True), writes=[B_rowidx])
    fw.op(g, lambda h: h.iota(iotaF, [[1, 128]], channel_multiplier=0,
                              allow_small_or_imprecise_dtypes=True), writes=[B_iotaF])
    fw.op(g, lambda h: h.iota(invf_full, [[1, 32]], channel_multiplier=0,
                              allow_small_or_imprecise_dtypes=True), writes=[B_invfF])
    fw.op(g, lambda h: h.memset(ones_bf[:, :], 1.0), writes=[B_onesbf])
    fw.op(g, lambda h: h.memset(ones_f[:, :], 1.0), writes=[B_onesf])
    fw.op(g, lambda h: h.memset(MkA[:, :], -3.0e38), writes=[B_MkA])
    fw.op(g, lambda h: h.memset(MkB[:, :], -3.0e38), writes=[B_MkB])
    fw.op(v, lambda h: h.tensor_scalar(ident_bf[:, :], iotaF, rowidx[:, :], None, ALU.is_equal),
          reads=[B_iotaF, B_rowidx], writes=[B_idbf])
    fw.op(v, lambda h: h.tensor_scalar(ident_f[:, :], iotaF, rowidx[:, :], None, ALU.is_equal),
          reads=[B_iotaF, B_rowidx], writes=[B_idf])
    fw.op(v, lambda h: h.tensor_scalar(mask01[:, :], iotaF, rowidx[:, :], None, ALU.is_ge),
          reads=[B_iotaF, B_rowidx], writes=[B_mask])
    fw.op(s, lambda h: h.activation(invf_full, invf_full, AF.Exp, scale=-LN1E4 / 32.0),
          reads=[B_invfF], writes=[B_invfF])
    for n in ("wq_b", "wk_b", "wv_b", "wo_b"):
        fw.op(g, lambda h, n=n: h.dma_start(brow[n][0],
                                            W[n].rearrange("(o n) -> o n", o=1)),
              writes=[brow[n][1]], dma=ch_cw)
    fw.op(g, lambda h: h.dma_start(ln1b[:, :], sap(W["ln1_w"], 0, [[0, 128], [1, 1024]])),
          writes=[B_ln1b], dma=ch_cw)
    fw.op(g, lambda h: h.dma_start(ln2b[:, :], sap(W["ln2_w"], 0, [[0, 128], [1, 1024]])),
          writes=[B_ln2b], dma=ch_cw)
    fw.op(g, lambda h: h.dma_start(rw_sb[:, :].rearrange("p (c n) -> p c n", n=8),
                                   W["router_w"].rearrange("(c p) n -> p c n", p=128)),
          writes=[B_rw], dma=ch_cw)
    for bi in range(NT):
        fw.op(g, lambda h, bi=bi: h.iota(posr, [[1, 128]], base=128 * bi,
                                         channel_multiplier=0,
                                         allow_small_or_imprecise_dtypes=True),
              writes=[B_posr])
        fw.op(t, lambda h: h.matmul(PS[0][0][:, 0:32], posr, invf_full[0:1, :],
                                    start=True, stop=True),
              reads=[B_posr, B_invfF], writes=[PS[0][1]])
        fw.op(v, lambda h: h.tensor_copy(ang0, PS[0][0][:, 0:32]),
              reads=[PS[0][1]], writes=[B_ang0])
        fw.op(v, lambda h: h.tensor_scalar(wrk, ang0, 1.0 / (2 * PI), None, ALU.mult),
              reads=[B_ang0], writes=[B_wrk])
        fw.op(v, lambda h: h.tensor_copy(wri, wrk), reads=[B_wrk], writes=[B_wri])
        fw.op(v, lambda h: h.tensor_copy(wrk, wri), reads=[B_wri], writes=[B_wrk])
        fw.op(v, lambda h: h.scalar_tensor_tensor(angw, wrk, -2 * PI, ang0,
                                                  ALU.mult, ALU.add),
              reads=[B_wrk, B_ang0], writes=[B_ang])
        fw.op(v, lambda h: h.tensor_scalar(wrk, angw, PI, None, ALU.is_gt),
              reads=[B_ang], writes=[B_wrk])
        fw.op(v, lambda h: h.scalar_tensor_tensor(angw, wrk, -2 * PI, angw,
                                                  ALU.mult, ALU.add),
              reads=[B_wrk, B_ang], writes=[B_ang])
        fw.op(v, lambda h: h.tensor_scalar(wrk, angw, -PI, None, ALU.is_lt),
              reads=[B_ang], writes=[B_wrk])
        fw.op(v, lambda h: h.scalar_tensor_tensor(angw, wrk, 2 * PI, angw,
                                                  ALU.mult, ALU.add),
              reads=[B_wrk, B_ang], writes=[B_ang])
        fw.op(s, lambda h, bi=bi: h.activation(sin_tm[:, 32 * bi:32 * bi + 32],
                                               angw, AF.Sin),
              reads=[B_ang], writes=[B_sin])
        fw.op(v, lambda h: h.tensor_scalar(wrk, angw, PI / 2, None, ALU.add),
              reads=[B_ang], writes=[B_wrk])
        fw.op(v, lambda h: h.tensor_scalar(angw, wrk, PI, None, ALU.is_gt),
              reads=[B_wrk], writes=[B_ang])
        fw.op(v, lambda h: h.scalar_tensor_tensor(wrk, angw, -2 * PI, wrk,
                                                  ALU.mult, ALU.add),
              reads=[B_ang, B_wrk], writes=[B_wrk])
        fw.op(s, lambda h, bi=bi: h.activation(cos_tm[:, 32 * bi:32 * bi + 32],
                                               wrk, AF.Sin),
              reads=[B_wrk], writes=[B_cos])

    def prep_tm(ph, gg):
        isA = ph == "A"
        fw.scope = "p" + ph
        for jj in range(NJ):
            i = 4 * gg + jj
            lx, B_lx = lnx4[jj]
            lT, B_lT = lnxT4[jj]
            transposes(lx, B_lx, 8,
                       lambda c, pv, Bp, lT=lT, B_lT=B_lT: fw.op(
                           v, lambda h, c=c, pv=pv: h.tensor_copy(
                               lT[:, 128 * c:128 * c + 128], pv),
                           reads=[Bp], writes=[B_lT]))
        if isA:
            passes = (("wk_w", "wk_b", True), ("wv_w", "wv_b", False))
        else:
            passes = (("fwk", None, True),)
        for (wname, bname, is_k) in passes:
            for c in range(8):
                wsl, B_wsl = wos[c % 2]
                if isA:
                    fw.op(g, lambda h, c=c, wsl=wsl, wname=wname: h.dma_start(
                        wsl[:, :], W[wname][128 * c:128 * c + 128, :]),
                        writes=[B_wsl], dma=ch_wo[c % 2])
                else:
                    fw.op(g, lambda h, c=c, wsl=wsl: h.dma_start(
                        wsl[:, 0:512], W["fwk"][128 * c:128 * c + 128, :]),
                        writes=[B_wsl], dma=ch_wo[c % 2])
                    fw.op(g, lambda h, c=c, wsl=wsl: h.dma_start(
                        wsl[:, 512:1024], W["fwv"][128 * c:128 * c + 128, :]),
                        wtok_only=[B_wsl], dma=ch_wo[c % 2])
                for jj in range(NJ):
                    PP, B_PP = PS[jj]
                    lT, B_lT = lnxT4[jj]
                    def kmm(h, c=c, jj=jj, wsl=wsl, PP=PP, lT=lT, bname=bname, isA=isA):
                        for dh in range(2):
                            first = (c == 0)
                            if first and isA:
                                r = brow[bname][2]
                                h.matmul(PP[:, 512 * dh:512 * dh + 512],
                                         ones_bf[r:r + 1, :],
                                         brow[bname][0][0:1, 512 * dh:512 * dh + 512],
                                         start=True, stop=False)
                                first = False
                            ii = h.matmul(PP[:, 512 * dh:512 * dh + 512],
                                          lT[:, 128 * c:128 * c + 128],
                                          wsl[:, 512 * dh:512 * dh + 512],
                                          start=first, stop=(c == 7))
                        return ii
                    rds = [B_lT, B_wsl] + ([brow[bname][1], B_onesbf] if (c == 0 and isA) else [])
                    if c == 0:
                        fw.op(t, kmm, reads=rds, writes=[B_PP])
                    else:
                        fw.op(t, kmm, reads=rds, wtok_only=[B_PP])
            for jj in range(NJ):
                i = 4 * gg + jj
                PP, B_PP = PS[jj]
                lx, B_lx = lnx4[jj]
                if is_k:
                    nh = 16 if isA else 8
                    rot, B_rot = rot2[jj % 2]
                    rope(PP[:, :], [B_PP], rot, B_rot, i, nh)
                    head_norms(nh, MkA if isA else MkB, B_MkA if isA else B_MkB,
                               lx, B_lx, PP[0:nh, 0:128], B_PP, first=(i == 0),
                               rsrc=rot, B_rsrc=B_rot)
                    own = [(sap(psb[jj], 0, [[2048, 128], [1, 128]]), B_PP),
                           (sap(psb[jj], 1024, [[2048, 128], [1, 128]]), B_PP)]
                    if isA:
                        transposes(rot, B_rot, 8,
                                   lambda c, pv, Bp, i=i: fw.op(
                                       v, lambda h, c=c, pv=pv, i=i: h.tensor_copy(
                                           KT[:, S * c + 128 * i:S * c + 128 * i + 128], pv),
                                       reads=[Bp], writes=[B_KT]),
                                   slots=own)
                    else:
                        fw.op(v, lambda h, jj=jj, i=i, PP=PP: h.tensor_copy(
                            sap(V_tm[:, :], 520 * i, [[1040 * NT, 128], [65, 8], [1, 64]]),
                            sap(PP[:, :], 512, [[1024, 128], [64, 8], [1, 64]])),
                            reads=[B_PP], writes=[B_V])
                        fw.op(v, lambda h, i=i: h.memset(
                            sap(V_tm[:, :], 520 * i + 64, [[1040 * NT, 128], [65, 8], [1, 1]]), 1.0),
                            writes=[B_V])
                        for cc in range(4):
                            pv, Bp = own[cc % 2]
                            fw.op(t, lambda h, cc=cc, pv=pv, rot=rot: h.transpose(
                                pv, rot[:, 128 * cc:128 * cc + 128], ident_bf[:, :]),
                                reads=[B_rot, B_idbf], writes=[Bp])
                            for gg2 in range(2):
                                kvh = 2 * cc + gg2
                                sub = pv[64 * gg2:64 * gg2 + 64, 0:128]
                                for half in range(2):
                                    fw.op(v, lambda h, kvh=kvh, half=half, sub=sub, i=i:
                                          h.tensor_copy(
                                              KT[64 * half:64 * half + 64,
                                                 S * kvh + 128 * i:S * kvh + 128 * i + 128],
                                              sub),
                                          reads=[Bp], writes=[B_KT])
                else:
                    for dh in range(2):
                        fw.op(v, lambda h, dh=dh, i=i, PP=PP: h.tensor_copy(
                            sap(V_tm[:, :], 1040 * i + 65 * 8 * dh, [[1040 * NT, 128], [65, 8], [1, 64]]),
                            sap(PP[:, :], 512 * dh, [[1024, 128], [64, 8], [1, 64]])),
                            reads=[B_PP], writes=[B_V])
                    fw.op(v, lambda h, i=i: h.memset(
                        sap(V_tm[:, :], 1040 * i + 64, [[1040 * NT, 128], [65, 16], [1, 1]]), 1.0),
                        writes=[B_V])

    def qstage(ph, tq):
        isA = ph == "A"
        fw.scope = "q" + ph
        Mk_t, B_Mk_t = (MkA, B_MkA) if isA else (MkB, B_MkB)
        nkv = 16 if isA else 8
        fw.op(t, lambda h, nkv=nkv: h.transpose(PS[2][0][0:1, 0:nkv], Mk_t[0:nkv, 0:1],
                                                ident_f[0:nkv, 0:nkv]),
              reads=[B_Mk_t, B_idf], writes=[PS[2][1]])
        fw.op(v, lambda h, nkv=nkv: h.tensor_copy(negrow[0:1, 0:nkv], PS[2][0][0:1, 0:nkv]),
              reads=[PS[2][1]], writes=[B_negrow])
        if isA:
            fw.op(v, lambda h: h.tensor_tensor(kp[:, :],
                                               sap(negrow[:, :], 0, [[16, 1], [2, 8]]),
                                               sap(negrow[:, :], 1, [[16, 1], [2, 8]]),
                                               ALU.max),
                  reads=[B_negrow], writes=[B_kp])
        else:
            fw.op(v, lambda h: h.tensor_copy(kp[:, :], negrow[0:1, 0:8]),
                  reads=[B_negrow], writes=[B_kp])
        fw.op(g, lambda h: h.memset(mq[:, :], -3.0e38), writes=[B_mq])
        wqn = "wq_w" if isA else "fwq"
        for c in range(8):
            wsl, B_wsl = wos[c % 2]
            fw.op(g, lambda h, c=c, wsl=wsl, wqn=wqn: h.dma_start(
                wsl[:, :], W[wqn][128 * c:128 * c + 128, :]),
                writes=[B_wsl], dma=ch_wo[c % 2])
            for jj in range(NJ):
                PQ_, B_PQ_ = PS[jj]
                lT, B_lT = lnxT4[jj]
                def qmm(h, c=c, jj=jj, wsl=wsl, PQ_=PQ_, lT=lT):
                    for dh in range(2):
                        first = (c == 0)
                        if first and isA:
                            h.matmul(PQ_[:, 512 * dh:512 * dh + 512], ones_bf[0:1, :],
                                     brow["wq_b"][0][0:1, 512 * dh:512 * dh + 512],
                                     start=True, stop=False)
                            first = False
                        ii = h.matmul(PQ_[:, 512 * dh:512 * dh + 512],
                                      lT[:, 128 * c:128 * c + 128],
                                      wsl[:, 512 * dh:512 * dh + 512],
                                      start=first, stop=(c == 7))
                    return ii
                rds = [B_lT, B_wsl] + ([brow["wq_b"][1], B_onesbf]
                                       if (c == 0 and isA) else [])
                if c == 0:
                    fw.op(t, qmm, reads=rds, writes=[B_PQ_])
                else:
                    fw.op(t, qmm, reads=rds, wtok_only=[B_PQ_])
        for jj in range(NJ):
            j = NJ * tq + jj
            PQ_, B_PQ_ = PS[jj]
            lx, B_lx = lnx4[jj]
            rot, B_rot = rot2[jj % 2]
            rope(PQ_[:, :], [B_PQ_], rot, B_rot, j, 16)
            head_norms(16, mq, B_mq, lx, B_lx, PQ_[0:16, 0:128], B_PQ_, first=False,
                       rsrc=rot, B_rsrc=B_rot)
            pqb = PQ_[:, :].bitcast(BF16)
            tsl_q = [(sap(pqb, 0, [[2048, 128], [1, 128]]), B_PQ_),
                     (sap(pqb, 1024, [[2048, 128], [1, 128]]), B_PQ_)]
            for c in range(8):
                pv, Bp = tsl_q[c % 2]
                fw.op(t, lambda h, c=c, pv=pv, rot=rot: h.transpose(
                    pv, rot[:, 128 * c:128 * c + 128], ident_bf[:, :]),
                    reads=[B_rot, B_idbf], writes=[Bp])
                fw.op(v, lambda h, c=c, pv=pv, jj=jj: h.tensor_copy(
                    QT[:, QW * c + 128 * jj:QW * c + 128 * jj + 128], pv),
                    reads=[Bp], writes=[B_QT])
        fw.op(t, lambda h: h.transpose(PS[2][0][0:1, 0:16], mq[0:16, 0:1],
                                       ident_f[0:16, 0:16]),
              reads=[B_mq, B_idf], writes=[PS[2][1]])
        fw.op(v, lambda h: h.tensor_copy(negrow[:, :], PS[2][0][0:1, 0:16]),
              reads=[PS[2][1]], writes=[B_negrow])
        fw.op(v, lambda h: h.tensor_tensor(prow[:, :],
                                           sap(negrow[:, :], 0, [[16, 1], [2, 8]]),
                                           sap(negrow[:, :], 1, [[16, 1], [2, 8]]),
                                           ALU.max),
              reads=[B_negrow], writes=[B_prow])
        fw.op(v, lambda h: h.tensor_tensor(prow[:, :], prow[:, :], kp[:, :], ALU.mult),
              reads=[B_kp, B_prow], writes=[B_prow])
        fw.op(v, lambda h: h.tensor_scalar(prow[:, :], prow[:, :], -0.125, None, ALU.mult),
              reads=[B_prow], writes=[B_prow])
        fw.op(t, lambda h: h.matmul(PS[2][0][:, 512:520], ones_f[:, :], prow[:, :],
                                    start=True, stop=True),
              reads=[B_onesf, B_prow], writes=[PS[2][1]])
        fw.op(v, lambda h: h.tensor_copy(pairb[:, :], PS[2][0][:, 512:520]),
              reads=[PS[2][1]], writes=[B_pairb])

    def pairs(ph, tq):
        isA = ph == "A"
        fw.scope = "s" + ph
        VST = 1040 if isA else 520
        nkb = NJ * (tq + 1)

        def smm_emit(c, kb):
            PSc, B_PSc = PS[kb % 2]
            qoff = 128 * (kb - NJ * tq) if kb >= NJ * tq else 0
            def smm(h, c=c, kb=kb, qoff=qoff, PSc=PSc):
                h.matmul(PSc[:, qoff:512],
                         KT[0:64, S * c + 128 * kb:S * c + 128 * kb + 128],
                         QT[0:64, QW * c + qoff:QW * c + QW],
                         start=True, stop=True, tile_position=(0, 0))
                return h.matmul(PSc[:, 512 + qoff:1024],
                                KT[64:128, S * c + 128 * kb:S * c + 128 * kb + 128],
                                QT[64:128, QW * c + qoff:QW * c + QW],
                                start=True, stop=True, tile_position=(64, 0))
            fw.op(t, smm, reads=[B_KT, B_QT], writes=[B_PSc])

        def finalize(c):
            PC, B_PC = PS[2 + (c % 2)]
            for half in range(2):
                fw.op(v, lambda h, half=half, PC=PC: h.reciprocal(
                    linvb[0:1, :], PC[64:65, 512 * half:512 * half + 512]),
                    reads=[B_PC], writes=[B_linvb])
                fw.op(t, lambda h, half=half, PC=PC: h.matmul(
                    PC[64:128, 512 * half:512 * half + 512],
                    ones_f[0:1, 0:64], linvb[0:1, :], start=True, stop=True,
                    tile_position=(0, 64)),
                    reads=[B_onesf, B_linvb], wtok_only=[B_PC])
                fw.op(v, lambda h, half=half, PC=PC: h.tensor_copy(
                    linvb[:, :], PC[64:128, 512 * half:512 * half + 512]),
                    reads=[B_PC], writes=[B_linvb])
                fw.op(v, lambda h, c=c, half=half, PC=PC: h.tensor_tensor(
                    ctxT[64 * half:64 * half + 64, QW * c:QW * c + QW],
                    PC[0:64, 512 * half:512 * half + 512],
                    linvb[0:64, 0:512], ALU.mult),
                    reads=[B_PC, B_linvb], writes=[B_ctxT])

        smm_emit(0, 0)
        if nkb > 1:
            smm_emit(0, 1)
        for c in range(8):
            PC, B_PC = PS[2 + (c % 2)]
            for kb in range(nkb):
                if kb == 2 and c >= 1:
                    finalize(c - 1)
                PSc, B_PSc = PS[kb % 2]
                qoff = 128 * (kb - NJ * tq) if kb >= NJ * tq else 0
                Asb, B_Asb = A_sb[kb % 2]
                if qoff == 0:
                    fw.op(s, lambda h, c=c, PSc=PSc, Asb=Asb: h.activation(
                        Asb[:, :], PSc[:, :], AF.Exp,
                        bias=pairb[:, c:c + 1], scale=0.125),
                        reads=[B_PSc, B_pairb], writes=[B_Asb])
                else:
                    fw.op(s, lambda h, PSc=PSc, Asb=Asb, c=c, qoff=qoff: h.activation(
                        Asb[:, qoff:512], PSc[:, qoff:512], AF.Exp,
                        bias=pairb[:, c:c + 1], scale=0.125),
                        reads=[B_PSc, B_pairb], writes=[B_Asb])
                    fw.op(s, lambda h, PSc=PSc, Asb=Asb, c=c, qoff=qoff: h.activation(
                        Asb[:, 512 + qoff:1024], PSc[:, 512 + qoff:1024], AF.Exp,
                        bias=pairb[:, c:c + 1], scale=0.125),
                        reads=[B_PSc, B_pairb], writes=[B_Asb])
                if kb >= NJ * tq:
                    for half in range(2):
                        o = 512 * half + qoff
                        fw.op(v, lambda h, o=o, Asb=Asb: h.tensor_tensor(
                            Asb[:, o:o + 128], Asb[:, o:o + 128], mask01[:, :], ALU.mult),
                            reads=[B_mask, B_Asb], writes=[B_Asb])
                hA = 65 * (2 * c) if isA else 65 * c
                hB = 65 * (2 * c + 1) if isA else 65 * c
                def avmm(h, kb=kb, qoff=qoff, Asb=Asb, hA=hA, hB=hB, PC=PC, VST=VST,
                         last=(kb == nkb - 1)):
                    h.matmul(PC[0:65, qoff:512],
                             sap(V_tm[:, :], VST * kb + hA, [[1040 * NT, 128], [1, 65]]),
                             Asb[:, qoff:512], start=(kb == 0), stop=last)
                    return h.matmul(PC[0:65, 512 + qoff:1024],
                                    sap(V_tm[:, :], VST * kb + hB, [[1040 * NT, 128], [1, 65]]),
                                    Asb[:, 512 + qoff:1024], start=(kb == 0), stop=last)
                if kb == 0:
                    fw.op(t, avmm, reads=[B_V, B_Asb], writes=[B_PC])
                else:
                    fw.op(t, avmm, reads=[B_V, B_Asb], wtok_only=[B_PC])
                if kb + 2 < nkb:
                    smm_emit(c, kb + 2)
            if c < 7:
                smm_emit(c + 1, 0)
                if nkb > 1:
                    smm_emit(c + 1, 1)
        finalize(7)

    def wo_evac(ph, tq):
        isA = ph == "A"
        fw.scope = "o" + ph
        won = "wo_w" if isA else "fwo"
        for c in range(8):
            wsl, B_wsl = wos[c % 2]
            fw.op(g, lambda h, c=c, wsl=wsl, won=won: h.dma_start(
                wsl[:, :], W[won][128 * c:128 * c + 128, :]),
                writes=[B_wsl], dma=ch_wo[c % 2])
            for jj in range(NJ):
                PT_, B_PT_ = PS[jj]
                for dh in range(2):
                    def omm(h, c=c, jj=jj, dh=dh, wsl=wsl, PT_=PT_):
                        first = (c == 0)
                        if first and isA:
                            h.matmul(PT_[:, 512 * dh:512 * dh + 512], ones_bf[0:1, :],
                                     brow["wo_b"][0][0:1, 512 * dh:512 * dh + 512],
                                     start=True, stop=False)
                            first = False
                        return h.matmul(PT_[:, 512 * dh:512 * dh + 512],
                                        ctxT[:, QW * c + 128 * jj:QW * c + 128 * jj + 128],
                                        wsl[:, 512 * dh:512 * dh + 512],
                                        start=first, stop=(c == 7))
                    rds = [B_ctxT, B_wsl] + ([brow["wo_b"][1], B_onesbf]
                                             if (c == 0 and isA) else [])
                    if c == 0 and dh == 0:
                        fw.op(t, omm, reads=rds, writes=[B_PT_])
                    else:
                        fw.op(t, omm, reads=rds, wtok_only=[B_PT_])
        p = tq % 2
        for jj in range(NJ):
            j = NJ * tq + jj
            sl = jj % 2
            PT_, B_PT_ = PS[jj]
            xb, B_xb = t12, B_t12
            col, B_col = cols[p][jj]
            src = X if isA else X1S
            B_src = None if isA else B_x1s
            fw.op(sy, lambda h, j=j, xb=xb, src=src: h.dma_start(
                xb[:, :], src[128 * j:128 * j + 128, :]),
                reads=[B_src] if B_src else [],
                writes=[B_xb], dma=ch_xq[sl])
            if isA:
                for dh in range(2):
                    fw.op(v, lambda h, dh=dh, PT_=PT_, xb=xb: h.tensor_tensor(
                        xb[:, 512 * dh:512 * dh + 512],
                        xb[:, 512 * dh:512 * dh + 512],
                        PT_[:, 512 * dh:512 * dh + 512], ALU.add),
                        reads=[B_PT_], writes=[B_xb])
                fw.op(g, lambda h, j=j, xb=xb: h.dma_start(
                    X1S[128 * j:128 * j + 128, :], xb[:, :]),
                    reads=[B_xb], writes=[B_x1s], dma=ch_sx1)
            else:
                fw.op(v, lambda h, xb=xb, col=col: h.tensor_scalar(
                    x2b, xb[:, :], col[:, 1:2], col[:, 4:5], ALU.subtract, op1=ALU.mult),
                    reads=[B_xb, B_col], writes=[B_x2])
                for dh in range(2):
                    fw.op(v, lambda h, dh=dh, PT_=PT_: h.tensor_tensor(
                        sap(x2b, x2b.offset + 512 * dh, [[16640, 128], [1, 512]]),
                        sap(x2b, x2b.offset + 512 * dh, [[16640, 128], [1, 512]]),
                        PT_[:, 512 * dh:512 * dh + 512], ALU.add),
                        reads=[B_PT_], writes=[B_x2])
                fw.op(v, lambda h, xb=xb: h.tensor_tensor(y1B, xb[:, :], x2b, ALU.add),
                      reads=[B_xb, B_x2], writes=[B_y1])
                fw.op(g, lambda h, j=j: h.dma_start(
                    RS[128 * j:128 * j + 128, :], y1B),
                    reads=[B_y1], writes=[B_rs], dma=ch_sr)
                fw.op(s, lambda h, col=col: h.activation(h2b, x2b, AF.Square,
                                                         accum_out=col[:, 6:7]),
                      reads=[B_x2], writes=[B_h2, B_col])
                fw.op(s, lambda h, col=col: h.activation(col[:, 6:7], col[:, 6:7], AF.Sqrt,
                                                         bias=epsc[:, 1:2], scale=1.0 / 1024),
                      reads=[B_col, B_eps], writes=[B_col])
                fw.op(v, lambda h, col=col: h.reciprocal(col[:, 6:7], col[:, 6:7]),
                      reads=[B_col], writes=[B_col])
                fw.op(v, lambda h, col=col, xb=xb: h.tensor_scalar(xb[:, :], x2b, col[:, 6:7],
                                                                   None, ALU.mult),
                      reads=[B_x2, B_col], writes=[B_xb])
                fw.op(v, lambda h, xb=xb: h.tensor_tensor(h2b, xb[:, :], ln2b[:, :], ALU.mult),
                      reads=[B_ln2b, B_xb], writes=[B_h2])
                fw.op(g, lambda h, j=j: h.dma_start(
                    H2S[128 * j:128 * j + 128, :], h2b),
                    reads=[B_h2], writes=[B_h2s], dma=ch_sh2)

    for ph in ("A", "B"):
        for gg in range(NQ):
            if gg == 0:
                fw.scope = "f" + ph
                for jj in range(NJ):
                    front(ph, 0, jj)
            prep_tm(ph, gg)
            qstage(ph, gg)
            if gg + 1 < NQ:
                fw.scope = "f" + ph
                for jj in range(NJ):
                    front(ph, gg + 1, jj)
            pairs(ph, gg)
            wo_evac(ph, gg)
    fw.scope = "mend"
    fw.drain_chans()
    with nc.Block("main") as blkM:
        fw.flush(blkM)

    scM.close()
    fw.new_epoch()

    with contextlib.ExitStack() as pc:
        psA = lambda n, shp, dt: (pc.enter_context(nc.psum_tensor(n, shp, dt)), fw.buf(n))
        sbl = lambda n, shp, dt: (pc.enter_context(nc.sbuf_tensor(n, shp, dt)), fw.buf(n))
        PG = [psA(f"PG{k}", [128, 512], F32) for k in range(2)]
        PU = [psA(f"PU{k}", [128, 512], F32) for k in range(2)]
        PD = [psA(f"PD{k}", [128, 1024], F32) for k in range(2)]
        acc, B_acc = sbl("acc", [128, HNT * 1024], F32)
        h2l, B_h2l = sbl("h2l", [128, HTT * 4096], BF16)
        h2st = [sbl(f"h2st{k}", [128, 1024], BF16) for k in range(2)]
        wcol, B_wcol = sbl("wcol", [128, 8 * HNT], F32)
        e8, B_e8 = sbl("e8", [128, 8], F32)
        w8, B_w8 = sbl("w8", [128, 8], F32)
        rcol, B_rcol = sbl("rcol", [128, 6], F32)
        wg_b = [sbl(f"wg{k}", [128, 1024], BF16) for k in range(2 * MB)]
        wu_b = [sbl(f"wu{k}", [128, 1024], BF16) for k in range(2 * MB)]
        wd_b = [sbl(f"wd{k}", [128, 1024], BF16) for k in range(MB)]
        silu_b = [sbl(f"silu{k}", [128, 512], BF16) for k in range(2)]
        actp, B_actp = sbl("actp", [128, MB * 512], BF16)
        pdb = [PD[k][0][:, :].bitcast(BF16) for k in range(2)]
        mslots = [(sap(pdb[0], 0, [[2048, 128], [1, 128]]), PD[0][1]),
                  (sap(pdb[1], 0, [[2048, 128], [1, 128]]), PD[1][1]),
                  (sap(pdb[0], 1024, [[2048, 128], [1, 128]]), PD[0][1]),
                  (sap(pdb[1], 1024, [[2048, 128], [1, 128]]), PD[1][1])]

        fw.scope = "moe"
        for hh in range(CH):
            t0 = hh * HT
            fw.op(sy, lambda h, t0=t0: h.dma_start(
                acc[:, :].rearrange("p (i n) -> p i n", n=1024),
                sap(RS, t0 * 1024, [[1024, 128], [131072, HNT], [1, 1024]])),
                reads=[B_rs], writes=[B_acc], dma=ch_acc)
            for it in range(HNT):
                tt, sub = it // 4, it % 4
                hb, B_hb = h2st[it % 2]
                fw.op(sy, lambda h, t0=t0, it=it, hb=hb: h.dma_start(
                    hb[:, :], H2S[t0 + 128 * it:t0 + 128 * it + 128, :]),
                    reads=[B_h2s], writes=[B_hb], dma=ch_h2l[it % 2])
                for c in range(8):
                    pv, Bp = mslots[c % 4]
                    fw.op(t, lambda h, c=c, pv=pv, hb=hb: h.transpose(
                        pv, hb[:, 128 * c:128 * c + 128], ident_bf[:, :]),
                        reads=[B_hb, B_idbf], writes=[Bp])
                    fw.op(v, lambda h, c=c, pv=pv, tt=tt, sub=sub: h.tensor_copy(
                        h2l[:, 4096 * tt + 512 * c + 128 * sub:
                            4096 * tt + 512 * c + 128 * sub + 128], pv),
                        reads=[Bp], writes=[B_h2l])
            for it in range(HNT):
                tt, sub = it // 4, it % 4
                pd, B_pd = PD[it % 2]
                def rmm(h, tt=tt, sub=sub, pd=pd):
                    for cc in range(8):
                        ii = h.matmul(pd[:, 512:520],
                                      h2l[:, 4096 * tt + 512 * cc + 128 * sub:
                                          4096 * tt + 512 * cc + 128 * sub + 128],
                                      rw_sb[:, 8 * cc:8 * cc + 8],
                                      start=(cc == 0), stop=(cc == 7))
                    return ii
                fw.op(t, rmm, reads=[B_h2l, B_rw], writes=[B_pd])
                fw.op(v, lambda h, pd=pd: h.reduce_max(rcol[:, 0:1], pd[:, 512:520], AX.X),
                      reads=[B_pd], writes=[B_rcol])
                fw.op(v, lambda h: h.tensor_scalar(rcol[:, 1:2], rcol[:, 0:1], -1.0, None, ALU.mult),
                      reads=[B_rcol], writes=[B_rcol])
                fw.op(s, lambda h, pd=pd: h.activation(e8[:, :], pd[:, 512:520], AF.Exp,
                                                       bias=rcol[:, 1:2]),
                      reads=[B_pd, B_rcol], writes=[B_e8])
                fw.op(v, lambda h: h.reduce_max(rcol[:, 2:3], e8[:, :], AX.X),
                      reads=[B_e8], writes=[B_rcol])
                fw.op(v, lambda h: h.tensor_scalar(w8[:, :], e8[:, :], rcol[:, 2:3], None, ALU.is_equal),
                      reads=[B_e8, B_rcol], writes=[B_w8])
                fw.op(v, lambda h: h.scalar_tensor_tensor(w8[:, :], w8[:, :], -2.0, e8[:, :],
                                                          ALU.mult, ALU.add),
                      reads=[B_w8, B_e8], writes=[B_w8])
                fw.op(v, lambda h: h.reduce_max(rcol[:, 3:4], w8[:, :], AX.X),
                      reads=[B_w8], writes=[B_rcol])
                fw.op(v, lambda h: h.tensor_tensor(rcol[:, 4:5], rcol[:, 2:3], rcol[:, 3:4], ALU.add),
                      reads=[B_rcol], writes=[B_rcol])
                fw.op(v, lambda h: h.reciprocal(rcol[:, 4:5], rcol[:, 4:5]),
                      reads=[B_rcol], writes=[B_rcol])
                fw.op(v, lambda h: h.tensor_scalar(w8[:, :], e8[:, :], rcol[:, 3:4], None, ALU.is_ge),
                      reads=[B_e8, B_rcol], writes=[B_w8])
                fw.op(v, lambda h: h.tensor_tensor(w8[:, :], w8[:, :], e8[:, :], ALU.mult),
                      reads=[B_e8, B_w8], writes=[B_w8])
                fw.op(v, lambda h, it=it: h.tensor_scalar(
                    wcol[:, 8 * it:8 * it + 8], w8[:, :], rcol[:, 4:5], None, ALU.mult),
                    reads=[B_rcol, B_w8], writes=[B_wcol])
            for e in range(8):
                for b in range(NB):
                    par = (e * NB + b) % 2
                    for mi in range(MB):
                        mc = MB * b + mi
                        sl = par * MB + mi
                        fw.op(g, lambda h, e=e, mc=mc, sl=sl: h.dma_start(
                            wg_b[sl][0][:, :].rearrange("p (c n) -> p c n", n=128),
                            sap(W["w_gate"], e * 1024 * 3584 + 128 * mc,
                                [[3584, 128], [128 * 3584, 8], [1, 128]])),
                            writes=[wg_b[sl][1]], dma=ch_wg[sl])
                        fw.op(g, lambda h, e=e, mc=mc, sl=sl: h.dma_start(
                            wu_b[sl][0][:, :].rearrange("p (c n) -> p c n", n=128),
                            sap(W["w_up"], e * 1024 * 3584 + 128 * mc,
                                [[3584, 128], [128 * 3584, 8], [1, 128]])),
                            writes=[wu_b[sl][1]], dma=ch_wu[sl])
                        fw.op(g, lambda h, e=e, mc=mc, mi=mi: h.dma_start(
                            wd_b[mi][0][:, :],
                            sap(W["w_down"], e * 3584 * 1024 + 128 * mc * 1024,
                                [[1024, 128], [1, 1024]])),
                            writes=[wd_b[mi][1]], dma=ch_wdd[par * MB + mi])
                    for tt in range(HTT):
                        for mi in range(MB):
                            sl = par * MB + mi
                            pg, B_pg = PG[mi % 2]
                            pu, B_pu = PU[mi % 2]
                            def gmm(h, tt=tt, sl=sl, pg=pg):
                                for kc in range(8):
                                    ii = h.matmul(pg[:, :], wg_b[sl][0][:, 128 * kc:128 * kc + 128],
                                                  h2l[:, 4096 * tt + 512 * kc:4096 * tt + 512 * kc + 512],
                                                  start=(kc == 0), stop=(kc == 7))
                                return ii
                            fw.op(t, gmm, reads=[wg_b[sl][1], B_h2l], writes=[B_pg])
                            def umm(h, tt=tt, sl=sl, pu=pu):
                                for kc in range(8):
                                    ii = h.matmul(pu[:, :], wu_b[sl][0][:, 128 * kc:128 * kc + 128],
                                                  h2l[:, 4096 * tt + 512 * kc:4096 * tt + 512 * kc + 512],
                                                  start=(kc == 0), stop=(kc == 7))
                                return ii
                            fw.op(t, umm, reads=[wu_b[sl][1], B_h2l], writes=[B_pu])
                            sb_, B_sb_ = silu_b[mi % 2]
                            fw.op(s, lambda h, pg=pg, sb_=sb_: h.activation(sb_[:, :], pg[:, :], AF.Silu),
                                  reads=[B_pg], writes=[B_sb_])
                            fw.op(v, lambda h, mi=mi, pu=pu, sb_=sb_: h.tensor_tensor(
                                actp[:, 512 * mi:512 * mi + 512], sb_[:, :], pu[:, :], ALU.mult),
                                reads=[B_sb_, B_pu], writes=[B_actp])
                        for jj in range(4):
                            pd, B_pd = PD[jj % 2]
                            for mi in range(MB):
                                def dmm(h, jj=jj, pd=pd, mi=mi):
                                    h.matmul(pd[:, 0:512],
                                             actp[:, 512 * mi + 128 * jj:512 * mi + 128 * jj + 128],
                                             wd_b[mi][0][:, 0:512],
                                             start=(mi == 0), stop=(mi == MB - 1))
                                    return h.matmul(pd[:, 512:1024],
                                                    actp[:, 512 * mi + 128 * jj:512 * mi + 128 * jj + 128],
                                                    wd_b[mi][0][:, 512:1024],
                                                    start=(mi == 0), stop=(mi == MB - 1))
                                if mi == 0:
                                    fw.op(t, dmm, reads=[B_actp, wd_b[mi][1]], writes=[B_pd])
                                else:
                                    fw.op(t, dmm, reads=[B_actp, wd_b[mi][1]], wtok_only=[B_pd])
                            it = 4 * tt + jj
                            fw.op(v, lambda h, it=it, pd=pd, e=e: h.scalar_tensor_tensor(
                                acc[:, 1024 * it:1024 * it + 1024], pd[:, :],
                                wcol[:, 8 * it + e:8 * it + e + 1],
                                acc[:, 1024 * it:1024 * it + 1024],
                                ALU.mult, ALU.add),
                                reads=[B_pd, B_wcol, B_acc], writes=[B_acc])
            fw.op(sy, lambda h, t0=t0: h.dma_start(
                sap(OUT, t0 * 1024, [[1024, 128], [131072, HNT], [1, 1024]]),
                acc[:, :].rearrange("p (i n) -> p i n", n=1024)),
                reads=[B_acc], dma=ch_out)
        fw.drain_chans()
        with nc.Block("moe") as blk:
            fw.flush(blk)

    ctx.close()
    return nc


_NC_CACHE = {}


def kernel(**inputs):
    x = np.ascontiguousarray(inputs["L0_attn_x"], dtype=np.float32)
    B, S, D = x.shape
    if S not in _NC_CACHE:
        _NC_CACHE[S] = build_program(S)
    nc = _NC_CACHE[S]
    wmap = {
        "wq_w": inputs["L0_attn_wq_w"], "wq_b": inputs["L0_attn_wq_b"],
        "wk_w": inputs["L0_attn_wk_w"], "wk_b": inputs["L0_attn_wk_b"],
        "wv_w": inputs["L0_attn_wv_w"], "wv_b": inputs["L0_attn_wv_b"],
        "wo_w": inputs["L0_attn_wo_w"], "wo_b": inputs["L0_attn_wo_b"],
        "ln1_w": inputs["L0_ffn_ln1_w"], "ln2_w": inputs["L0_ffn_ln2_w"],
        "fwq": inputs["L0_ffn_wq"], "fwk": inputs["L0_ffn_wk"],
        "fwv": inputs["L0_ffn_wv"], "fwo": inputs["L0_ffn_wo"],
        "router_w": inputs["L0_ffn_router_w"], "w_gate": inputs["L0_ffn_w_gate"],
        "w_up": inputs["L0_ffn_w_up"], "w_down": inputs["L0_ffn_w_down"],
    }
    wmap = {k: np.ascontiguousarray(v, dtype=np.float32) for k, v in wmap.items()}
    in_maps = [dict(x=np.ascontiguousarray(x[i]), **wmap) for i in range(B)]
    res = run_bass_kernel_spmd(nc, in_maps, core_ids=list(range(B)))
    return np.stack([np.asarray(res.results[i]["out"]) for i in range(B)], axis=0)


# version 14
# speedup vs baseline: 1.1363x; 1.1363x over previous
import contextlib
import numpy as np
import concourse.bass as bass
import concourse.mybir as mybir
from concourse.bass_utils import run_bass_kernel_spmd

F32 = mybir.dt.float32
BF16 = mybir.dt.bfloat16
I32 = mybir.dt.int32
AF = mybir.ActivationFunctionType
ALU = mybir.AluOpType
AX = mybir.AxisListType
PI = float(np.pi)
LN1E4 = float(np.log(10000.0))

ENGINES = ("sync", "scalar", "vector", "tensor", "gpsimd")


class Buf:
    __slots__ = ("name", "wtok", "rtoks")

    def __init__(self, name="buf"):
        self.name = name
        self.wtok = None
        self.rtoks = []


class Chan:
    __slots__ = ("name", "sem", "count", "touched", "eng")

    def __init__(self, name, sem, eng):
        self.name = name
        self.sem = sem
        self.count = 0
        self.touched = []
        self.eng = eng


class FW:
    def __init__(self, nc, ctx):
        self.nc = nc
        self.ctx = ctx
        self.streams = {e: [] for e in ENGINES}
        self.esem = {}
        self.epos = {e: 0 for e in ENGINES}
        self._semkey = {}
        self._epoch = -1
        self.wm = {e: {} for e in ENGINES}
        self.chans = []
        self.scope = "x"
        self.new_epoch()

    def new_epoch(self):
        self._epoch += 1
        for e in ENGINES:
            sem = self.ctx.enter_context(self.nc.semaphore(f"prod_{e}_{self._epoch}"))
            self.esem[e] = sem
            self.epos[e] = 0
            self._semkey[id(sem)] = ("e", e, self._epoch)

    def buf(self, name="buf"):
        return Buf(name)

    def chan(self, name, eng):
        sem = self.ctx.enter_context(self.nc.semaphore(f"ch_{name}"))
        c = Chan(name, sem, eng)
        self._semkey[id(sem)] = ("c", name)
        self.chans.append(c)
        return c

    def _wait(self, eng, tok):
        if tok is None:
            return
        sem, val = tok
        key = self._semkey[id(sem)]
        w = self.wm[eng].get(key, 0)
        if w >= val:
            return
        self.wm[eng][key] = val

        def f(h, sem=sem, val=val):
            h.wait_ge(sem, val)

        self.streams[eng].append((self.scope, f))

    def op(self, eng, fn, reads=(), writes=(), dma=None, wtok_only=()):
        toks = []
        for b in reads:
            if b is not None and b.wtok is not None:
                toks.append(b.wtok)
        for b in writes:
            if b is None:
                continue
            if b.wtok is not None:
                toks.append(b.wtok)
            toks.extend(b.rtoks)
        seen = {}
        for tk in toks:
            k = self._semkey[id(tk[0])]
            if k not in seen or seen[k][1] < tk[1]:
                seen[k] = tk
        for tk in seen.values():
            self._wait(eng, tk)
        if dma is not None:
            sem = dma.sem
            dma.count += 16
            val = dma.count
            amt = 16
            still = []
            for b in dma.touched:
                hit = False
                if b.wtok is not None and b.wtok[0] is sem:
                    b.wtok = (sem, val)
                    hit = True
                nr = []
                for tk in b.rtoks:
                    if tk[0] is sem:
                        tk = (sem, val)
                        hit = True
                    nr.append(tk)
                b.rtoks = nr
                if hit:
                    still.append(b)
            dma.touched = still
        else:
            sem = self.esem[eng]
            self.epos[eng] += 1
            val = self.epos[eng]
            amt = 1

        def f(h, fn=fn, sem=sem, amt=amt):
            fn(h).then_inc(sem, amt)

        self.streams[eng].append((self.scope, f))
        tok = (sem, val)
        for b in writes:
            if b is not None:
                b.wtok = tok
                b.rtoks = []
                if dma is not None:
                    dma.touched.append(b)
        for b in wtok_only:
            if b is not None:
                b.wtok = tok
                if dma is not None:
                    dma.touched.append(b)
        for b in reads:
            if b is not None:
                b.rtoks.append(tok)
                if dma is not None:
                    dma.touched.append(b)
        return tok

    def drain_chans(self):
        for c in self.chans:
            if c.count > 0:
                def f(h, sem=c.sem, v=c.count):
                    h.wait_ge(sem, v)
                self.streams[c.eng].append((self.scope, f))

    def flush(self, block):
        nc = self.nc
        for name in ENGINES:
            stream = self.streams[name]
            if not stream:
                continue
            dec = getattr(block, name)

            def body(h, stream=tuple(stream)):
                i = 0
                n = len(stream)
                while i < n:
                    tag = stream[i][0]
                    j = i
                    with nc.named_scope(tag):
                        while j < n and stream[j][0] == tag:
                            stream[j][1](h)
                            j += 1
                    i = j

            dec(body)
        self.streams = {e: [] for e in ENGINES}


def sap(t_ap, offset, dims):
    return bass.AP(t_ap.tensor, offset, dims)


def build_program(S=4096):
    NT = S // 128
    QW = 512
    NJ = QW // 128
    NQ = S // QW
    CH = 2 if S >= 2048 else 1
    HT = S // CH
    HTT = HT // QW
    HNT = HT // 128
    MB = 7
    NB = 28 // MB

    nc = bass.Bass("TRN2", target_bir_lowering=False)
    dr = lambda name, shape, dt, kind: nc.dram_tensor(name, shape, dt, kind=kind).ap()
    X = dr("x", [S, 1024], F32, "ExternalInput")
    shapes = {"wq_w": [1024, 1024], "wq_b": [1024], "wk_w": [1024, 1024],
              "wk_b": [1024], "wv_w": [1024, 1024], "wv_b": [1024],
              "wo_w": [1024, 1024], "wo_b": [1024], "ln1_w": [1024],
              "ln2_w": [1024], "fwq": [1024, 1024], "fwk": [1024, 512],
              "fwv": [1024, 512], "fwo": [1024, 1024], "router_w": [1024, 8],
              "w_gate": [8, 1024, 3584], "w_up": [8, 1024, 3584],
              "w_down": [8, 3584, 1024]}
    W = {n: dr(n, s, F32, "ExternalInput") for n, s in shapes.items()}
    OUT = dr("out", [S, 1024], F32, "ExternalOutput")
    X1S = dr("x1_s", [S, 1024], F32, "Internal")
    RS = dr("r_s", [S, 1024], F32, "Internal")
    H2S = dr("h2_s", [S, 1024], BF16, "Internal")

    ctx = contextlib.ExitStack()
    fw = FW(nc, ctx)
    sbA = lambda n, shp, dt: (ctx.enter_context(nc.sbuf_tensor(n, shp, dt)), fw.buf(n))

    ch_cw = fw.chan("cw", "gpsimd")
    ch_x = [fw.chan(f"x{k}", "sync") for k in range(2)]
    ch_xq = [fw.chan(f"xq{k}", "sync") for k in range(2)]
    ch_wo = [fw.chan(f"wo{k}", "gpsimd") for k in range(2)]
    ch_sx1 = fw.chan("sx1", "gpsimd")
    ch_sr = fw.chan("sr", "gpsimd")
    ch_sh2 = fw.chan("sh2", "gpsimd")
    ch_acc = fw.chan("acc", "sync")
    ch_h2l = [fw.chan(f"h2l{k}", "sync") for k in range(2)]
    ch_out = fw.chan("out", "sync")
    ch_wg = [fw.chan(f"wg{k}", "gpsimd") for k in range(14)]
    ch_wu = [fw.chan(f"wu{k}", "gpsimd") for k in range(14)]
    ch_wdd = [fw.chan(f"wd{k}", "gpsimd") for k in range(14)]

    ident_bf, B_idbf = sbA("ident_bf", [128, 128], BF16)
    ident_f, B_idf = sbA("ident_f", [128, 128], F32)
    mask01, B_mask = sbA("mask01", [128, 128], BF16)
    rowidx, B_rowidx = sbA("rowidx", [128, 1], F32)
    ones_bf, B_onesbf = sbA("ones_bf", [128, 128], BF16)
    ones_f, B_onesf = sbA("ones_f", [1, 128], F32)
    invf, B_invf = sbA("invf", [1, 32], F32)
    epsc, B_eps = sbA("epsc", [128, 2], F32)
    rw_sb, B_rw = sbA("rw_sb", [128, 64], BF16)

    B_x1s = fw.buf("x1_s")
    B_rs = fw.buf("r_s")
    B_h2s = fw.buf("h2_s")

    v, s, t, g, sy = "vector", "scalar", "tensor", "gpsimd", "sync"

    scM = contextlib.ExitStack()
    sbM = lambda n, shp, dt: (scM.enter_context(nc.sbuf_tensor(n, shp, dt)), fw.buf(n))
    psM = lambda n, shp, dt: (scM.enter_context(nc.psum_tensor(n, shp, dt)), fw.buf(n))

    cos_tm, B_cos = sbM("cos_tm", [128, 32 * NT], F32)
    sin_tm, B_sin = sbM("sin_tm", [128, 32 * NT], F32)
    KT, B_KT = sbM("KT", [128, 8 * S], BF16)
    V_tm, B_V = sbM("V_tm", [128, 1040 * NT], BF16)
    brow_pk, B_brow = sbM("brow_pk", [128, 1024], BF16)
    brow_o, B_browo = sbM("brow_o", [1, 1024], BF16)
    _brows = {"wq_b": (brow_pk, 0), "wk_b": (brow_pk, 32), "wv_b": (brow_pk, 64),
              "wo_b": (brow_o, 0)}
    brow = {n: ((lambda tt=tt, r=r: tt[r:r + 1, :])(),
                B_brow if n != "wo_b" else B_browo, r)
            for n, (tt, r) in _brows.items()}
    ln1b, B_ln1b = sbM("ln1b", [128, 1024], BF16)
    ln2b, B_ln2b = sbM("ln2b", [128, 1024], BF16)

    xin = [sbM(f"xin{k}", [128, 1024], F32) for k in range(2)]
    lnx4 = [sbM(f"lnx{k}", [128, 1024], BF16) for k in range(4)]
    lnxT4 = [sbM(f"lnxT{k}", [128, 1024], BF16) for k in range(4)]
    rot2 = [sbM(f"rot{k}", [128, 1024], BF16) for k in range(2)]
    t12, B_t12 = sbM("t12", [128, 1024], F32)
    QT, B_QT = sbM("QT", [128, 8 * QW], BF16)
    A_sb = [sbM(f"Asb{k}", [128, 1024], BF16) for k in range(2)]
    ctxT, B_ctxT = sbM("ctxT", [128, 8 * QW], BF16)
    linvb, B_linvb = sbM("linvb", [64, 512], F32)
    wos = [sbM(f"wos{k}", [128, 1024], BF16) for k in range(2)]
    cols = [[sbM(f"col{p}_{k}", [128, 8], F32) for k in range(4)] for p in range(2)]
    nq2, B_nq2 = sbM("nq2", [128, 64], F32)
    gcol, B_gcol = sbM("gcol", [64, 1], F32)
    grow, B_grow = sbM("grow", [1, 64], F32)
    m16a, B_m16a = sbM("m16a", [1, 16], F32)
    m16b, B_m16b = sbM("m16b", [1, 16], F32)
    MkAr, B_MkAr = sbM("MkAr", [1, 16], F32)
    MkBr, B_MkBr = sbM("MkBr", [1, 8], F32)
    kp, B_kp = sbM("kp", [1, 8], F32)
    prow, B_prow = sbM("prow", [1, 8], F32)
    pairb, B_pairb = sbM("pairb", [128, 8], F32)

    PS = [psM(f"PS{k}", [128, 1024], F32) for k in range(4)]
    psb = [PS[k][0][:, :].bitcast(BF16) for k in range(4)]
    tslots = [(sap(psb[0], 0, [[2048, 128], [1, 128]]), PS[0][1]),
              (sap(psb[1], 0, [[2048, 128], [1, 128]]), PS[1][1]),
              (sap(psb[0], 1024, [[2048, 128], [1, 128]]), PS[0][1]),
              (sap(psb[1], 1024, [[2048, 128], [1, 128]]), PS[1][1])]

    V32 = V_tm[:, :].bitcast(F32)
    y1B = sap(V32, 8320, [[16640, 128], [1, 1024]])
    B_y1 = fw.buf("y1B")
    x2b = sap(V32, 9344, [[16640, 128], [1, 1024]])
    B_x2 = fw.buf("x2b")
    h2b = sap(V_tm[:, :], 20736, [[33280, 128], [1, 1024]])
    B_h2 = fw.buf("h2b")

    def front(ph, gg, jj):
        i = 4 * gg + jj
        p = gg % 2
        col, B_col = cols[p][jj]
        xb, B_xb = xin[jj % 2]
        lx, B_lx = lnx4[jj]
        src = X if ph == "A" else X1S
        B_src = None if ph == "A" else B_x1s
        fw.op(sy, lambda h: h.dma_start(xb[:, :], src[128 * i:128 * i + 128, :]),
              reads=[B_src] if B_src else [], writes=[B_xb], dma=ch_x[jj % 2])
        fw.op(v, lambda h: h.tensor_reduce(col[:, 0:1], xb[:, :], AX.X, ALU.add),
              reads=[B_xb], writes=[B_col])
        fw.op(v, lambda h: h.tensor_scalar(col[:, 1:2], col[:, 0:1], 1.0 / 1024, None, ALU.mult),
              reads=[B_col], writes=[B_col])
        fw.op(s, lambda h: h.activation(lx[:, :], xb[:, :], AF.Square, accum_out=col[:, 2:3]),
              reads=[B_xb], writes=[B_lx, B_col])
        fw.op(v, lambda h: h.tensor_tensor(col[:, 3:4], col[:, 1:2], col[:, 1:2], ALU.mult),
              reads=[B_col], writes=[B_col])
        fw.op(v, lambda h: h.scalar_tensor_tensor(col[:, 3:4], col[:, 2:3], 1.0 / 1024,
                                                  col[:, 3:4], ALU.mult, ALU.subtract),
              reads=[B_col], writes=[B_col])
        fw.op(s, lambda h: h.activation(col[:, 4:5], col[:, 3:4], AF.Sqrt, bias=epsc[:, 0:1]),
              reads=[B_col, B_eps], writes=[B_col])
        fw.op(v, lambda h: h.reciprocal(col[:, 4:5], col[:, 4:5]),
              reads=[B_col], writes=[B_col])
        if ph == "A":
            fw.op(v, lambda h: h.tensor_scalar(lx[:, :], xb[:, :], col[:, 1:2],
                                               col[:, 4:5], ALU.subtract, op1=ALU.mult),
                  reads=[B_xb, B_col], writes=[B_lx])
        else:
            fw.op(v, lambda h: h.tensor_tensor(col[:, 6:7], col[:, 4:5], col[:, 4:5], ALU.mult),
                  reads=[B_col], writes=[B_col])
            fw.op(v, lambda h: h.tensor_tensor(col[:, 6:7], col[:, 3:4], col[:, 6:7], ALU.mult),
                  reads=[B_col], writes=[B_col])
            fw.op(s, lambda h: h.activation(col[:, 6:7], col[:, 6:7], AF.Sqrt, bias=epsc[:, 1:2]),
                  reads=[B_col, B_eps], writes=[B_col])
            fw.op(v, lambda h: h.reciprocal(col[:, 6:7], col[:, 6:7]),
                  reads=[B_col], writes=[B_col])
            fw.op(v, lambda h: h.tensor_tensor(col[:, 7:8], col[:, 4:5], col[:, 6:7], ALU.mult),
                  reads=[B_col], writes=[B_col])
            fw.op(v, lambda h: h.tensor_scalar(t12[:, :], xb[:, :], col[:, 1:2],
                                               col[:, 7:8], ALU.subtract, op1=ALU.mult),
                  reads=[B_xb, B_col], writes=[B_t12])
            fw.op(v, lambda h: h.tensor_tensor(lx[:, :], t12[:, :], ln1b[:, :], ALU.mult),
                  reads=[B_ln1b, B_t12], writes=[B_lx])

    def transposes(src, B_src, nch, emit_out, base=0, slots=None):
        slots = slots or tslots
        for c in range(nch):
            pv, Bp = slots[c % len(slots)]
            fw.op(t, lambda h, c=c, pv=pv: h.transpose(
                pv, src[:, base + 128 * c:base + 128 * c + 128], ident_bf[:, :]),
                reads=[B_src, B_idbf], writes=[Bp])
            emit_out(c, pv, Bp)

    def rope(src_ap, B_srcs, dst, B_dst, i, nh):
        co = 32 * i
        cosap = sap(cos_tm[:, :], co, [[32 * NT, 128], [0, nh], [1, 32]])
        sinap = sap(sin_tm[:, :], co, [[32 * NT, 128], [0, nh], [1, 32]])
        flen = src_ap.ap[0][0]
        off0 = src_ap.offset
        srcf = sap(src_ap, off0 + 0, [[flen, 128], [64, nh], [1, 32]])
        srcs = sap(src_ap, off0 + 32, [[flen, 128], [64, nh], [1, 32]])
        dlen = dst[:, :].ap[0][0]
        dstf = sap(dst[:, :], 0, [[dlen, 128], [64, nh], [1, 32]])
        dsts = sap(dst[:, :], 32, [[dlen, 128], [64, nh], [1, 32]])
        t1v = sap(t12[:, :], 0, [[1024, 128], [32, nh], [1, 32]])
        t2v = sap(t12[:, :], 512, [[1024, 128], [32, nh], [1, 32]])
        fw.op(v, lambda h: h.tensor_tensor(t1v, srcf, cosap, ALU.mult),
              reads=list(B_srcs) + [B_cos], writes=[B_t12])
        fw.op(v, lambda h: h.tensor_tensor(t2v, srcs, sinap, ALU.mult),
              reads=list(B_srcs) + [B_sin], wtok_only=[B_t12])
        fw.op(v, lambda h: h.tensor_tensor(dstf, t1v, t2v, ALU.subtract),
              reads=[B_t12], writes=[B_dst])
        fw.op(v, lambda h: h.tensor_tensor(t1v, srcs, cosap, ALU.mult),
              reads=list(B_srcs) + [B_cos], writes=[B_t12])
        fw.op(v, lambda h: h.tensor_tensor(t2v, srcf, sinap, ALU.mult),
              reads=list(B_srcs) + [B_cos], wtok_only=[B_t12])
        fw.op(v, lambda h: h.tensor_tensor(dsts, t1v, t2v, ALU.add),
              reads=[B_t12], writes=[B_dst])

    def head_sq(nh, jj, rsrc, B_rsrc, sqf, B_sqf):
        fw.op(s, lambda h, rsrc=rsrc, sqf=sqf: h.activation(
            sqf[:, 0:64 * nh], rsrc[:, 0:64 * nh], AF.Square),
            reads=[B_rsrc], writes=[B_sqf])
        fw.op(v, lambda h, jj=jj, nh=nh, sqf=sqf: h.tensor_reduce(
            nq2[:, nh * jj:nh * jj + nh],
            sap(sqf[:, :], 0, [[sqf[:, :].ap[0][0], 128], [64, nh], [1, 64]]),
            AX.X, ALU.add), reads=[B_sqf], writes=[B_nq2])

    def group_norms(nh, Mkrow, B_Mkrow, running):
        w = 4 * nh
        PX, B_PX = PS[3]
        fw.op(t, lambda h, w=w: h.transpose(PX[0:w, 0:128], nq2[:, 0:w], ident_f[:, :]),
              reads=[B_nq2, B_idf], writes=[B_PX])
        fw.op(v, lambda h, w=w: h.reduce_max(gcol[0:w, 0:1], PX[0:w, 0:128], AX.X),
              reads=[B_PX], writes=[B_gcol])
        fw.op(t, lambda h, w=w: h.transpose(PX[0:1, 128:128 + w], gcol[0:w, 0:1],
                                            ident_f[0:w, 0:w]),
              reads=[B_gcol, B_idf], wtok_only=[B_PX])
        fw.op(v, lambda h, w=w: h.tensor_copy(grow[0:1, 0:w], PX[0:1, 128:128 + w]),
              reads=[B_PX], writes=[B_grow])
        fw.op(v, lambda h, nh=nh: h.tensor_max(m16a[0:1, 0:nh], grow[0:1, 0:nh],
                                               grow[0:1, nh:2 * nh]),
              reads=[B_grow], writes=[B_m16a])
        fw.op(v, lambda h, nh=nh: h.tensor_max(m16b[0:1, 0:nh], grow[0:1, 2 * nh:3 * nh],
                                               grow[0:1, 3 * nh:4 * nh]),
              reads=[B_grow], writes=[B_m16b])
        fw.op(v, lambda h, nh=nh: h.tensor_max(m16a[0:1, 0:nh], m16a[0:1, 0:nh],
                                               m16b[0:1, 0:nh]),
              reads=[B_m16b], writes=[B_m16a])
        if running:
            fw.op(v, lambda h, nh=nh: h.tensor_max(Mkrow[0:1, 0:nh], Mkrow[0:1, 0:nh],
                                                   m16a[0:1, 0:nh]),
                  reads=[B_m16a], writes=[B_Mkrow])

    fw.scope = "pro"
    _iotaF, B_iotaF = sbM("p_iotaF", [128, 128], F32)
    _wrk, B_wrk = sbM("p_wrk", [128, 32], F32)
    _angw, B_ang = sbM("p_ang", [128, 32], F32)
    _wri, B_wri = sbM("p_wri", [128, 32], I32)
    _invfF, B_invfF = sbM("p_invf", [128, 32], F32)
    _rowcol, B_rowcol = sbM("p_rowcol", [128, 1], F32)
    _ang0, B_ang0 = sbM("p_ang0", [128, 32], F32)
    iotaF = _iotaF[:, :]
    wrk = _wrk[:, :]
    angw = _angw[:, :]
    wri = _wri[:, :]
    invf_full = _invfF[:, :]
    rowcol = _rowcol[:, :]
    ang0 = _ang0[:, :]
    import os
    if os.environ.get("SBUF_DEBUG"):
        print("SBUF remaining at peak:", nc.sbuf_bytes_remaining)

    fw.op(g, lambda h: h.memset(epsc[:, 0:1], 1e-5), writes=[B_eps])
    fw.op(g, lambda h: h.memset(epsc[:, 1:2], 1e-6), writes=[B_eps])
    fw.op(g, lambda h: h.iota(rowidx[:, :], [[0, 1]], channel_multiplier=1,
                              allow_small_or_imprecise_dtypes=True), writes=[B_rowidx])
    fw.op(g, lambda h: h.iota(iotaF, [[1, 128]], channel_multiplier=0,
                              allow_small_or_imprecise_dtypes=True), writes=[B_iotaF])
    fw.op(g, lambda h: h.iota(invf_full, [[1, 32]], channel_multiplier=0,
                              allow_small_or_imprecise_dtypes=True), writes=[B_invfF])
    fw.op(g, lambda h: h.memset(ones_bf[:, :], 1.0), writes=[B_onesbf])
    fw.op(g, lambda h: h.memset(ones_f[:, :], 1.0), writes=[B_onesf])
    fw.op(g, lambda h: h.memset(MkAr[:, :], 0.0), writes=[B_MkAr])
    fw.op(g, lambda h: h.memset(MkBr[:, :], 0.0), writes=[B_MkBr])
    fw.op(v, lambda h: h.tensor_scalar(ident_bf[:, :], iotaF, rowidx[:, :], None, ALU.is_equal),
          reads=[B_iotaF, B_rowidx], writes=[B_idbf])
    fw.op(v, lambda h: h.tensor_scalar(ident_f[:, :], iotaF, rowidx[:, :], None, ALU.is_equal),
          reads=[B_iotaF, B_rowidx], writes=[B_idf])
    fw.op(v, lambda h: h.tensor_scalar(mask01[:, :], iotaF, rowidx[:, :], None, ALU.is_ge),
          reads=[B_iotaF, B_rowidx], writes=[B_mask])
    fw.op(s, lambda h: h.activation(invf_full, invf_full, AF.Exp, scale=-LN1E4 / 32.0),
          reads=[B_invfF], writes=[B_invfF])
    for n in ("wq_b", "wk_b", "wv_b", "wo_b"):
        fw.op(g, lambda h, n=n: h.dma_start(brow[n][0],
                                            W[n].rearrange("(o n) -> o n", o=1)),
              writes=[brow[n][1]], dma=ch_cw)
    fw.op(g, lambda h: h.dma_start(ln1b[:, :], sap(W["ln1_w"], 0, [[0, 128], [1, 1024]])),
          writes=[B_ln1b], dma=ch_cw)
    fw.op(g, lambda h: h.dma_start(ln2b[:, :], sap(W["ln2_w"], 0, [[0, 128], [1, 1024]])),
          writes=[B_ln2b], dma=ch_cw)
    fw.op(g, lambda h: h.dma_start(rw_sb[:, :].rearrange("p (c n) -> p c n", n=8),
                                   W["router_w"].rearrange("(c p) n -> p c n", p=128)),
          writes=[B_rw], dma=ch_cw)
    for bi in range(NT):
        fw.op(v, lambda h, bi=bi: h.tensor_scalar(rowcol, rowidx[:, :], 128.0 * bi,
                                                  None, ALU.add),
              reads=[B_rowidx], writes=[B_rowcol])
        fw.op(v, lambda h: h.tensor_scalar(ang0, invf_full, rowcol, None, ALU.mult),
              reads=[B_rowcol, B_invfF], writes=[B_ang0])
        fw.op(v, lambda h: h.tensor_scalar(wrk, ang0, 1.0 / (2 * PI), None, ALU.mult),
              reads=[B_ang0], writes=[B_wrk])
        fw.op(v, lambda h: h.tensor_copy(wri, wrk), reads=[B_wrk], writes=[B_wri])
        fw.op(v, lambda h: h.tensor_copy(wrk, wri), reads=[B_wri], writes=[B_wrk])
        fw.op(v, lambda h: h.scalar_tensor_tensor(angw, wrk, -2 * PI, ang0,
                                                  ALU.mult, ALU.add),
              reads=[B_wrk, B_ang0], writes=[B_ang])
        fw.op(v, lambda h: h.tensor_scalar(wrk, angw, PI, None, ALU.is_gt),
              reads=[B_ang], writes=[B_wrk])
        fw.op(v, lambda h: h.scalar_tensor_tensor(angw, wrk, -2 * PI, angw,
                                                  ALU.mult, ALU.add),
              reads=[B_wrk, B_ang], writes=[B_ang])
        fw.op(v, lambda h: h.tensor_scalar(wrk, angw, -PI, None, ALU.is_lt),
              reads=[B_ang], writes=[B_wrk])
        fw.op(v, lambda h: h.scalar_tensor_tensor(angw, wrk, 2 * PI, angw,
                                                  ALU.mult, ALU.add),
              reads=[B_wrk, B_ang], writes=[B_ang])
        fw.op(s, lambda h, bi=bi: h.activation(sin_tm[:, 32 * bi:32 * bi + 32],
                                               angw, AF.Sin),
              reads=[B_ang], writes=[B_sin])
        fw.op(v, lambda h: h.tensor_scalar(wrk, angw, PI / 2, None, ALU.add),
              reads=[B_ang], writes=[B_wrk])
        fw.op(v, lambda h: h.tensor_scalar(angw, wrk, PI, None, ALU.is_gt),
              reads=[B_wrk], writes=[B_ang])
        fw.op(v, lambda h: h.scalar_tensor_tensor(wrk, angw, -2 * PI, wrk,
                                                  ALU.mult, ALU.add),
              reads=[B_ang, B_wrk], writes=[B_wrk])
        fw.op(s, lambda h, bi=bi: h.activation(cos_tm[:, 32 * bi:32 * bi + 32],
                                               wrk, AF.Sin),
              reads=[B_wrk], writes=[B_cos])

    def prep_tm(ph, gg):
        isA = ph == "A"
        fw.scope = "p" + ph
        for jj in range(NJ):
            i = 4 * gg + jj
            lx, B_lx = lnx4[jj]
            lT, B_lT = lnxT4[jj]
            transposes(lx, B_lx, 8,
                       lambda c, pv, Bp, lT=lT, B_lT=B_lT: fw.op(
                           v, lambda h, c=c, pv=pv: h.tensor_copy(
                               lT[:, 128 * c:128 * c + 128], pv),
                           reads=[Bp], writes=[B_lT]))
        if isA:
            passes = (("wk_w", "wk_b", True), ("wv_w", "wv_b", False))
        else:
            passes = (("fwk", None, True),)
        for (wname, bname, is_k) in passes:
            for c in range(8):
                wsl, B_wsl = wos[c % 2]
                if isA:
                    fw.op(g, lambda h, c=c, wsl=wsl, wname=wname: h.dma_start(
                        wsl[:, :], W[wname][128 * c:128 * c + 128, :]),
                        writes=[B_wsl], dma=ch_wo[c % 2])
                else:
                    fw.op(g, lambda h, c=c, wsl=wsl: h.dma_start(
                        wsl[:, 0:512], W["fwk"][128 * c:128 * c + 128, :]),
                        writes=[B_wsl], dma=ch_wo[c % 2])
                    fw.op(g, lambda h, c=c, wsl=wsl: h.dma_start(
                        wsl[:, 512:1024], W["fwv"][128 * c:128 * c + 128, :]),
                        wtok_only=[B_wsl], dma=ch_wo[c % 2])
                for jj in range(NJ):
                    PP, B_PP = PS[jj]
                    lT, B_lT = lnxT4[jj]
                    def kmm(h, c=c, jj=jj, wsl=wsl, PP=PP, lT=lT, bname=bname, isA=isA):
                        for dh in range(2):
                            first = (c == 0)
                            if first and isA:
                                r = brow[bname][2]
                                h.matmul(PP[:, 512 * dh:512 * dh + 512],
                                         ones_bf[r:r + 1, :],
                                         brow[bname][0][0:1, 512 * dh:512 * dh + 512],
                                         start=True, stop=False)
                                first = False
                            ii = h.matmul(PP[:, 512 * dh:512 * dh + 512],
                                          lT[:, 128 * c:128 * c + 128],
                                          wsl[:, 512 * dh:512 * dh + 512],
                                          start=first, stop=(c == 7))
                        return ii
                    rds = [B_lT, B_wsl] + ([brow[bname][1], B_onesbf] if (c == 0 and isA) else [])
                    if c == 0:
                        fw.op(t, kmm, reads=rds, writes=[B_PP])
                    else:
                        fw.op(t, kmm, reads=rds, wtok_only=[B_PP])
            for jj in range(NJ):
                i = 4 * gg + jj
                PP, B_PP = PS[jj]
                lx, B_lx = lnx4[jj]
                if is_k:
                    nh = 16 if isA else 8
                    rot, B_rot = rot2[jj % 2]
                    rope(PP[:, :], [B_PP], rot, B_rot, i, nh)
                    head_sq(nh, jj, rot, B_rot, lx, B_lx)
                    own = [(sap(psb[jj], 0, [[2048, 128], [1, 128]]), B_PP),
                           (sap(psb[jj], 1024, [[2048, 128], [1, 128]]), B_PP)]
                    if isA:
                        transposes(rot, B_rot, 8,
                                   lambda c, pv, Bp, i=i: fw.op(
                                       v, lambda h, c=c, pv=pv, i=i: h.tensor_copy(
                                           KT[:, S * c + 128 * i:S * c + 128 * i + 128], pv),
                                       reads=[Bp], writes=[B_KT]),
                                   slots=own)
                    else:
                        fw.op(v, lambda h, jj=jj, i=i, PP=PP: h.tensor_copy(
                            sap(V_tm[:, :], 520 * i, [[1040 * NT, 128], [65, 8], [1, 64]]),
                            sap(PP[:, :], 512, [[1024, 128], [64, 8], [1, 64]])),
                            reads=[B_PP], writes=[B_V])
                        fw.op(v, lambda h, i=i: h.memset(
                            sap(V_tm[:, :], 520 * i + 64, [[1040 * NT, 128], [65, 8], [1, 1]]), 1.0),
                            writes=[B_V])
                        for cc in range(4):
                            pv, Bp = own[cc % 2]
                            fw.op(t, lambda h, cc=cc, pv=pv, rot=rot: h.transpose(
                                pv, rot[:, 128 * cc:128 * cc + 128], ident_bf[:, :]),
                                reads=[B_rot, B_idbf], writes=[Bp])
                            for gg2 in range(2):
                                kvh = 2 * cc + gg2
                                sub = pv[64 * gg2:64 * gg2 + 64, 0:128]
                                for half in range(2):
                                    fw.op(v, lambda h, kvh=kvh, half=half, sub=sub, i=i:
                                          h.tensor_copy(
                                              KT[64 * half:64 * half + 64,
                                                 S * kvh + 128 * i:S * kvh + 128 * i + 128],
                                              sub),
                                          reads=[Bp], writes=[B_KT])
                else:
                    for dh in range(2):
                        fw.op(v, lambda h, dh=dh, i=i, PP=PP: h.tensor_copy(
                            sap(V_tm[:, :], 1040 * i + 65 * 8 * dh, [[1040 * NT, 128], [65, 8], [1, 64]]),
                            sap(PP[:, :], 512 * dh, [[1024, 128], [64, 8], [1, 64]])),
                            reads=[B_PP], writes=[B_V])
                    fw.op(v, lambda h, i=i: h.memset(
                        sap(V_tm[:, :], 1040 * i + 64, [[1040 * NT, 128], [65, 16], [1, 1]]), 1.0),
                        writes=[B_V])
            if is_k:
                group_norms(16 if isA else 8, MkAr if isA else MkBr,
                            B_MkAr if isA else B_MkBr, running=True)

    def qstage(ph, tq):
        isA = ph == "A"
        fw.scope = "q" + ph
        Mkrow, B_Mkrow = (MkAr, B_MkAr) if isA else (MkBr, B_MkBr)
        wqn = "wq_w" if isA else "fwq"
        for c in range(8):
            wsl, B_wsl = wos[c % 2]
            fw.op(g, lambda h, c=c, wsl=wsl, wqn=wqn: h.dma_start(
                wsl[:, :], W[wqn][128 * c:128 * c + 128, :]),
                writes=[B_wsl], dma=ch_wo[c % 2])
            for jj in range(NJ):
                PQ_, B_PQ_ = PS[jj]
                lT, B_lT = lnxT4[jj]
                def qmm(h, c=c, jj=jj, wsl=wsl, PQ_=PQ_, lT=lT):
                    for dh in range(2):
                        first = (c == 0)
                        if first and isA:
                            h.matmul(PQ_[:, 512 * dh:512 * dh + 512], ones_bf[0:1, :],
                                     brow["wq_b"][0][0:1, 512 * dh:512 * dh + 512],
                                     start=True, stop=False)
                            first = False
                        ii = h.matmul(PQ_[:, 512 * dh:512 * dh + 512],
                                      lT[:, 128 * c:128 * c + 128],
                                      wsl[:, 512 * dh:512 * dh + 512],
                                      start=first, stop=(c == 7))
                    return ii
                rds = [B_lT, B_wsl] + ([brow["wq_b"][1], B_onesbf]
                                       if (c == 0 and isA) else [])
                if c == 0:
                    fw.op(t, qmm, reads=rds, writes=[B_PQ_])
                else:
                    fw.op(t, qmm, reads=rds, wtok_only=[B_PQ_])
        for jj in range(NJ):
            j = NJ * tq + jj
            PQ_, B_PQ_ = PS[jj]
            lx, B_lx = lnx4[jj]
            rot, B_rot = rot2[jj % 2]
            rope(PQ_[:, :], [B_PQ_], rot, B_rot, j, 16)
            head_sq(16, jj, rot, B_rot, lx, B_lx)
            pqb = PQ_[:, :].bitcast(BF16)
            tsl_q = [(sap(pqb, 0, [[2048, 128], [1, 128]]), B_PQ_),
                     (sap(pqb, 1024, [[2048, 128], [1, 128]]), B_PQ_)]
            for c in range(8):
                pv, Bp = tsl_q[c % 2]
                fw.op(t, lambda h, c=c, pv=pv, rot=rot: h.transpose(
                    pv, rot[:, 128 * c:128 * c + 128], ident_bf[:, :]),
                    reads=[B_rot, B_idbf], writes=[Bp])
                fw.op(v, lambda h, c=c, pv=pv, jj=jj: h.tensor_copy(
                    QT[:, QW * c + 128 * jj:QW * c + 128 * jj + 128], pv),
                    reads=[Bp], writes=[B_QT])
        group_norms(16, None, None, running=False)
        if isA:
            fw.op(v, lambda h: h.tensor_tensor(kp[:, :],
                                               sap(Mkrow[:, :], 0, [[16, 1], [2, 8]]),
                                               sap(Mkrow[:, :], 1, [[16, 1], [2, 8]]),
                                               ALU.max),
                  reads=[B_Mkrow], writes=[B_kp])
        else:
            fw.op(v, lambda h: h.tensor_copy(kp[:, :], Mkrow[0:1, 0:8]),
                  reads=[B_Mkrow], writes=[B_kp])
        fw.op(v, lambda h: h.tensor_tensor(prow[:, :],
                                           sap(m16a[:, :], 0, [[16, 1], [2, 8]]),
                                           sap(m16a[:, :], 1, [[16, 1], [2, 8]]),
                                           ALU.max),
              reads=[B_m16a], writes=[B_prow])
        fw.op(v, lambda h: h.tensor_tensor(prow[:, :], prow[:, :], kp[:, :], ALU.mult),
              reads=[B_kp, B_prow], writes=[B_prow])
        fw.op(s, lambda h: h.activation(prow[:, :], prow[:, :], AF.Sqrt),
              reads=[B_prow], writes=[B_prow])
        fw.op(v, lambda h: h.tensor_scalar(prow[:, :], prow[:, :], -0.125, None, ALU.mult),
              reads=[B_prow], writes=[B_prow])
        fw.op(t, lambda h: h.matmul(PS[2][0][:, 512:520], ones_f[:, :], prow[:, :],
                                    start=True, stop=True),
              reads=[B_onesf, B_prow], writes=[PS[2][1]])
        fw.op(v, lambda h: h.tensor_copy(pairb[:, :], PS[2][0][:, 512:520]),
              reads=[PS[2][1]], writes=[B_pairb])

    def pairs(ph, tq):
        isA = ph == "A"
        fw.scope = "s" + ph
        VST = 1040 if isA else 520
        nkb = NJ * (tq + 1)

        def smm_emit(c, kb):
            PSc, B_PSc = PS[kb % 2]
            qoff = 128 * (kb - NJ * tq) if kb >= NJ * tq else 0
            def smm(h, c=c, kb=kb, qoff=qoff, PSc=PSc):
                h.matmul(PSc[:, qoff:512],
                         KT[0:64, S * c + 128 * kb:S * c + 128 * kb + 128],
                         QT[0:64, QW * c + qoff:QW * c + QW],
                         start=True, stop=True, tile_position=(0, 0))
                return h.matmul(PSc[:, 512 + qoff:1024],
                                KT[64:128, S * c + 128 * kb:S * c + 128 * kb + 128],
                                QT[64:128, QW * c + qoff:QW * c + QW],
                                start=True, stop=True, tile_position=(64, 0))
            fw.op(t, smm, reads=[B_KT, B_QT], writes=[B_PSc])

        def finalize(c):
            PC, B_PC = PS[2 + (c % 2)]
            for half in range(2):
                fw.op(v, lambda h, half=half, PC=PC: h.reciprocal(
                    linvb[0:1, :], PC[64:65, 512 * half:512 * half + 512]),
                    reads=[B_PC], writes=[B_linvb])
                fw.op(t, lambda h, half=half, PC=PC: h.matmul(
                    PC[64:128, 512 * half:512 * half + 512],
                    ones_f[0:1, 0:64], linvb[0:1, :], start=True, stop=True,
                    tile_position=(0, 64)),
                    reads=[B_onesf, B_linvb], wtok_only=[B_PC])
                fw.op(v, lambda h, half=half, PC=PC: h.tensor_copy(
                    linvb[:, :], PC[64:128, 512 * half:512 * half + 512]),
                    reads=[B_PC], writes=[B_linvb])
                fw.op(v, lambda h, c=c, half=half, PC=PC: h.tensor_tensor(
                    ctxT[64 * half:64 * half + 64, QW * c:QW * c + QW],
                    PC[0:64, 512 * half:512 * half + 512],
                    linvb[0:64, 0:512], ALU.mult),
                    reads=[B_PC, B_linvb], writes=[B_ctxT])

        smm_emit(0, 0)
        if nkb > 1:
            smm_emit(0, 1)
        for c in range(8):
            PC, B_PC = PS[2 + (c % 2)]
            for kb in range(nkb):
                PSc, B_PSc = PS[kb % 2]
                qoff = 128 * (kb - NJ * tq) if kb >= NJ * tq else 0
                Asb, B_Asb = A_sb[kb % 2]
                if qoff == 0:
                    fw.op(s, lambda h, c=c, PSc=PSc, Asb=Asb: h.activation(
                        Asb[:, :], PSc[:, :], AF.Exp,
                        bias=pairb[:, c:c + 1], scale=0.125),
                        reads=[B_PSc, B_pairb], writes=[B_Asb])
                else:
                    fw.op(s, lambda h, PSc=PSc, Asb=Asb, c=c, qoff=qoff: h.activation(
                        Asb[:, qoff:512], PSc[:, qoff:512], AF.Exp,
                        bias=pairb[:, c:c + 1], scale=0.125),
                        reads=[B_PSc, B_pairb], writes=[B_Asb])
                    fw.op(s, lambda h, PSc=PSc, Asb=Asb, c=c, qoff=qoff: h.activation(
                        Asb[:, 512 + qoff:1024], PSc[:, 512 + qoff:1024], AF.Exp,
                        bias=pairb[:, c:c + 1], scale=0.125),
                        reads=[B_PSc, B_pairb], writes=[B_Asb])
                if kb >= NJ * tq:
                    for half in range(2):
                        o = 512 * half + qoff
                        fw.op(v, lambda h, o=o, Asb=Asb: h.tensor_tensor(
                            Asb[:, o:o + 128], Asb[:, o:o + 128], mask01[:, :], ALU.mult),
                            reads=[B_mask, B_Asb], writes=[B_Asb])
                hA = 65 * (2 * c) if isA else 65 * c
                hB = 65 * (2 * c + 1) if isA else 65 * c
                def avmm(h, kb=kb, qoff=qoff, Asb=Asb, hA=hA, hB=hB, PC=PC, VST=VST,
                         last=(kb == nkb - 1)):
                    h.matmul(PC[0:65, qoff:512],
                             sap(V_tm[:, :], VST * kb + hA, [[1040 * NT, 128], [1, 65]]),
                             Asb[:, qoff:512], start=(kb == 0), stop=last)
                    return h.matmul(PC[0:65, 512 + qoff:1024],
                                    sap(V_tm[:, :], VST * kb + hB, [[1040 * NT, 128], [1, 65]]),
                                    Asb[:, 512 + qoff:1024], start=(kb == 0), stop=last)
                if kb == 0:
                    fw.op(t, avmm, reads=[B_V, B_Asb], writes=[B_PC])
                else:
                    fw.op(t, avmm, reads=[B_V, B_Asb], wtok_only=[B_PC])
                if kb + 2 < nkb:
                    smm_emit(c, kb + 2)
            if c < 7:
                smm_emit(c + 1, 0)
                if nkb > 1:
                    smm_emit(c + 1, 1)
            if c >= 1:
                finalize(c - 1)
        finalize(7)

    def wo_evac(ph, tq):
        isA = ph == "A"
        fw.scope = "o" + ph
        won = "wo_w" if isA else "fwo"
        for c in range(8):
            wsl, B_wsl = wos[c % 2]
            fw.op(g, lambda h, c=c, wsl=wsl, won=won: h.dma_start(
                wsl[:, :], W[won][128 * c:128 * c + 128, :]),
                writes=[B_wsl], dma=ch_wo[c % 2])
            for jj in range(NJ):
                PT_, B_PT_ = PS[jj]
                for dh in range(2):
                    def omm(h, c=c, jj=jj, dh=dh, wsl=wsl, PT_=PT_):
                        first = (c == 0)
                        if first and isA:
                            h.matmul(PT_[:, 512 * dh:512 * dh + 512], ones_bf[0:1, :],
                                     brow["wo_b"][0][0:1, 512 * dh:512 * dh + 512],
                                     start=True, stop=False)
                            first = False
                        return h.matmul(PT_[:, 512 * dh:512 * dh + 512],
                                        ctxT[:, QW * c + 128 * jj:QW * c + 128 * jj + 128],
                                        wsl[:, 512 * dh:512 * dh + 512],
                                        start=first, stop=(c == 7))
                    rds = [B_ctxT, B_wsl] + ([brow["wo_b"][1], B_onesbf]
                                             if (c == 0 and isA) else [])
                    if c == 0 and dh == 0:
                        fw.op(t, omm, reads=rds, writes=[B_PT_])
                    else:
                        fw.op(t, omm, reads=rds, wtok_only=[B_PT_])
        p = tq % 2
        for jj in range(NJ):
            j = NJ * tq + jj
            sl = jj % 2
            PT_, B_PT_ = PS[jj]
            xb, B_xb = t12, B_t12
            col, B_col = cols[p][jj]
            src = X if isA else X1S
            B_src = None if isA else B_x1s
            fw.op(sy, lambda h, j=j, xb=xb, src=src: h.dma_start(
                xb[:, :], src[128 * j:128 * j + 128, :]),
                reads=[B_src] if B_src else [],
                writes=[B_xb], dma=ch_xq[sl])
            if isA:
                for dh in range(2):
                    fw.op(v, lambda h, dh=dh, PT_=PT_, xb=xb: h.tensor_tensor(
                        xb[:, 512 * dh:512 * dh + 512],
                        xb[:, 512 * dh:512 * dh + 512],
                        PT_[:, 512 * dh:512 * dh + 512], ALU.add),
                        reads=[B_PT_], writes=[B_xb])
                fw.op(g, lambda h, j=j, xb=xb: h.dma_start(
                    X1S[128 * j:128 * j + 128, :], xb[:, :]),
                    reads=[B_xb], writes=[B_x1s], dma=ch_sx1)
            else:
                fw.op(v, lambda h, xb=xb, col=col: h.tensor_scalar(
                    x2b, xb[:, :], col[:, 1:2], col[:, 4:5], ALU.subtract, op1=ALU.mult),
                    reads=[B_xb, B_col], writes=[B_x2])
                for dh in range(2):
                    fw.op(v, lambda h, dh=dh, PT_=PT_: h.tensor_tensor(
                        sap(x2b, x2b.offset + 512 * dh, [[16640, 128], [1, 512]]),
                        sap(x2b, x2b.offset + 512 * dh, [[16640, 128], [1, 512]]),
                        PT_[:, 512 * dh:512 * dh + 512], ALU.add),
                        reads=[B_PT_], writes=[B_x2])
                fw.op(v, lambda h, xb=xb: h.tensor_tensor(y1B, xb[:, :], x2b, ALU.add),
                      reads=[B_xb, B_x2], writes=[B_y1])
                fw.op(g, lambda h, j=j: h.dma_start(
                    RS[128 * j:128 * j + 128, :], y1B),
                    reads=[B_y1], writes=[B_rs], dma=ch_sr)
                fw.op(s, lambda h, col=col: h.activation(h2b, x2b, AF.Square,
                                                         accum_out=col[:, 6:7]),
                      reads=[B_x2], writes=[B_h2, B_col])
                fw.op(s, lambda h, col=col: h.activation(col[:, 6:7], col[:, 6:7], AF.Sqrt,
                                                         bias=epsc[:, 1:2], scale=1.0 / 1024),
                      reads=[B_col, B_eps], writes=[B_col])
                fw.op(v, lambda h, col=col: h.reciprocal(col[:, 6:7], col[:, 6:7]),
                      reads=[B_col], writes=[B_col])
                fw.op(v, lambda h, col=col, xb=xb: h.tensor_scalar(xb[:, :], x2b, col[:, 6:7],
                                                                   None, ALU.mult),
                      reads=[B_x2, B_col], writes=[B_xb])
                fw.op(v, lambda h, xb=xb: h.tensor_tensor(h2b, xb[:, :], ln2b[:, :], ALU.mult),
                      reads=[B_ln2b, B_xb], writes=[B_h2])
                fw.op(g, lambda h, j=j: h.dma_start(
                    H2S[128 * j:128 * j + 128, :], h2b),
                    reads=[B_h2], writes=[B_h2s], dma=ch_sh2)

    for ph in ("A", "B"):
        for gg in range(NQ):
            if gg == 0:
                fw.scope = "f" + ph
                for jj in range(NJ):
                    front(ph, 0, jj)
            prep_tm(ph, gg)
            qstage(ph, gg)
            if gg + 1 < NQ:
                fw.scope = "f" + ph
                for jj in range(NJ):
                    front(ph, gg + 1, jj)
            pairs(ph, gg)
            wo_evac(ph, gg)
    fw.scope = "mend"
    fw.drain_chans()
    with nc.Block("main") as blkM:
        fw.flush(blkM)

    scM.close()
    fw.new_epoch()

    with contextlib.ExitStack() as pc:
        psA = lambda n, shp, dt: (pc.enter_context(nc.psum_tensor(n, shp, dt)), fw.buf(n))
        sbl = lambda n, shp, dt: (pc.enter_context(nc.sbuf_tensor(n, shp, dt)), fw.buf(n))
        PG = [psA(f"PG{k}", [128, 512], F32) for k in range(2)]
        PU = [psA(f"PU{k}", [128, 512], F32) for k in range(2)]
        PD = [psA(f"PD{k}", [128, 1024], F32) for k in range(2)]
        acc, B_acc = sbl("acc", [128, HNT * 1024], F32)
        h2l, B_h2l = sbl("h2l", [128, HTT * 4096], BF16)
        h2st = [sbl(f"h2st{k}", [128, 1024], BF16) for k in range(2)]
        wcol, B_wcol = sbl("wcol", [128, 8 * HNT], F32)
        e8, B_e8 = sbl("e8", [128, 8], F32)
        w8, B_w8 = sbl("w8", [128, 8], F32)
        rcol, B_rcol = sbl("rcol", [128, 6], F32)
        wg_b = [sbl(f"wg{k}", [128, 1024], BF16) for k in range(2 * MB)]
        wu_b = [sbl(f"wu{k}", [128, 1024], BF16) for k in range(2 * MB)]
        wd_b = [sbl(f"wd{k}", [128, 1024], BF16) for k in range(MB)]
        silu_b = [sbl(f"silu{k}", [128, 512], BF16) for k in range(2)]
        actp, B_actp = sbl("actp", [128, MB * 512], BF16)
        pdb = [PD[k][0][:, :].bitcast(BF16) for k in range(2)]
        mslots = [(sap(pdb[0], 0, [[2048, 128], [1, 128]]), PD[0][1]),
                  (sap(pdb[1], 0, [[2048, 128], [1, 128]]), PD[1][1]),
                  (sap(pdb[0], 1024, [[2048, 128], [1, 128]]), PD[0][1]),
                  (sap(pdb[1], 1024, [[2048, 128], [1, 128]]), PD[1][1])]

        fw.scope = "moe"
        for hh in range(CH):
            t0 = hh * HT
            fw.op(sy, lambda h, t0=t0: h.dma_start(
                acc[:, :].rearrange("p (i n) -> p i n", n=1024),
                sap(RS, t0 * 1024, [[1024, 128], [131072, HNT], [1, 1024]])),
                reads=[B_rs], writes=[B_acc], dma=ch_acc)
            for it in range(HNT):
                tt, sub = it // 4, it % 4
                hb, B_hb = h2st[it % 2]
                fw.op(sy, lambda h, t0=t0, it=it, hb=hb: h.dma_start(
                    hb[:, :], H2S[t0 + 128 * it:t0 + 128 * it + 128, :]),
                    reads=[B_h2s], writes=[B_hb], dma=ch_h2l[it % 2])
                for c in range(8):
                    pv, Bp = mslots[c % 4]
                    fw.op(t, lambda h, c=c, pv=pv, hb=hb: h.transpose(
                        pv, hb[:, 128 * c:128 * c + 128], ident_bf[:, :]),
                        reads=[B_hb, B_idbf], writes=[Bp])
                    fw.op(v, lambda h, c=c, pv=pv, tt=tt, sub=sub: h.tensor_copy(
                        h2l[:, 4096 * tt + 512 * c + 128 * sub:
                            4096 * tt + 512 * c + 128 * sub + 128], pv),
                        reads=[Bp], writes=[B_h2l])
            for it in range(HNT):
                tt, sub = it // 4, it % 4
                pd, B_pd = PD[it % 2]
                def rmm(h, tt=tt, sub=sub, pd=pd):
                    for cc in range(8):
                        ii = h.matmul(pd[:, 512:520],
                                      h2l[:, 4096 * tt + 512 * cc + 128 * sub:
                                          4096 * tt + 512 * cc + 128 * sub + 128],
                                      rw_sb[:, 8 * cc:8 * cc + 8],
                                      start=(cc == 0), stop=(cc == 7))
                    return ii
                fw.op(t, rmm, reads=[B_h2l, B_rw], writes=[B_pd])
                fw.op(v, lambda h, pd=pd: h.reduce_max(rcol[:, 0:1], pd[:, 512:520], AX.X),
                      reads=[B_pd], writes=[B_rcol])
                fw.op(v, lambda h: h.tensor_scalar(rcol[:, 1:2], rcol[:, 0:1], -1.0, None, ALU.mult),
                      reads=[B_rcol], writes=[B_rcol])
                fw.op(s, lambda h, pd=pd: h.activation(e8[:, :], pd[:, 512:520], AF.Exp,
                                                       bias=rcol[:, 1:2]),
                      reads=[B_pd, B_rcol], writes=[B_e8])
                fw.op(v, lambda h: h.reduce_max(rcol[:, 2:3], e8[:, :], AX.X),
                      reads=[B_e8], writes=[B_rcol])
                fw.op(v, lambda h: h.tensor_scalar(w8[:, :], e8[:, :], rcol[:, 2:3], None, ALU.is_equal),
                      reads=[B_e8, B_rcol], writes=[B_w8])
                fw.op(v, lambda h: h.scalar_tensor_tensor(w8[:, :], w8[:, :], -2.0, e8[:, :],
                                                          ALU.mult, ALU.add),
                      reads=[B_w8, B_e8], writes=[B_w8])
                fw.op(v, lambda h: h.reduce_max(rcol[:, 3:4], w8[:, :], AX.X),
                      reads=[B_w8], writes=[B_rcol])
                fw.op(v, lambda h: h.tensor_tensor(rcol[:, 4:5], rcol[:, 2:3], rcol[:, 3:4], ALU.add),
                      reads=[B_rcol], writes=[B_rcol])
                fw.op(v, lambda h: h.reciprocal(rcol[:, 4:5], rcol[:, 4:5]),
                      reads=[B_rcol], writes=[B_rcol])
                fw.op(v, lambda h: h.tensor_scalar(w8[:, :], e8[:, :], rcol[:, 3:4], None, ALU.is_ge),
                      reads=[B_e8, B_rcol], writes=[B_w8])
                fw.op(v, lambda h: h.tensor_tensor(w8[:, :], w8[:, :], e8[:, :], ALU.mult),
                      reads=[B_e8, B_w8], writes=[B_w8])
                fw.op(v, lambda h, it=it: h.tensor_scalar(
                    wcol[:, 8 * it:8 * it + 8], w8[:, :], rcol[:, 4:5], None, ALU.mult),
                    reads=[B_rcol, B_w8], writes=[B_wcol])
            for e in range(8):
                for b in range(NB):
                    par = (e * NB + b) % 2
                    for mi in range(MB):
                        mc = MB * b + mi
                        sl = par * MB + mi
                        fw.op(g, lambda h, e=e, mc=mc, sl=sl: h.dma_start(
                            wg_b[sl][0][:, :].rearrange("p (c n) -> p c n", n=128),
                            sap(W["w_gate"], e * 1024 * 3584 + 128 * mc,
                                [[3584, 128], [128 * 3584, 8], [1, 128]])),
                            writes=[wg_b[sl][1]], dma=ch_wg[sl])
                        fw.op(g, lambda h, e=e, mc=mc, sl=sl: h.dma_start(
                            wu_b[sl][0][:, :].rearrange("p (c n) -> p c n", n=128),
                            sap(W["w_up"], e * 1024 * 3584 + 128 * mc,
                                [[3584, 128], [128 * 3584, 8], [1, 128]])),
                            writes=[wu_b[sl][1]], dma=ch_wu[sl])
                        fw.op(g, lambda h, e=e, mc=mc, mi=mi: h.dma_start(
                            wd_b[mi][0][:, :],
                            sap(W["w_down"], e * 3584 * 1024 + 128 * mc * 1024,
                                [[1024, 128], [1, 1024]])),
                            writes=[wd_b[mi][1]], dma=ch_wdd[par * MB + mi])
                    for tt in range(HTT):
                        for mi in range(MB):
                            sl = par * MB + mi
                            pg, B_pg = PG[mi % 2]
                            pu, B_pu = PU[mi % 2]
                            def gmm(h, tt=tt, sl=sl, pg=pg):
                                for kc in range(8):
                                    ii = h.matmul(pg[:, :], wg_b[sl][0][:, 128 * kc:128 * kc + 128],
                                                  h2l[:, 4096 * tt + 512 * kc:4096 * tt + 512 * kc + 512],
                                                  start=(kc == 0), stop=(kc == 7))
                                return ii
                            fw.op(t, gmm, reads=[wg_b[sl][1], B_h2l], writes=[B_pg])
                            def umm(h, tt=tt, sl=sl, pu=pu):
                                for kc in range(8):
                                    ii = h.matmul(pu[:, :], wu_b[sl][0][:, 128 * kc:128 * kc + 128],
                                                  h2l[:, 4096 * tt + 512 * kc:4096 * tt + 512 * kc + 512],
                                                  start=(kc == 0), stop=(kc == 7))
                                return ii
                            fw.op(t, umm, reads=[wu_b[sl][1], B_h2l], writes=[B_pu])
                            sb_, B_sb_ = silu_b[mi % 2]
                            fw.op(s, lambda h, pg=pg, sb_=sb_: h.activation(sb_[:, :], pg[:, :], AF.Silu),
                                  reads=[B_pg], writes=[B_sb_])
                            fw.op(v, lambda h, mi=mi, pu=pu, sb_=sb_: h.tensor_tensor(
                                actp[:, 512 * mi:512 * mi + 512], sb_[:, :], pu[:, :], ALU.mult),
                                reads=[B_sb_, B_pu], writes=[B_actp])
                        for jj in range(4):
                            pd, B_pd = PD[jj % 2]
                            for mi in range(MB):
                                def dmm(h, jj=jj, pd=pd, mi=mi):
                                    h.matmul(pd[:, 0:512],
                                             actp[:, 512 * mi + 128 * jj:512 * mi + 128 * jj + 128],
                                             wd_b[mi][0][:, 0:512],
                                             start=(mi == 0), stop=(mi == MB - 1))
                                    return h.matmul(pd[:, 512:1024],
                                                    actp[:, 512 * mi + 128 * jj:512 * mi + 128 * jj + 128],
                                                    wd_b[mi][0][:, 512:1024],
                                                    start=(mi == 0), stop=(mi == MB - 1))
                                if mi == 0:
                                    fw.op(t, dmm, reads=[B_actp, wd_b[mi][1]], writes=[B_pd])
                                else:
                                    fw.op(t, dmm, reads=[B_actp, wd_b[mi][1]], wtok_only=[B_pd])
                            it = 4 * tt + jj
                            fw.op(v, lambda h, it=it, pd=pd, e=e: h.scalar_tensor_tensor(
                                acc[:, 1024 * it:1024 * it + 1024], pd[:, :],
                                wcol[:, 8 * it + e:8 * it + e + 1],
                                acc[:, 1024 * it:1024 * it + 1024],
                                ALU.mult, ALU.add),
                                reads=[B_pd, B_wcol, B_acc], writes=[B_acc])
            fw.op(sy, lambda h, t0=t0: h.dma_start(
                sap(OUT, t0 * 1024, [[1024, 128], [131072, HNT], [1, 1024]]),
                acc[:, :].rearrange("p (i n) -> p i n", n=1024)),
                reads=[B_acc], dma=ch_out)
        fw.drain_chans()
        with nc.Block("moe") as blk:
            fw.flush(blk)

    ctx.close()
    return nc


_NC_CACHE = {}


def kernel(**inputs):
    x = np.ascontiguousarray(inputs["L0_attn_x"], dtype=np.float32)
    B, S, D = x.shape
    if S not in _NC_CACHE:
        _NC_CACHE[S] = build_program(S)
    nc = _NC_CACHE[S]
    wmap = {
        "wq_w": inputs["L0_attn_wq_w"], "wq_b": inputs["L0_attn_wq_b"],
        "wk_w": inputs["L0_attn_wk_w"], "wk_b": inputs["L0_attn_wk_b"],
        "wv_w": inputs["L0_attn_wv_w"], "wv_b": inputs["L0_attn_wv_b"],
        "wo_w": inputs["L0_attn_wo_w"], "wo_b": inputs["L0_attn_wo_b"],
        "ln1_w": inputs["L0_ffn_ln1_w"], "ln2_w": inputs["L0_ffn_ln2_w"],
        "fwq": inputs["L0_ffn_wq"], "fwk": inputs["L0_ffn_wk"],
        "fwv": inputs["L0_ffn_wv"], "fwo": inputs["L0_ffn_wo"],
        "router_w": inputs["L0_ffn_router_w"], "w_gate": inputs["L0_ffn_w_gate"],
        "w_up": inputs["L0_ffn_w_up"], "w_down": inputs["L0_ffn_w_down"],
    }
    wmap = {k: np.ascontiguousarray(v, dtype=np.float32) for k, v in wmap.items()}
    in_maps = [dict(x=np.ascontiguousarray(x[i]), **wmap) for i in range(B)]
    res = run_bass_kernel_spmd(nc, in_maps, core_ids=list(range(B)))
    return np.stack([np.asarray(res.results[i]["out"]) for i in range(B)], axis=0)


# version 15
# speedup vs baseline: 1.2139x; 1.0683x over previous
import contextlib
import numpy as np
import concourse.bass as bass
import concourse.mybir as mybir
from concourse.bass_utils import run_bass_kernel_spmd

F32 = mybir.dt.float32
BF16 = mybir.dt.bfloat16
I32 = mybir.dt.int32
AF = mybir.ActivationFunctionType
ALU = mybir.AluOpType
AX = mybir.AxisListType
PI = float(np.pi)
LN1E4 = float(np.log(10000.0))

ENGINES = ("sync", "scalar", "vector", "tensor", "gpsimd")


class Buf:
    __slots__ = ("name", "wtok", "rtoks")

    def __init__(self, name="buf"):
        self.name = name
        self.wtok = None
        self.rtoks = []


class Chan:
    __slots__ = ("name", "sem", "count", "touched", "eng")

    def __init__(self, name, sem, eng):
        self.name = name
        self.sem = sem
        self.count = 0
        self.touched = []
        self.eng = eng


class FW:
    def __init__(self, nc, ctx):
        self.nc = nc
        self.ctx = ctx
        self.streams = {e: [] for e in ENGINES}
        self.esem = {}
        self.epos = {e: 0 for e in ENGINES}
        self._semkey = {}
        self._epoch = -1
        self.wm = {e: {} for e in ENGINES}
        self.chans = []
        self.scope = "x"
        self.new_epoch()

    def new_epoch(self):
        self._epoch += 1
        for e in ENGINES:
            sem = self.ctx.enter_context(self.nc.semaphore(f"prod_{e}_{self._epoch}"))
            self.esem[e] = sem
            self.epos[e] = 0
            self._semkey[id(sem)] = ("e", e, self._epoch)

    def buf(self, name="buf"):
        return Buf(name)

    def chan(self, name, eng):
        sem = self.ctx.enter_context(self.nc.semaphore(f"ch_{name}"))
        c = Chan(name, sem, eng)
        self._semkey[id(sem)] = ("c", name)
        self.chans.append(c)
        return c

    def _wait(self, eng, tok):
        if tok is None:
            return
        sem, val = tok
        key = self._semkey[id(sem)]
        w = self.wm[eng].get(key, 0)
        if w >= val:
            return
        self.wm[eng][key] = val

        def f(h, sem=sem, val=val):
            h.wait_ge(sem, val)

        self.streams[eng].append((self.scope, f))

    def op(self, eng, fn, reads=(), writes=(), dma=None, wtok_only=()):
        toks = []
        for b in reads:
            if b is not None and b.wtok is not None:
                toks.append(b.wtok)
        for b in writes:
            if b is None:
                continue
            if b.wtok is not None:
                toks.append(b.wtok)
            toks.extend(b.rtoks)
        seen = {}
        for tk in toks:
            k = self._semkey[id(tk[0])]
            if k not in seen or seen[k][1] < tk[1]:
                seen[k] = tk
        for tk in seen.values():
            self._wait(eng, tk)
        if dma is not None:
            sem = dma.sem
            dma.count += 16
            val = dma.count
            amt = 16
            still = []
            for b in dma.touched:
                hit = False
                if b.wtok is not None and b.wtok[0] is sem:
                    b.wtok = (sem, val)
                    hit = True
                nr = []
                for tk in b.rtoks:
                    if tk[0] is sem:
                        tk = (sem, val)
                        hit = True
                    nr.append(tk)
                b.rtoks = nr
                if hit:
                    still.append(b)
            dma.touched = still
        else:
            sem = self.esem[eng]
            self.epos[eng] += 1
            val = self.epos[eng]
            amt = 1

        def f(h, fn=fn, sem=sem, amt=amt):
            fn(h).then_inc(sem, amt)

        self.streams[eng].append((self.scope, f))
        tok = (sem, val)
        for b in writes:
            if b is not None:
                b.wtok = tok
                b.rtoks = []
                if dma is not None:
                    dma.touched.append(b)
        for b in wtok_only:
            if b is not None:
                b.wtok = tok
                if dma is not None:
                    dma.touched.append(b)
        for b in reads:
            if b is not None:
                b.rtoks.append(tok)
                if dma is not None:
                    dma.touched.append(b)
        return tok

    def drain_chans(self):
        for c in self.chans:
            if c.count > 0:
                def f(h, sem=c.sem, v=c.count):
                    h.wait_ge(sem, v)
                self.streams[c.eng].append((self.scope, f))

    def flush(self, block):
        nc = self.nc
        for name in ENGINES:
            stream = self.streams[name]
            if not stream:
                continue
            dec = getattr(block, name)

            def body(h, stream=tuple(stream)):
                i = 0
                n = len(stream)
                while i < n:
                    tag = stream[i][0]
                    j = i
                    with nc.named_scope(tag):
                        while j < n and stream[j][0] == tag:
                            stream[j][1](h)
                            j += 1
                    i = j

            dec(body)
        self.streams = {e: [] for e in ENGINES}


def sap(t_ap, offset, dims):
    return bass.AP(t_ap.tensor, offset, dims)


def build_program(S=4096):
    NT = S // 128
    QW = 512
    NJ = QW // 128
    NQ = S // QW
    CH = 2 if S >= 2048 else 1
    HT = S // CH
    HTT = HT // QW
    HNT = HT // 128
    MB = 7
    NB = 28 // MB

    nc = bass.Bass("TRN2", target_bir_lowering=False)
    dr = lambda name, shape, dt, kind: nc.dram_tensor(name, shape, dt, kind=kind).ap()
    X = dr("x", [S, 1024], F32, "ExternalInput")
    shapes = {"wq_w": [1024, 1024], "wq_b": [1024], "wk_w": [1024, 1024],
              "wk_b": [1024], "wv_w": [1024, 1024], "wv_b": [1024],
              "wo_w": [1024, 1024], "wo_b": [1024], "ln1_w": [1024],
              "ln2_w": [1024], "fwq": [1024, 1024], "fwk": [1024, 512],
              "fwv": [1024, 512], "fwo": [1024, 1024], "router_w": [1024, 8],
              "w_gate": [8, 1024, 3584], "w_up": [8, 1024, 3584],
              "w_down": [8, 3584, 1024]}
    W = {n: dr(n, s, F32, "ExternalInput") for n, s in shapes.items()}
    OUT = dr("out", [S, 1024], F32, "ExternalOutput")
    X1S = dr("x1_s", [S, 1024], F32, "Internal")
    RS = dr("r_s", [S, 1024], F32, "Internal")
    H2S = dr("h2_s", [S, 1024], BF16, "Internal")

    ctx = contextlib.ExitStack()
    fw = FW(nc, ctx)
    sbA = lambda n, shp, dt: (ctx.enter_context(nc.sbuf_tensor(n, shp, dt)), fw.buf(n))

    ch_cw = fw.chan("cw", "gpsimd")
    ch_x = [fw.chan(f"x{k}", "sync") for k in range(2)]
    ch_xq = [fw.chan(f"xq{k}", "sync") for k in range(2)]
    ch_wo = [fw.chan(f"wo{k}", "gpsimd") for k in range(2)]
    ch_sx1 = fw.chan("sx1", "gpsimd")
    ch_sr = fw.chan("sr", "gpsimd")
    ch_sh2 = fw.chan("sh2", "gpsimd")
    ch_acc = fw.chan("acc", "sync")
    ch_h2l = [fw.chan(f"h2l{k}", "sync") for k in range(2)]
    ch_out = fw.chan("out", "sync")
    ch_wg = [fw.chan(f"wg{k}", "gpsimd") for k in range(14)]
    ch_wu = [fw.chan(f"wu{k}", "gpsimd") for k in range(14)]
    ch_wdd = [fw.chan(f"wd{k}", "gpsimd") for k in range(14)]

    ident_bf, B_idbf = sbA("ident_bf", [128, 128], BF16)
    ident_f, B_idf = sbA("ident_f", [128, 128], F32)
    mask01, B_mask = sbA("mask01", [128, 128], BF16)
    rowidx, B_rowidx = sbA("rowidx", [128, 1], F32)
    ones_bf, B_onesbf = sbA("ones_bf", [128, 128], BF16)
    ones_f, B_onesf = sbA("ones_f", [1, 128], F32)
    invf, B_invf = sbA("invf", [1, 32], F32)
    epsc, B_eps = sbA("epsc", [128, 2], F32)
    rw_sb, B_rw = sbA("rw_sb", [128, 64], BF16)

    B_x1s = fw.buf("x1_s")
    B_rs = fw.buf("r_s")
    B_h2s = fw.buf("h2_s")

    v, s, t, g, sy = "vector", "scalar", "tensor", "gpsimd", "sync"

    scM = contextlib.ExitStack()
    sbM = lambda n, shp, dt: (scM.enter_context(nc.sbuf_tensor(n, shp, dt)), fw.buf(n))
    psM = lambda n, shp, dt: (scM.enter_context(nc.psum_tensor(n, shp, dt)), fw.buf(n))

    cos_tm, B_cos = sbM("cos_tm", [128, 32 * NT], F32)
    sin_tm, B_sin = sbM("sin_tm", [128, 32 * NT], F32)
    KT, B_KT = sbM("KT", [128, 8 * S], BF16)
    V_tm, B_V = sbM("V_tm", [128, 1040 * NT], BF16)
    brow_pk, B_brow = sbM("brow_pk", [128, 1024], BF16)
    brow_o, B_browo = sbM("brow_o", [1, 1024], BF16)
    _brows = {"wq_b": (brow_pk, 0), "wk_b": (brow_pk, 32), "wv_b": (brow_pk, 64),
              "wo_b": (brow_o, 0)}
    brow = {n: ((lambda tt=tt, r=r: tt[r:r + 1, :])(),
                B_brow if n != "wo_b" else B_browo, r)
            for n, (tt, r) in _brows.items()}
    ln1b, B_ln1b = sbM("ln1b", [128, 1024], BF16)
    ln2b, B_ln2b = sbM("ln2b", [128, 1024], BF16)

    xin = [sbM(f"xin{k}", [128, 1024], F32) for k in range(2)]
    lnx4 = [sbM(f"lnx{k}", [128, 1024], BF16) for k in range(4)]
    lnxT4 = [sbM(f"lnxT{k}", [128, 1024], BF16) for k in range(4)]
    rot2 = [sbM(f"rot{k}", [128, 1024], BF16) for k in range(2)]
    t12, B_t12 = sbM("t12", [128, 1024], F32)
    QT, B_QT = sbM("QT", [128, 8 * QW], BF16)
    A_sb = [sbM(f"Asb{k}", [128, 1024], BF16) for k in range(2)]
    ctxT, B_ctxT = sbM("ctxT", [128, 8 * QW], BF16)
    linvb, B_linvb = sbM("linvb", [64, 512], F32)
    wos = [sbM(f"wos{k}", [128, 1024], BF16) for k in range(2)]
    cols = [[sbM(f"col{p}_{k}", [128, 8], F32) for k in range(4)] for p in range(2)]
    nq2, B_nq2 = sbM("nq2", [128, 64], F32)
    gcol, B_gcol = sbM("gcol", [64, 1], F32)
    grow, B_grow = sbM("grow", [1, 64], F32)
    m16a, B_m16a = sbM("m16a", [1, 16], F32)
    m16b, B_m16b = sbM("m16b", [1, 16], F32)
    MkAr, B_MkAr = sbM("MkAr", [1, 16], F32)
    MkBr, B_MkBr = sbM("MkBr", [1, 8], F32)
    kp, B_kp = sbM("kp", [1, 8], F32)
    prow, B_prow = sbM("prow", [1, 8], F32)
    pairb, B_pairb = sbM("pairb", [128, 8], F32)

    PS = [psM(f"PS{k}", [128, 1024], F32) for k in range(4)]
    psb = [PS[k][0][:, :].bitcast(BF16) for k in range(4)]
    tslots = [(sap(psb[0], 0, [[2048, 128], [1, 128]]), PS[0][1]),
              (sap(psb[1], 0, [[2048, 128], [1, 128]]), PS[1][1]),
              (sap(psb[0], 1024, [[2048, 128], [1, 128]]), PS[0][1]),
              (sap(psb[1], 1024, [[2048, 128], [1, 128]]), PS[1][1])]

    V32 = V_tm[:, :].bitcast(F32)
    y1B = sap(V32, 8320, [[16640, 128], [1, 1024]])
    B_y1 = fw.buf("y1B")
    x2b = sap(V32, 9344, [[16640, 128], [1, 1024]])
    B_x2 = fw.buf("x2b")
    h2b = sap(V_tm[:, :], 20736, [[33280, 128], [1, 1024]])
    B_h2 = fw.buf("h2b")

    def front(ph, gg, jj):
        i = 4 * gg + jj
        p = gg % 2
        col, B_col = cols[p][jj]
        xb, B_xb = xin[jj % 2]
        lx, B_lx = lnx4[jj]
        src = X if ph == "A" else X1S
        B_src = None if ph == "A" else B_x1s
        fw.op(sy, lambda h: h.dma_start(xb[:, :], src[128 * i:128 * i + 128, :]),
              reads=[B_src] if B_src else [], writes=[B_xb], dma=ch_x[jj % 2])
        fw.op(v, lambda h: h.tensor_reduce(col[:, 0:1], xb[:, :], AX.X, ALU.add),
              reads=[B_xb], writes=[B_col])
        fw.op(v, lambda h: h.tensor_scalar(col[:, 1:2], col[:, 0:1], 1.0 / 1024, None, ALU.mult),
              reads=[B_col], writes=[B_col])
        fw.op(s, lambda h: h.activation(lx[:, :], xb[:, :], AF.Square, accum_out=col[:, 2:3]),
              reads=[B_xb], writes=[B_lx, B_col])
        fw.op(v, lambda h: h.tensor_tensor(col[:, 3:4], col[:, 1:2], col[:, 1:2], ALU.mult),
              reads=[B_col], writes=[B_col])
        fw.op(v, lambda h: h.scalar_tensor_tensor(col[:, 3:4], col[:, 2:3], 1.0 / 1024,
                                                  col[:, 3:4], ALU.mult, ALU.subtract),
              reads=[B_col], writes=[B_col])
        fw.op(s, lambda h: h.activation(col[:, 4:5], col[:, 3:4], AF.Sqrt, bias=epsc[:, 0:1]),
              reads=[B_col, B_eps], writes=[B_col])
        fw.op(v, lambda h: h.reciprocal(col[:, 4:5], col[:, 4:5]),
              reads=[B_col], writes=[B_col])
        if ph == "A":
            fw.op(v, lambda h: h.tensor_scalar(lx[:, :], xb[:, :], col[:, 1:2],
                                               col[:, 4:5], ALU.subtract, op1=ALU.mult),
                  reads=[B_xb, B_col], writes=[B_lx])
        else:
            fw.op(v, lambda h: h.tensor_tensor(col[:, 6:7], col[:, 4:5], col[:, 4:5], ALU.mult),
                  reads=[B_col], writes=[B_col])
            fw.op(v, lambda h: h.tensor_tensor(col[:, 6:7], col[:, 3:4], col[:, 6:7], ALU.mult),
                  reads=[B_col], writes=[B_col])
            fw.op(s, lambda h: h.activation(col[:, 6:7], col[:, 6:7], AF.Sqrt, bias=epsc[:, 1:2]),
                  reads=[B_col, B_eps], writes=[B_col])
            fw.op(v, lambda h: h.reciprocal(col[:, 6:7], col[:, 6:7]),
                  reads=[B_col], writes=[B_col])
            fw.op(v, lambda h: h.tensor_tensor(col[:, 7:8], col[:, 4:5], col[:, 6:7], ALU.mult),
                  reads=[B_col], writes=[B_col])
            fw.op(v, lambda h: h.tensor_scalar(t12[:, :], xb[:, :], col[:, 1:2],
                                               col[:, 7:8], ALU.subtract, op1=ALU.mult),
                  reads=[B_xb, B_col], writes=[B_t12])
            fw.op(v, lambda h: h.tensor_tensor(lx[:, :], t12[:, :], ln1b[:, :], ALU.mult),
                  reads=[B_ln1b, B_t12], writes=[B_lx])

    def transposes(src, B_src, nch, emit_out, base=0, slots=None):
        slots = slots or tslots
        for c in range(nch):
            pv, Bp = slots[c % len(slots)]
            fw.op(t, lambda h, c=c, pv=pv: h.transpose(
                pv, src[:, base + 128 * c:base + 128 * c + 128], ident_bf[:, :]),
                reads=[B_src, B_idbf], writes=[Bp])
            emit_out(c, pv, Bp)

    def rope(src_ap, B_srcs, dst, B_dst, i, nh):
        co = 32 * i
        cosap = sap(cos_tm[:, :], co, [[32 * NT, 128], [0, nh], [1, 32]])
        sinap = sap(sin_tm[:, :], co, [[32 * NT, 128], [0, nh], [1, 32]])
        flen = src_ap.ap[0][0]
        off0 = src_ap.offset
        srcf = sap(src_ap, off0 + 0, [[flen, 128], [64, nh], [1, 32]])
        srcs = sap(src_ap, off0 + 32, [[flen, 128], [64, nh], [1, 32]])
        dlen = dst[:, :].ap[0][0]
        dstf = sap(dst[:, :], 0, [[dlen, 128], [64, nh], [1, 32]])
        dsts = sap(dst[:, :], 32, [[dlen, 128], [64, nh], [1, 32]])
        t1v = sap(t12[:, :], 0, [[1024, 128], [32, nh], [1, 32]])
        t2v = sap(t12[:, :], 512, [[1024, 128], [32, nh], [1, 32]])
        fw.op(v, lambda h: h.tensor_tensor(t1v, srcf, cosap, ALU.mult),
              reads=list(B_srcs) + [B_cos], writes=[B_t12])
        fw.op(v, lambda h: h.tensor_tensor(t2v, srcs, sinap, ALU.mult),
              reads=list(B_srcs) + [B_sin], wtok_only=[B_t12])
        fw.op(v, lambda h: h.tensor_tensor(dstf, t1v, t2v, ALU.subtract),
              reads=[B_t12], writes=[B_dst])
        fw.op(v, lambda h: h.tensor_tensor(t1v, srcs, cosap, ALU.mult),
              reads=list(B_srcs) + [B_cos], writes=[B_t12])
        fw.op(v, lambda h: h.tensor_tensor(t2v, srcf, sinap, ALU.mult),
              reads=list(B_srcs) + [B_cos], wtok_only=[B_t12])
        fw.op(v, lambda h: h.tensor_tensor(dsts, t1v, t2v, ALU.add),
              reads=[B_t12], writes=[B_dst])

    def head_sq(nh, jj, rsrc, B_rsrc, sqf, B_sqf):
        fw.op(s, lambda h, rsrc=rsrc, sqf=sqf: h.activation(
            sqf[:, 0:64 * nh], rsrc[:, 0:64 * nh], AF.Square),
            reads=[B_rsrc], writes=[B_sqf])
        fw.op(v, lambda h, jj=jj, nh=nh, sqf=sqf: h.tensor_reduce(
            nq2[:, nh * jj:nh * jj + nh],
            sap(sqf[:, :], 0, [[sqf[:, :].ap[0][0], 128], [64, nh], [1, 64]]),
            AX.X, ALU.add), reads=[B_sqf], writes=[B_nq2])

    def group_norms(nh, Mkrow, B_Mkrow, running):
        w = 4 * nh
        PX, B_PX = PS[3]
        fw.op(t, lambda h, w=w: h.transpose(PX[0:w, 0:128], nq2[:, 0:w], ident_f[:, :]),
              reads=[B_nq2, B_idf], writes=[B_PX])
        fw.op(v, lambda h, w=w: h.reduce_max(gcol[0:w, 0:1], PX[0:w, 0:128], AX.X),
              reads=[B_PX], writes=[B_gcol])
        fw.op(t, lambda h, w=w: h.transpose(PX[0:1, 128:128 + w], gcol[0:w, 0:1],
                                            ident_f[0:w, 0:w]),
              reads=[B_gcol, B_idf], wtok_only=[B_PX])
        fw.op(v, lambda h, w=w: h.tensor_copy(grow[0:1, 0:w], PX[0:1, 128:128 + w]),
              reads=[B_PX], writes=[B_grow])
        fw.op(v, lambda h, nh=nh: h.tensor_max(m16a[0:1, 0:nh], grow[0:1, 0:nh],
                                               grow[0:1, nh:2 * nh]),
              reads=[B_grow], writes=[B_m16a])
        fw.op(v, lambda h, nh=nh: h.tensor_max(m16b[0:1, 0:nh], grow[0:1, 2 * nh:3 * nh],
                                               grow[0:1, 3 * nh:4 * nh]),
              reads=[B_grow], writes=[B_m16b])
        fw.op(v, lambda h, nh=nh: h.tensor_max(m16a[0:1, 0:nh], m16a[0:1, 0:nh],
                                               m16b[0:1, 0:nh]),
              reads=[B_m16b], writes=[B_m16a])
        if running:
            fw.op(v, lambda h, nh=nh: h.tensor_max(Mkrow[0:1, 0:nh], Mkrow[0:1, 0:nh],
                                                   m16a[0:1, 0:nh]),
                  reads=[B_m16a], writes=[B_Mkrow])

    fw.scope = "pro"
    _iotaF, B_iotaF = sbM("p_iotaF", [128, 128], F32)
    _wrk, B_wrk = sbM("p_wrk", [128, 32], F32)
    _angw, B_ang = sbM("p_ang", [128, 32], F32)
    _wri, B_wri = sbM("p_wri", [128, 32], I32)
    _invfF, B_invfF = sbM("p_invf", [128, 32], F32)
    _rowcol, B_rowcol = sbM("p_rowcol", [128, 1], F32)
    _ang0, B_ang0 = sbM("p_ang0", [128, 32], F32)
    iotaF = _iotaF[:, :]
    wrk = _wrk[:, :]
    angw = _angw[:, :]
    wri = _wri[:, :]
    invf_full = _invfF[:, :]
    rowcol = _rowcol[:, :]
    ang0 = _ang0[:, :]
    import os
    if os.environ.get("SBUF_DEBUG"):
        print("SBUF remaining at peak:", nc.sbuf_bytes_remaining)

    fw.op(g, lambda h: h.memset(epsc[:, 0:1], 1e-5), writes=[B_eps])
    fw.op(g, lambda h: h.memset(epsc[:, 1:2], 1e-6), writes=[B_eps])
    fw.op(g, lambda h: h.iota(rowidx[:, :], [[0, 1]], channel_multiplier=1,
                              allow_small_or_imprecise_dtypes=True), writes=[B_rowidx])
    fw.op(g, lambda h: h.iota(iotaF, [[1, 128]], channel_multiplier=0,
                              allow_small_or_imprecise_dtypes=True), writes=[B_iotaF])
    fw.op(g, lambda h: h.iota(invf_full, [[1, 32]], channel_multiplier=0,
                              allow_small_or_imprecise_dtypes=True), writes=[B_invfF])
    fw.op(g, lambda h: h.memset(ones_bf[:, :], 1.0), writes=[B_onesbf])
    fw.op(g, lambda h: h.memset(ones_f[:, :], 1.0), writes=[B_onesf])
    fw.op(g, lambda h: h.memset(MkAr[:, :], 0.0), writes=[B_MkAr])
    fw.op(g, lambda h: h.memset(MkBr[:, :], 0.0), writes=[B_MkBr])
    fw.op(v, lambda h: h.tensor_scalar(ident_bf[:, :], iotaF, rowidx[:, :], None, ALU.is_equal),
          reads=[B_iotaF, B_rowidx], writes=[B_idbf])
    fw.op(v, lambda h: h.tensor_scalar(ident_f[:, :], iotaF, rowidx[:, :], None, ALU.is_equal),
          reads=[B_iotaF, B_rowidx], writes=[B_idf])
    fw.op(v, lambda h: h.tensor_scalar(mask01[:, :], iotaF, rowidx[:, :], None, ALU.is_ge),
          reads=[B_iotaF, B_rowidx], writes=[B_mask])
    fw.op(s, lambda h: h.activation(invf_full, invf_full, AF.Exp, scale=-LN1E4 / 32.0),
          reads=[B_invfF], writes=[B_invfF])
    for n in ("wq_b", "wk_b", "wv_b", "wo_b"):
        fw.op(g, lambda h, n=n: h.dma_start(brow[n][0],
                                            W[n].rearrange("(o n) -> o n", o=1)),
              writes=[brow[n][1]], dma=ch_cw)
    fw.op(g, lambda h: h.dma_start(ln1b[:, :], sap(W["ln1_w"], 0, [[0, 128], [1, 1024]])),
          writes=[B_ln1b], dma=ch_cw)
    fw.op(g, lambda h: h.dma_start(ln2b[:, :], sap(W["ln2_w"], 0, [[0, 128], [1, 1024]])),
          writes=[B_ln2b], dma=ch_cw)
    fw.op(g, lambda h: h.dma_start(rw_sb[:, :].rearrange("p (c n) -> p c n", n=8),
                                   W["router_w"].rearrange("(c p) n -> p c n", p=128)),
          writes=[B_rw], dma=ch_cw)
    for bi in range(NT):
        fw.op(v, lambda h, bi=bi: h.tensor_scalar(rowcol, rowidx[:, :], 128.0 * bi,
                                                  None, ALU.add),
              reads=[B_rowidx], writes=[B_rowcol])
        fw.op(v, lambda h: h.tensor_scalar(ang0, invf_full, rowcol, None, ALU.mult),
              reads=[B_rowcol, B_invfF], writes=[B_ang0])
        fw.op(v, lambda h: h.tensor_scalar(wrk, ang0, 1.0 / (2 * PI), None, ALU.mult),
              reads=[B_ang0], writes=[B_wrk])
        fw.op(v, lambda h: h.tensor_copy(wri, wrk), reads=[B_wrk], writes=[B_wri])
        fw.op(v, lambda h: h.tensor_copy(wrk, wri), reads=[B_wri], writes=[B_wrk])
        fw.op(v, lambda h: h.scalar_tensor_tensor(angw, wrk, -2 * PI, ang0,
                                                  ALU.mult, ALU.add),
              reads=[B_wrk, B_ang0], writes=[B_ang])
        fw.op(v, lambda h: h.tensor_scalar(wrk, angw, PI, None, ALU.is_gt),
              reads=[B_ang], writes=[B_wrk])
        fw.op(v, lambda h: h.scalar_tensor_tensor(angw, wrk, -2 * PI, angw,
                                                  ALU.mult, ALU.add),
              reads=[B_wrk, B_ang], writes=[B_ang])
        fw.op(v, lambda h: h.tensor_scalar(wrk, angw, -PI, None, ALU.is_lt),
              reads=[B_ang], writes=[B_wrk])
        fw.op(v, lambda h: h.scalar_tensor_tensor(angw, wrk, 2 * PI, angw,
                                                  ALU.mult, ALU.add),
              reads=[B_wrk, B_ang], writes=[B_ang])
        fw.op(s, lambda h, bi=bi: h.activation(sin_tm[:, 32 * bi:32 * bi + 32],
                                               angw, AF.Sin),
              reads=[B_ang], writes=[B_sin])
        fw.op(v, lambda h: h.tensor_scalar(wrk, angw, PI / 2, None, ALU.add),
              reads=[B_ang], writes=[B_wrk])
        fw.op(v, lambda h: h.tensor_scalar(angw, wrk, PI, None, ALU.is_gt),
              reads=[B_wrk], writes=[B_ang])
        fw.op(v, lambda h: h.scalar_tensor_tensor(wrk, angw, -2 * PI, wrk,
                                                  ALU.mult, ALU.add),
              reads=[B_ang, B_wrk], writes=[B_wrk])
        fw.op(s, lambda h, bi=bi: h.activation(cos_tm[:, 32 * bi:32 * bi + 32],
                                               wrk, AF.Sin),
              reads=[B_wrk], writes=[B_cos])

    def prep_tm(ph, gg):
        isA = ph == "A"
        fw.scope = "p" + ph
        for jj in range(NJ):
            i = 4 * gg + jj
            lx, B_lx = lnx4[jj]
            lT, B_lT = lnxT4[jj]
            transposes(lx, B_lx, 8,
                       lambda c, pv, Bp, lT=lT, B_lT=B_lT: fw.op(
                           v, lambda h, c=c, pv=pv: h.tensor_copy(
                               lT[:, 128 * c:128 * c + 128], pv),
                           reads=[Bp], writes=[B_lT]))
        if isA:
            passes = (("wk_w", "wk_b", True), ("wv_w", "wv_b", False))
        else:
            passes = (("fwk", None, True),)
        for (wname, bname, is_k) in passes:
            for c in range(8):
                wsl, B_wsl = wos[c % 2]
                if isA:
                    fw.op(g, lambda h, c=c, wsl=wsl, wname=wname: h.dma_start(
                        wsl[:, :], W[wname][128 * c:128 * c + 128, :]),
                        writes=[B_wsl], dma=ch_wo[c % 2])
                else:
                    fw.op(g, lambda h, c=c, wsl=wsl: h.dma_start(
                        wsl[:, 0:512], W["fwk"][128 * c:128 * c + 128, :]),
                        writes=[B_wsl], dma=ch_wo[c % 2])
                    fw.op(g, lambda h, c=c, wsl=wsl: h.dma_start(
                        wsl[:, 512:1024], W["fwv"][128 * c:128 * c + 128, :]),
                        wtok_only=[B_wsl], dma=ch_wo[c % 2])
                for jj in range(NJ):
                    PP, B_PP = PS[jj]
                    lT, B_lT = lnxT4[jj]
                    def kmm(h, c=c, jj=jj, wsl=wsl, PP=PP, lT=lT, bname=bname, isA=isA):
                        for dh in range(2):
                            first = (c == 0)
                            if first and isA:
                                r = brow[bname][2]
                                h.matmul(PP[:, 512 * dh:512 * dh + 512],
                                         ones_bf[r:r + 1, :],
                                         brow[bname][0][0:1, 512 * dh:512 * dh + 512],
                                         start=True, stop=False)
                                first = False
                            ii = h.matmul(PP[:, 512 * dh:512 * dh + 512],
                                          lT[:, 128 * c:128 * c + 128],
                                          wsl[:, 512 * dh:512 * dh + 512],
                                          start=first, stop=(c == 7))
                        return ii
                    rds = [B_lT, B_wsl] + ([brow[bname][1], B_onesbf] if (c == 0 and isA) else [])
                    if c == 0:
                        fw.op(t, kmm, reads=rds, writes=[B_PP])
                    else:
                        fw.op(t, kmm, reads=rds, wtok_only=[B_PP])
            for jj in range(NJ):
                i = 4 * gg + jj
                PP, B_PP = PS[jj]
                lx, B_lx = lnx4[jj]
                if is_k:
                    nh = 16 if isA else 8
                    rot, B_rot = rot2[jj % 2]
                    rope(PP[:, :], [B_PP], rot, B_rot, i, nh)
                    head_sq(nh, jj, rot, B_rot, lx, B_lx)
                    own = [(sap(psb[jj], 0, [[2048, 128], [1, 128]]), B_PP),
                           (sap(psb[jj], 1024, [[2048, 128], [1, 128]]), B_PP)]
                    if isA:
                        transposes(rot, B_rot, 8,
                                   lambda c, pv, Bp, i=i: fw.op(
                                       s, lambda h, c=c, pv=pv, i=i: h.activation(
                                           KT[:, S * c + 128 * i:S * c + 128 * i + 128], pv,
                                           AF.Copy),
                                       reads=[Bp], writes=[B_KT]),
                                   slots=own)
                    else:
                        fw.op(v, lambda h, jj=jj, i=i, PP=PP: h.tensor_copy(
                            sap(V_tm[:, :], 520 * i, [[1040 * NT, 128], [65, 8], [1, 64]]),
                            sap(PP[:, :], 512, [[1024, 128], [64, 8], [1, 64]])),
                            reads=[B_PP], writes=[B_V])
                        fw.op(v, lambda h, i=i: h.memset(
                            sap(V_tm[:, :], 520 * i + 64, [[1040 * NT, 128], [65, 8], [1, 1]]), 1.0),
                            writes=[B_V])
                        for cc in range(4):
                            pv, Bp = own[cc % 2]
                            fw.op(t, lambda h, cc=cc, pv=pv, rot=rot: h.transpose(
                                pv, rot[:, 128 * cc:128 * cc + 128], ident_bf[:, :]),
                                reads=[B_rot, B_idbf], writes=[Bp])
                            for gg2 in range(2):
                                kvh = 2 * cc + gg2
                                sub = pv[64 * gg2:64 * gg2 + 64, 0:128]
                                for half in range(2):
                                    fw.op(s, lambda h, kvh=kvh, half=half, sub=sub, i=i:
                                          h.activation(
                                              KT[64 * half:64 * half + 64,
                                                 S * kvh + 128 * i:S * kvh + 128 * i + 128],
                                              sub, AF.Copy),
                                          reads=[Bp], writes=[B_KT])
                else:
                    for dh in range(2):
                        fw.op(v, lambda h, dh=dh, i=i, PP=PP: h.tensor_copy(
                            sap(V_tm[:, :], 1040 * i + 65 * 8 * dh, [[1040 * NT, 128], [65, 8], [1, 64]]),
                            sap(PP[:, :], 512 * dh, [[1024, 128], [64, 8], [1, 64]])),
                            reads=[B_PP], writes=[B_V])
                    fw.op(v, lambda h, i=i: h.memset(
                        sap(V_tm[:, :], 1040 * i + 64, [[1040 * NT, 128], [65, 16], [1, 1]]), 1.0),
                        writes=[B_V])
            if is_k:
                group_norms(16 if isA else 8, MkAr if isA else MkBr,
                            B_MkAr if isA else B_MkBr, running=True)

    def qstage(ph, tq):
        isA = ph == "A"
        fw.scope = "q" + ph
        Mkrow, B_Mkrow = (MkAr, B_MkAr) if isA else (MkBr, B_MkBr)
        wqn = "wq_w" if isA else "fwq"
        for c in range(8):
            wsl, B_wsl = wos[c % 2]
            fw.op(g, lambda h, c=c, wsl=wsl, wqn=wqn: h.dma_start(
                wsl[:, :], W[wqn][128 * c:128 * c + 128, :]),
                writes=[B_wsl], dma=ch_wo[c % 2])
            for jj in range(NJ):
                PQ_, B_PQ_ = PS[jj]
                lT, B_lT = lnxT4[jj]
                def qmm(h, c=c, jj=jj, wsl=wsl, PQ_=PQ_, lT=lT):
                    for dh in range(2):
                        first = (c == 0)
                        if first and isA:
                            h.matmul(PQ_[:, 512 * dh:512 * dh + 512], ones_bf[0:1, :],
                                     brow["wq_b"][0][0:1, 512 * dh:512 * dh + 512],
                                     start=True, stop=False)
                            first = False
                        ii = h.matmul(PQ_[:, 512 * dh:512 * dh + 512],
                                      lT[:, 128 * c:128 * c + 128],
                                      wsl[:, 512 * dh:512 * dh + 512],
                                      start=first, stop=(c == 7))
                    return ii
                rds = [B_lT, B_wsl] + ([brow["wq_b"][1], B_onesbf]
                                       if (c == 0 and isA) else [])
                if c == 0:
                    fw.op(t, qmm, reads=rds, writes=[B_PQ_])
                else:
                    fw.op(t, qmm, reads=rds, wtok_only=[B_PQ_])
        for jj in range(NJ):
            j = NJ * tq + jj
            PQ_, B_PQ_ = PS[jj]
            lx, B_lx = lnx4[jj]
            rot, B_rot = rot2[jj % 2]
            rope(PQ_[:, :], [B_PQ_], rot, B_rot, j, 16)
            head_sq(16, jj, rot, B_rot, lx, B_lx)
            pqb = PQ_[:, :].bitcast(BF16)
            tsl_q = [(sap(pqb, 0, [[2048, 128], [1, 128]]), B_PQ_),
                     (sap(pqb, 1024, [[2048, 128], [1, 128]]), B_PQ_)]
            for c in range(8):
                pv, Bp = tsl_q[c % 2]
                fw.op(t, lambda h, c=c, pv=pv, rot=rot: h.transpose(
                    pv, rot[:, 128 * c:128 * c + 128], ident_bf[:, :]),
                    reads=[B_rot, B_idbf], writes=[Bp])
                fw.op(s, lambda h, c=c, pv=pv, jj=jj: h.activation(
                    QT[:, QW * c + 128 * jj:QW * c + 128 * jj + 128], pv, AF.Copy),
                    reads=[Bp], writes=[B_QT])
        group_norms(16, None, None, running=False)
        if isA:
            fw.op(v, lambda h: h.tensor_tensor(kp[:, :],
                                               sap(Mkrow[:, :], 0, [[16, 1], [2, 8]]),
                                               sap(Mkrow[:, :], 1, [[16, 1], [2, 8]]),
                                               ALU.max),
                  reads=[B_Mkrow], writes=[B_kp])
        else:
            fw.op(v, lambda h: h.tensor_copy(kp[:, :], Mkrow[0:1, 0:8]),
                  reads=[B_Mkrow], writes=[B_kp])
        fw.op(v, lambda h: h.tensor_tensor(prow[:, :],
                                           sap(m16a[:, :], 0, [[16, 1], [2, 8]]),
                                           sap(m16a[:, :], 1, [[16, 1], [2, 8]]),
                                           ALU.max),
              reads=[B_m16a], writes=[B_prow])
        fw.op(v, lambda h: h.tensor_tensor(prow[:, :], prow[:, :], kp[:, :], ALU.mult),
              reads=[B_kp, B_prow], writes=[B_prow])
        fw.op(s, lambda h: h.activation(prow[:, :], prow[:, :], AF.Sqrt),
              reads=[B_prow], writes=[B_prow])
        fw.op(v, lambda h: h.tensor_scalar(prow[:, :], prow[:, :], -0.125, None, ALU.mult),
              reads=[B_prow], writes=[B_prow])
        fw.op(t, lambda h: h.matmul(PS[2][0][:, 512:520], ones_f[:, :], prow[:, :],
                                    start=True, stop=True),
              reads=[B_onesf, B_prow], writes=[PS[2][1]])
        fw.op(v, lambda h: h.tensor_copy(pairb[:, :], PS[2][0][:, 512:520]),
              reads=[PS[2][1]], writes=[B_pairb])

    def pairs(ph, tq):
        isA = ph == "A"
        fw.scope = "s" + ph
        VST = 1040 if isA else 520
        nkb = NJ * (tq + 1)

        def smm_emit(c, kb):
            PSc, B_PSc = PS[kb % 2]
            qoff = 128 * (kb - NJ * tq) if kb >= NJ * tq else 0
            def smm(h, c=c, kb=kb, qoff=qoff, PSc=PSc):
                h.matmul(PSc[:, qoff:512],
                         KT[0:64, S * c + 128 * kb:S * c + 128 * kb + 128],
                         QT[0:64, QW * c + qoff:QW * c + QW],
                         start=True, stop=True, tile_position=(0, 0))
                return h.matmul(PSc[:, 512 + qoff:1024],
                                KT[64:128, S * c + 128 * kb:S * c + 128 * kb + 128],
                                QT[64:128, QW * c + qoff:QW * c + QW],
                                start=True, stop=True, tile_position=(64, 0))
            fw.op(t, smm, reads=[B_KT, B_QT], writes=[B_PSc])

        def finalize(c):
            PC, B_PC = PS[2 + (c % 2)]
            for half in range(2):
                fw.op(v, lambda h, half=half, PC=PC: h.reciprocal(
                    linvb[0:1, :], PC[64:65, 512 * half:512 * half + 512]),
                    reads=[B_PC], writes=[B_linvb])
                fw.op(t, lambda h, half=half, PC=PC: h.matmul(
                    PC[64:128, 512 * half:512 * half + 512],
                    ones_f[0:1, 0:64], linvb[0:1, :], start=True, stop=True,
                    tile_position=(0, 64)),
                    reads=[B_onesf, B_linvb], wtok_only=[B_PC])
                fw.op(v, lambda h, half=half, PC=PC: h.tensor_copy(
                    linvb[:, :], PC[64:128, 512 * half:512 * half + 512]),
                    reads=[B_PC], writes=[B_linvb])
                fw.op(v, lambda h, c=c, half=half, PC=PC: h.tensor_tensor(
                    ctxT[64 * half:64 * half + 64, QW * c:QW * c + QW],
                    PC[0:64, 512 * half:512 * half + 512],
                    linvb[0:64, 0:512], ALU.mult),
                    reads=[B_PC, B_linvb], writes=[B_ctxT])

        smm_emit(0, 0)
        if nkb > 1:
            smm_emit(0, 1)
        for c in range(8):
            PC, B_PC = PS[2 + (c % 2)]
            for kb in range(nkb):
                PSc, B_PSc = PS[kb % 2]
                qoff = 128 * (kb - NJ * tq) if kb >= NJ * tq else 0
                Asb, B_Asb = A_sb[kb % 2]
                if qoff == 0:
                    fw.op(s, lambda h, c=c, PSc=PSc, Asb=Asb: h.activation(
                        Asb[:, :], PSc[:, :], AF.Exp,
                        bias=pairb[:, c:c + 1], scale=0.125),
                        reads=[B_PSc, B_pairb], writes=[B_Asb])
                else:
                    fw.op(s, lambda h, PSc=PSc, Asb=Asb, c=c, qoff=qoff: h.activation(
                        Asb[:, qoff:512], PSc[:, qoff:512], AF.Exp,
                        bias=pairb[:, c:c + 1], scale=0.125),
                        reads=[B_PSc, B_pairb], writes=[B_Asb])
                    fw.op(s, lambda h, PSc=PSc, Asb=Asb, c=c, qoff=qoff: h.activation(
                        Asb[:, 512 + qoff:1024], PSc[:, 512 + qoff:1024], AF.Exp,
                        bias=pairb[:, c:c + 1], scale=0.125),
                        reads=[B_PSc, B_pairb], writes=[B_Asb])
                if kb >= NJ * tq:
                    for half in range(2):
                        o = 512 * half + qoff
                        fw.op(v, lambda h, o=o, Asb=Asb: h.tensor_tensor(
                            Asb[:, o:o + 128], Asb[:, o:o + 128], mask01[:, :], ALU.mult),
                            reads=[B_mask, B_Asb], writes=[B_Asb])
                hA = 65 * (2 * c) if isA else 65 * c
                hB = 65 * (2 * c + 1) if isA else 65 * c
                def avmm(h, kb=kb, qoff=qoff, Asb=Asb, hA=hA, hB=hB, PC=PC, VST=VST,
                         last=(kb == nkb - 1)):
                    h.matmul(PC[0:65, qoff:512],
                             sap(V_tm[:, :], VST * kb + hA, [[1040 * NT, 128], [1, 65]]),
                             Asb[:, qoff:512], start=(kb == 0), stop=last)
                    return h.matmul(PC[0:65, 512 + qoff:1024],
                                    sap(V_tm[:, :], VST * kb + hB, [[1040 * NT, 128], [1, 65]]),
                                    Asb[:, 512 + qoff:1024], start=(kb == 0), stop=last)
                if kb == 0:
                    fw.op(t, avmm, reads=[B_V, B_Asb], writes=[B_PC])
                else:
                    fw.op(t, avmm, reads=[B_V, B_Asb], wtok_only=[B_PC])
                if kb + 2 < nkb:
                    smm_emit(c, kb + 2)
                if kb == 0 and c >= 1:
                    finalize(c - 1)
            if c < 7:
                smm_emit(c + 1, 0)
                if nkb > 1:
                    smm_emit(c + 1, 1)
        finalize(7)

    def wo_evac(ph, tq):
        isA = ph == "A"
        fw.scope = "o" + ph
        won = "wo_w" if isA else "fwo"
        for c in range(8):
            wsl, B_wsl = wos[c % 2]
            fw.op(g, lambda h, c=c, wsl=wsl, won=won: h.dma_start(
                wsl[:, :], W[won][128 * c:128 * c + 128, :]),
                writes=[B_wsl], dma=ch_wo[c % 2])
            for jj in range(NJ):
                PT_, B_PT_ = PS[jj]
                for dh in range(2):
                    def omm(h, c=c, jj=jj, dh=dh, wsl=wsl, PT_=PT_):
                        first = (c == 0)
                        if first and isA:
                            h.matmul(PT_[:, 512 * dh:512 * dh + 512], ones_bf[0:1, :],
                                     brow["wo_b"][0][0:1, 512 * dh:512 * dh + 512],
                                     start=True, stop=False)
                            first = False
                        return h.matmul(PT_[:, 512 * dh:512 * dh + 512],
                                        ctxT[:, QW * c + 128 * jj:QW * c + 128 * jj + 128],
                                        wsl[:, 512 * dh:512 * dh + 512],
                                        start=first, stop=(c == 7))
                    rds = [B_ctxT, B_wsl] + ([brow["wo_b"][1], B_onesbf]
                                             if (c == 0 and isA) else [])
                    if c == 0 and dh == 0:
                        fw.op(t, omm, reads=rds, writes=[B_PT_])
                    else:
                        fw.op(t, omm, reads=rds, wtok_only=[B_PT_])
        p = tq % 2
        for jj in range(NJ):
            j = NJ * tq + jj
            sl = jj % 2
            PT_, B_PT_ = PS[jj]
            xb, B_xb = t12, B_t12
            col, B_col = cols[p][jj]
            src = X if isA else X1S
            B_src = None if isA else B_x1s
            fw.op(sy, lambda h, j=j, xb=xb, src=src: h.dma_start(
                xb[:, :], src[128 * j:128 * j + 128, :]),
                reads=[B_src] if B_src else [],
                writes=[B_xb], dma=ch_xq[sl])
            if isA:
                for dh in range(2):
                    fw.op(v, lambda h, dh=dh, PT_=PT_, xb=xb: h.tensor_tensor(
                        xb[:, 512 * dh:512 * dh + 512],
                        xb[:, 512 * dh:512 * dh + 512],
                        PT_[:, 512 * dh:512 * dh + 512], ALU.add),
                        reads=[B_PT_], writes=[B_xb])
                fw.op(g, lambda h, j=j, xb=xb: h.dma_start(
                    X1S[128 * j:128 * j + 128, :], xb[:, :]),
                    reads=[B_xb], writes=[B_x1s], dma=ch_sx1)
            else:
                fw.op(v, lambda h, xb=xb, col=col: h.tensor_scalar(
                    x2b, xb[:, :], col[:, 1:2], col[:, 4:5], ALU.subtract, op1=ALU.mult),
                    reads=[B_xb, B_col], writes=[B_x2])
                for dh in range(2):
                    fw.op(v, lambda h, dh=dh, PT_=PT_: h.tensor_tensor(
                        sap(x2b, x2b.offset + 512 * dh, [[16640, 128], [1, 512]]),
                        sap(x2b, x2b.offset + 512 * dh, [[16640, 128], [1, 512]]),
                        PT_[:, 512 * dh:512 * dh + 512], ALU.add),
                        reads=[B_PT_], writes=[B_x2])
                fw.op(v, lambda h, xb=xb: h.tensor_tensor(y1B, xb[:, :], x2b, ALU.add),
                      reads=[B_xb, B_x2], writes=[B_y1])
                fw.op(g, lambda h, j=j: h.dma_start(
                    RS[128 * j:128 * j + 128, :], y1B),
                    reads=[B_y1], writes=[B_rs], dma=ch_sr)
                fw.op(s, lambda h, col=col: h.activation(h2b, x2b, AF.Square,
                                                         accum_out=col[:, 6:7]),
                      reads=[B_x2], writes=[B_h2, B_col])
                fw.op(s, lambda h, col=col: h.activation(col[:, 6:7], col[:, 6:7], AF.Sqrt,
                                                         bias=epsc[:, 1:2], scale=1.0 / 1024),
                      reads=[B_col, B_eps], writes=[B_col])
                fw.op(v, lambda h, col=col: h.reciprocal(col[:, 6:7], col[:, 6:7]),
                      reads=[B_col], writes=[B_col])
                fw.op(v, lambda h, col=col, xb=xb: h.tensor_scalar(xb[:, :], x2b, col[:, 6:7],
                                                                   None, ALU.mult),
                      reads=[B_x2, B_col], writes=[B_xb])
                fw.op(v, lambda h, xb=xb: h.tensor_tensor(h2b, xb[:, :], ln2b[:, :], ALU.mult),
                      reads=[B_ln2b, B_xb], writes=[B_h2])
                fw.op(g, lambda h, j=j: h.dma_start(
                    H2S[128 * j:128 * j + 128, :], h2b),
                    reads=[B_h2], writes=[B_h2s], dma=ch_sh2)

    for ph in ("A", "B"):
        for gg in range(NQ):
            if gg == 0:
                fw.scope = "f" + ph
                for jj in range(NJ):
                    front(ph, 0, jj)
            prep_tm(ph, gg)
            qstage(ph, gg)
            if gg + 1 < NQ:
                fw.scope = "f" + ph
                for jj in range(NJ):
                    front(ph, gg + 1, jj)
            pairs(ph, gg)
            wo_evac(ph, gg)
    fw.scope = "mend"
    fw.drain_chans()
    with nc.Block("main") as blkM:
        fw.flush(blkM)

    scM.close()
    fw.new_epoch()

    with contextlib.ExitStack() as pc:
        psA = lambda n, shp, dt: (pc.enter_context(nc.psum_tensor(n, shp, dt)), fw.buf(n))
        sbl = lambda n, shp, dt: (pc.enter_context(nc.sbuf_tensor(n, shp, dt)), fw.buf(n))
        PG = [psA(f"PG{k}", [128, 512], F32) for k in range(2)]
        PU = [psA(f"PU{k}", [128, 512], F32) for k in range(2)]
        PD = [psA(f"PD{k}", [128, 1024], F32) for k in range(2)]
        acc, B_acc = sbl("acc", [128, HNT * 1024], F32)
        h2l, B_h2l = sbl("h2l", [128, HTT * 4096], BF16)
        h2st = [sbl(f"h2st{k}", [128, 1024], BF16) for k in range(2)]
        wcol, B_wcol = sbl("wcol", [128, 8 * HNT], F32)
        e8, B_e8 = sbl("e8", [128, 8], F32)
        w8, B_w8 = sbl("w8", [128, 8], F32)
        rcol, B_rcol = sbl("rcol", [128, 6], F32)
        wg_b = [sbl(f"wg{k}", [128, 1024], BF16) for k in range(2 * MB)]
        wu_b = [sbl(f"wu{k}", [128, 1024], BF16) for k in range(2 * MB)]
        wd_b = [sbl(f"wd{k}", [128, 1024], BF16) for k in range(MB)]
        silu_b = [sbl(f"silu{k}", [128, 512], BF16) for k in range(2)]
        actp, B_actp = sbl("actp", [128, MB * 512], BF16)
        pdb = [PD[k][0][:, :].bitcast(BF16) for k in range(2)]
        mslots = [(sap(pdb[0], 0, [[2048, 128], [1, 128]]), PD[0][1]),
                  (sap(pdb[1], 0, [[2048, 128], [1, 128]]), PD[1][1]),
                  (sap(pdb[0], 1024, [[2048, 128], [1, 128]]), PD[0][1]),
                  (sap(pdb[1], 1024, [[2048, 128], [1, 128]]), PD[1][1])]

        fw.scope = "moe"
        for hh in range(CH):
            t0 = hh * HT
            fw.op(sy, lambda h, t0=t0: h.dma_start(
                acc[:, :].rearrange("p (i n) -> p i n", n=1024),
                sap(RS, t0 * 1024, [[1024, 128], [131072, HNT], [1, 1024]])),
                reads=[B_rs], writes=[B_acc], dma=ch_acc)
            for it in range(HNT):
                tt, sub = it // 4, it % 4
                hb, B_hb = h2st[it % 2]
                fw.op(sy, lambda h, t0=t0, it=it, hb=hb: h.dma_start(
                    hb[:, :], H2S[t0 + 128 * it:t0 + 128 * it + 128, :]),
                    reads=[B_h2s], writes=[B_hb], dma=ch_h2l[it % 2])
                for c in range(8):
                    pv, Bp = mslots[c % 4]
                    fw.op(t, lambda h, c=c, pv=pv, hb=hb: h.transpose(
                        pv, hb[:, 128 * c:128 * c + 128], ident_bf[:, :]),
                        reads=[B_hb, B_idbf], writes=[Bp])
                    fw.op(v, lambda h, c=c, pv=pv, tt=tt, sub=sub: h.tensor_copy(
                        h2l[:, 4096 * tt + 512 * c + 128 * sub:
                            4096 * tt + 512 * c + 128 * sub + 128], pv),
                        reads=[Bp], writes=[B_h2l])
            for it in range(HNT):
                tt, sub = it // 4, it % 4
                pd, B_pd = PD[it % 2]
                def rmm(h, tt=tt, sub=sub, pd=pd):
                    for cc in range(8):
                        ii = h.matmul(pd[:, 512:520],
                                      h2l[:, 4096 * tt + 512 * cc + 128 * sub:
                                          4096 * tt + 512 * cc + 128 * sub + 128],
                                      rw_sb[:, 8 * cc:8 * cc + 8],
                                      start=(cc == 0), stop=(cc == 7))
                    return ii
                fw.op(t, rmm, reads=[B_h2l, B_rw], writes=[B_pd])
                fw.op(v, lambda h, pd=pd: h.reduce_max(rcol[:, 0:1], pd[:, 512:520], AX.X),
                      reads=[B_pd], writes=[B_rcol])
                fw.op(v, lambda h: h.tensor_scalar(rcol[:, 1:2], rcol[:, 0:1], -1.0, None, ALU.mult),
                      reads=[B_rcol], writes=[B_rcol])
                fw.op(s, lambda h, pd=pd: h.activation(e8[:, :], pd[:, 512:520], AF.Exp,
                                                       bias=rcol[:, 1:2]),
                      reads=[B_pd, B_rcol], writes=[B_e8])
                fw.op(v, lambda h: h.reduce_max(rcol[:, 2:3], e8[:, :], AX.X),
                      reads=[B_e8], writes=[B_rcol])
                fw.op(v, lambda h: h.tensor_scalar(w8[:, :], e8[:, :], rcol[:, 2:3], None, ALU.is_equal),
                      reads=[B_e8, B_rcol], writes=[B_w8])
                fw.op(v, lambda h: h.scalar_tensor_tensor(w8[:, :], w8[:, :], -2.0, e8[:, :],
                                                          ALU.mult, ALU.add),
                      reads=[B_w8, B_e8], writes=[B_w8])
                fw.op(v, lambda h: h.reduce_max(rcol[:, 3:4], w8[:, :], AX.X),
                      reads=[B_w8], writes=[B_rcol])
                fw.op(v, lambda h: h.tensor_tensor(rcol[:, 4:5], rcol[:, 2:3], rcol[:, 3:4], ALU.add),
                      reads=[B_rcol], writes=[B_rcol])
                fw.op(v, lambda h: h.reciprocal(rcol[:, 4:5], rcol[:, 4:5]),
                      reads=[B_rcol], writes=[B_rcol])
                fw.op(v, lambda h: h.tensor_scalar(w8[:, :], e8[:, :], rcol[:, 3:4], None, ALU.is_ge),
                      reads=[B_e8, B_rcol], writes=[B_w8])
                fw.op(v, lambda h: h.tensor_tensor(w8[:, :], w8[:, :], e8[:, :], ALU.mult),
                      reads=[B_e8, B_w8], writes=[B_w8])
                fw.op(v, lambda h, it=it: h.tensor_scalar(
                    wcol[:, 8 * it:8 * it + 8], w8[:, :], rcol[:, 4:5], None, ALU.mult),
                    reads=[B_rcol, B_w8], writes=[B_wcol])
            for e in range(8):
                for b in range(NB):
                    par = (e * NB + b) % 2
                    for mi in range(MB):
                        mc = MB * b + mi
                        sl = par * MB + mi
                        fw.op(g, lambda h, e=e, mc=mc, sl=sl: h.dma_start(
                            wg_b[sl][0][:, :].rearrange("p (c n) -> p c n", n=128),
                            sap(W["w_gate"], e * 1024 * 3584 + 128 * mc,
                                [[3584, 128], [128 * 3584, 8], [1, 128]])),
                            writes=[wg_b[sl][1]], dma=ch_wg[sl])
                        fw.op(g, lambda h, e=e, mc=mc, sl=sl: h.dma_start(
                            wu_b[sl][0][:, :].rearrange("p (c n) -> p c n", n=128),
                            sap(W["w_up"], e * 1024 * 3584 + 128 * mc,
                                [[3584, 128], [128 * 3584, 8], [1, 128]])),
                            writes=[wu_b[sl][1]], dma=ch_wu[sl])
                        fw.op(g, lambda h, e=e, mc=mc, mi=mi: h.dma_start(
                            wd_b[mi][0][:, :],
                            sap(W["w_down"], e * 3584 * 1024 + 128 * mc * 1024,
                                [[1024, 128], [1, 1024]])),
                            writes=[wd_b[mi][1]], dma=ch_wdd[par * MB + mi])
                    for tt in range(HTT):
                        for mi in range(MB):
                            sl = par * MB + mi
                            pg, B_pg = PG[mi % 2]
                            pu, B_pu = PU[mi % 2]
                            def gmm(h, tt=tt, sl=sl, pg=pg):
                                for kc in range(8):
                                    ii = h.matmul(pg[:, :], wg_b[sl][0][:, 128 * kc:128 * kc + 128],
                                                  h2l[:, 4096 * tt + 512 * kc:4096 * tt + 512 * kc + 512],
                                                  start=(kc == 0), stop=(kc == 7))
                                return ii
                            fw.op(t, gmm, reads=[wg_b[sl][1], B_h2l], writes=[B_pg])
                            def umm(h, tt=tt, sl=sl, pu=pu):
                                for kc in range(8):
                                    ii = h.matmul(pu[:, :], wu_b[sl][0][:, 128 * kc:128 * kc + 128],
                                                  h2l[:, 4096 * tt + 512 * kc:4096 * tt + 512 * kc + 512],
                                                  start=(kc == 0), stop=(kc == 7))
                                return ii
                            fw.op(t, umm, reads=[wu_b[sl][1], B_h2l], writes=[B_pu])
                            sb_, B_sb_ = silu_b[mi % 2]
                            fw.op(s, lambda h, pg=pg, sb_=sb_: h.activation(sb_[:, :], pg[:, :], AF.Silu),
                                  reads=[B_pg], writes=[B_sb_])
                            fw.op(v, lambda h, mi=mi, pu=pu, sb_=sb_: h.tensor_tensor(
                                actp[:, 512 * mi:512 * mi + 512], sb_[:, :], pu[:, :], ALU.mult),
                                reads=[B_sb_, B_pu], writes=[B_actp])
                        for jj in range(4):
                            pd, B_pd = PD[jj % 2]
                            for mi in range(MB):
                                def dmm(h, jj=jj, pd=pd, mi=mi):
                                    h.matmul(pd[:, 0:512],
                                             actp[:, 512 * mi + 128 * jj:512 * mi + 128 * jj + 128],
                                             wd_b[mi][0][:, 0:512],
                                             start=(mi == 0), stop=(mi == MB - 1))
                                    return h.matmul(pd[:, 512:1024],
                                                    actp[:, 512 * mi + 128 * jj:512 * mi + 128 * jj + 128],
                                                    wd_b[mi][0][:, 512:1024],
                                                    start=(mi == 0), stop=(mi == MB - 1))
                                if mi == 0:
                                    fw.op(t, dmm, reads=[B_actp, wd_b[mi][1]], writes=[B_pd])
                                else:
                                    fw.op(t, dmm, reads=[B_actp, wd_b[mi][1]], wtok_only=[B_pd])
                            it = 4 * tt + jj
                            fw.op(v, lambda h, it=it, pd=pd, e=e: h.scalar_tensor_tensor(
                                acc[:, 1024 * it:1024 * it + 1024], pd[:, :],
                                wcol[:, 8 * it + e:8 * it + e + 1],
                                acc[:, 1024 * it:1024 * it + 1024],
                                ALU.mult, ALU.add),
                                reads=[B_pd, B_wcol, B_acc], writes=[B_acc])
            fw.op(sy, lambda h, t0=t0: h.dma_start(
                sap(OUT, t0 * 1024, [[1024, 128], [131072, HNT], [1, 1024]]),
                acc[:, :].rearrange("p (i n) -> p i n", n=1024)),
                reads=[B_acc], dma=ch_out)
        fw.drain_chans()
        with nc.Block("moe") as blk:
            fw.flush(blk)

    ctx.close()
    return nc


_NC_CACHE = {}


def kernel(**inputs):
    x = np.ascontiguousarray(inputs["L0_attn_x"], dtype=np.float32)
    B, S, D = x.shape
    if S not in _NC_CACHE:
        _NC_CACHE[S] = build_program(S)
    nc = _NC_CACHE[S]
    wmap = {
        "wq_w": inputs["L0_attn_wq_w"], "wq_b": inputs["L0_attn_wq_b"],
        "wk_w": inputs["L0_attn_wk_w"], "wk_b": inputs["L0_attn_wk_b"],
        "wv_w": inputs["L0_attn_wv_w"], "wv_b": inputs["L0_attn_wv_b"],
        "wo_w": inputs["L0_attn_wo_w"], "wo_b": inputs["L0_attn_wo_b"],
        "ln1_w": inputs["L0_ffn_ln1_w"], "ln2_w": inputs["L0_ffn_ln2_w"],
        "fwq": inputs["L0_ffn_wq"], "fwk": inputs["L0_ffn_wk"],
        "fwv": inputs["L0_ffn_wv"], "fwo": inputs["L0_ffn_wo"],
        "router_w": inputs["L0_ffn_router_w"], "w_gate": inputs["L0_ffn_w_gate"],
        "w_up": inputs["L0_ffn_w_up"], "w_down": inputs["L0_ffn_w_down"],
    }
    wmap = {k: np.ascontiguousarray(v, dtype=np.float32) for k, v in wmap.items()}
    in_maps = [dict(x=np.ascontiguousarray(x[i]), **wmap) for i in range(B)]
    res = run_bass_kernel_spmd(nc, in_maps, core_ids=list(range(B)))
    return np.stack([np.asarray(res.results[i]["out"]) for i in range(B)], axis=0)


# version 17
# speedup vs baseline: 1.2514x; 1.0308x over previous
import contextlib
import numpy as np
import concourse.bass as bass
import concourse.mybir as mybir
from concourse.bass_utils import run_bass_kernel_spmd

F32 = mybir.dt.float32
BF16 = mybir.dt.bfloat16
I32 = mybir.dt.int32
AF = mybir.ActivationFunctionType
ALU = mybir.AluOpType
AX = mybir.AxisListType
PI = float(np.pi)
LN1E4 = float(np.log(10000.0))

ENGINES = ("sync", "scalar", "vector", "tensor", "gpsimd")


class Buf:
    __slots__ = ("name", "wtok", "rtoks")

    def __init__(self, name="buf"):
        self.name = name
        self.wtok = None
        self.rtoks = []


class Chan:
    __slots__ = ("name", "sem", "count", "touched", "eng")

    def __init__(self, name, sem, eng):
        self.name = name
        self.sem = sem
        self.count = 0
        self.touched = []
        self.eng = eng


class FW:
    def __init__(self, nc, ctx):
        self.nc = nc
        self.ctx = ctx
        self.streams = {e: [] for e in ENGINES}
        self.esem = {}
        self.epos = {e: 0 for e in ENGINES}
        self._semkey = {}
        self._epoch = -1
        self.wm = {e: {} for e in ENGINES}
        self.chans = []
        self.scope = "x"
        self.new_epoch()

    def new_epoch(self):
        self._epoch += 1
        for e in ENGINES:
            sem = self.ctx.enter_context(self.nc.semaphore(f"prod_{e}_{self._epoch}"))
            self.esem[e] = sem
            self.epos[e] = 0
            self._semkey[id(sem)] = ("e", e, self._epoch)

    def buf(self, name="buf"):
        return Buf(name)

    def chan(self, name, eng):
        sem = self.ctx.enter_context(self.nc.semaphore(f"ch_{name}"))
        c = Chan(name, sem, eng)
        self._semkey[id(sem)] = ("c", name)
        self.chans.append(c)
        return c

    def _wait(self, eng, tok):
        if tok is None:
            return
        sem, val = tok
        key = self._semkey[id(sem)]
        w = self.wm[eng].get(key, 0)
        if w >= val:
            return
        self.wm[eng][key] = val

        def f(h, sem=sem, val=val):
            h.wait_ge(sem, val)

        self.streams[eng].append((self.scope, f))

    def op(self, eng, fn, reads=(), writes=(), dma=None, wtok_only=()):
        toks = []
        for b in reads:
            if b is not None and b.wtok is not None:
                toks.append(b.wtok)
        for b in writes:
            if b is None:
                continue
            if b.wtok is not None:
                toks.append(b.wtok)
            toks.extend(b.rtoks)
        seen = {}
        for tk in toks:
            k = self._semkey[id(tk[0])]
            if k not in seen or seen[k][1] < tk[1]:
                seen[k] = tk
        for tk in seen.values():
            self._wait(eng, tk)
        if dma is not None:
            sem = dma.sem
            dma.count += 16
            val = dma.count
            amt = 16
            still = []
            for b in dma.touched:
                hit = False
                if b.wtok is not None and b.wtok[0] is sem:
                    b.wtok = (sem, val)
                    hit = True
                nr = []
                for tk in b.rtoks:
                    if tk[0] is sem:
                        tk = (sem, val)
                        hit = True
                    nr.append(tk)
                b.rtoks = nr
                if hit:
                    still.append(b)
            dma.touched = still
        else:
            sem = self.esem[eng]
            self.epos[eng] += 1
            val = self.epos[eng]
            amt = 1

        def f(h, fn=fn, sem=sem, amt=amt):
            fn(h).then_inc(sem, amt)

        self.streams[eng].append((self.scope, f))
        tok = (sem, val)
        for b in writes:
            if b is not None:
                b.wtok = tok
                b.rtoks = []
                if dma is not None:
                    dma.touched.append(b)
        for b in wtok_only:
            if b is not None:
                b.wtok = tok
                if dma is not None:
                    dma.touched.append(b)
        for b in reads:
            if b is not None:
                b.rtoks.append(tok)
                if dma is not None:
                    dma.touched.append(b)
        return tok

    def drain_chans(self):
        for c in self.chans:
            if c.count > 0:
                def f(h, sem=c.sem, v=c.count):
                    h.wait_ge(sem, v)
                self.streams[c.eng].append((self.scope, f))

    def flush(self, block):
        nc = self.nc
        for name in ENGINES:
            stream = self.streams[name]
            if not stream:
                continue
            dec = getattr(block, name)

            def body(h, stream=tuple(stream)):
                i = 0
                n = len(stream)
                while i < n:
                    tag = stream[i][0]
                    j = i
                    with nc.named_scope(tag):
                        while j < n and stream[j][0] == tag:
                            stream[j][1](h)
                            j += 1
                    i = j

            dec(body)
        self.streams = {e: [] for e in ENGINES}


def sap(t_ap, offset, dims):
    return bass.AP(t_ap.tensor, offset, dims)


def build_program(S=4096):
    NT = S // 128
    QW = 512
    NJ = QW // 128
    NQ = S // QW
    CH = 2 if S >= 2048 else 1
    HT = S // CH
    HTT = HT // QW
    HNT = HT // 128
    MB = 7
    NB = 28 // MB

    nc = bass.Bass("TRN2", target_bir_lowering=False)
    dr = lambda name, shape, dt, kind: nc.dram_tensor(name, shape, dt, kind=kind).ap()
    X = dr("x", [S, 1024], F32, "ExternalInput")
    shapes = {"wq_w": [1024, 1024], "wq_b": [1024], "wk_w": [1024, 1024],
              "wk_b": [1024], "wv_w": [1024, 1024], "wv_b": [1024],
              "wo_w": [1024, 1024], "wo_b": [1024], "ln1_w": [1024],
              "ln2_w": [1024], "fwq": [1024, 1024], "fwk": [1024, 512],
              "fwv": [1024, 512], "fwo": [1024, 1024], "router_w": [1024, 8],
              "w_gate": [8, 1024, 3584], "w_up": [8, 1024, 3584],
              "w_down": [8, 3584, 1024]}
    W = {n: dr(n, s, F32, "ExternalInput") for n, s in shapes.items()}
    OUT = dr("out", [S, 1024], F32, "ExternalOutput")
    X1S = dr("x1_s", [S, 1024], F32, "Internal")
    RS = dr("r_s", [S, 1024], F32, "Internal")
    H2S = dr("h2_s", [S, 1024], BF16, "Internal")

    ctx = contextlib.ExitStack()
    fw = FW(nc, ctx)
    sbA = lambda n, shp, dt: (ctx.enter_context(nc.sbuf_tensor(n, shp, dt)), fw.buf(n))

    ch_cw = fw.chan("cw", "gpsimd")
    ch_x = [fw.chan(f"x{k}", "sync") for k in range(2)]
    ch_xq = [fw.chan(f"xq{k}", "sync") for k in range(2)]
    ch_wo = [fw.chan(f"wo{k}", "gpsimd") for k in range(2)]
    ch_sx1 = fw.chan("sx1", "gpsimd")
    ch_sr = fw.chan("sr", "gpsimd")
    ch_sh2 = fw.chan("sh2", "gpsimd")
    ch_acc = fw.chan("acc", "sync")
    ch_h2l = [fw.chan(f"h2l{k}", "sync") for k in range(2)]
    ch_out = fw.chan("out", "sync")
    ch_wg = [fw.chan(f"wg{k}", "gpsimd") for k in range(14)]
    ch_wu = [fw.chan(f"wu{k}", "gpsimd") for k in range(14)]
    ch_wdd = [fw.chan(f"wd{k}", "gpsimd") for k in range(14)]

    ident_bf, B_idbf = sbA("ident_bf", [128, 128], BF16)
    ident_f, B_idf = sbA("ident_f", [128, 128], F32)
    mask01, B_mask = sbA("mask01", [128, 128], BF16)
    rowidx, B_rowidx = sbA("rowidx", [128, 1], F32)
    ones_bf, B_onesbf = sbA("ones_bf", [128, 128], BF16)
    ones_f, B_onesf = sbA("ones_f", [128, 128], F32)
    invf, B_invf = sbA("invf", [1, 32], F32)
    epsc, B_eps = sbA("epsc", [128, 2], F32)
    rw_sb, B_rw = sbA("rw_sb", [128, 64], BF16)

    B_x1s = fw.buf("x1_s")
    B_rs = fw.buf("r_s")
    B_h2s = fw.buf("h2_s")

    v, s, t, g, sy = "vector", "scalar", "tensor", "gpsimd", "sync"

    scM = contextlib.ExitStack()
    sbM = lambda n, shp, dt: (scM.enter_context(nc.sbuf_tensor(n, shp, dt)), fw.buf(n))
    psM = lambda n, shp, dt: (scM.enter_context(nc.psum_tensor(n, shp, dt)), fw.buf(n))

    cos_tm, B_cos = sbM("cos_tm", [128, 32 * NT], F32)
    sin_tm, B_sin = sbM("sin_tm", [128, 32 * NT], F32)
    KT, B_KT = sbM("KT", [128, 8 * S], BF16)
    V_tm, B_V = sbM("V_tm", [128, 1040 * NT], BF16)
    brow_pk, B_brow = sbM("brow_pk", [128, 1024], BF16)
    brow_o, B_browo = sbM("brow_o", [1, 1024], BF16)
    _brows = {"wq_b": (brow_pk, 0), "wk_b": (brow_pk, 32), "wv_b": (brow_pk, 64),
              "wo_b": (brow_o, 0)}
    brow = {n: ((lambda tt=tt, r=r: tt[r:r + 1, :])(),
                B_brow if n != "wo_b" else B_browo, r)
            for n, (tt, r) in _brows.items()}
    ln1b, B_ln1b = sbM("ln1b", [128, 1024], BF16)
    ln2b, B_ln2b = sbM("ln2b", [128, 1024], BF16)

    xin = [sbM(f"xin{k}", [128, 1024], F32) for k in range(2)]
    lnx4 = [sbM(f"lnx{k}", [128, 1024], BF16) for k in range(4)]
    lnxT4 = [sbM(f"lnxT{k}", [128, 1024], BF16) for k in range(4)]
    rot2 = [sbM(f"rot{k}", [128, 1024], BF16) for k in range(2)]
    t12, B_t12 = sbM("t12", [128, 1024], F32)
    QT, B_QT = sbM("QT", [128, 8 * QW], BF16)
    A_sb = [sbM(f"Asb{k}", [128, 1024], BF16) for k in range(2)]
    ctxT, B_ctxT = sbM("ctxT", [128, 8 * QW], BF16)
    linvb, B_linvb = sbM("linvb", [64, 512], F32)
    wos = [sbM(f"wos{k}", [128, 1024], BF16) for k in range(2)]
    cols = [[sbM(f"col{p}_{k}", [128, 8], F32) for k in range(4)] for p in range(2)]
    nq2, B_nq2 = sbM("nq2", [128, 64], F32)
    gcol, B_gcol = sbM("gcol", [64, 1], F32)
    grow, B_grow = sbM("grow", [1, 64], F32)
    m16a, B_m16a = sbM("m16a", [1, 16], F32)
    m16b, B_m16b = sbM("m16b", [1, 16], F32)
    MkAr, B_MkAr = sbM("MkAr", [1, 16], F32)
    MkBr, B_MkBr = sbM("MkBr", [1, 8], F32)
    kp, B_kp = sbM("kp", [1, 8], F32)
    prow, B_prow = sbM("prow", [1, 8], F32)
    pairb, B_pairb = sbM("pairb", [128, 8], F32)

    PS = [psM(f"PS{k}", [128, 1024], F32) for k in range(4)]
    psb = [PS[k][0][:, :].bitcast(BF16) for k in range(4)]
    tslots = [(sap(psb[0], 0, [[2048, 128], [1, 128]]), PS[0][1]),
              (sap(psb[1], 0, [[2048, 128], [1, 128]]), PS[1][1]),
              (sap(psb[0], 1024, [[2048, 128], [1, 128]]), PS[0][1]),
              (sap(psb[1], 1024, [[2048, 128], [1, 128]]), PS[1][1])]

    V32 = V_tm[:, :].bitcast(F32)
    y1B2 = [(sap(V32, 8320 + 1024 * k, [[16640, 128], [1, 1024]]), fw.buf(f"y1B{k}"))
            for k in range(2)]
    x2b2 = [(sap(V32, 10368 + 1024 * k, [[16640, 128], [1, 1024]]), fw.buf(f"x2b{k}"))
            for k in range(2)]
    h2b2 = [(sap(V_tm[:, :], 24832 + 1024 * k, [[33280, 128], [1, 1024]]), fw.buf(f"h2b{k}"))
            for k in range(2)]

    def front(ph, gg, jj):
        i = 4 * gg + jj
        p = gg % 2
        col, B_col = cols[p][jj]
        xb, B_xb = xin[jj % 2]
        lx, B_lx = lnx4[jj]
        src = X if ph == "A" else X1S
        B_src = None if ph == "A" else B_x1s
        fw.op(sy, lambda h: h.dma_start(xb[:, :], src[128 * i:128 * i + 128, :]),
              reads=[B_src] if B_src else [], writes=[B_xb], dma=ch_x[jj % 2])
        fw.op(v, lambda h: h.tensor_reduce(col[:, 0:1], xb[:, :], AX.X, ALU.add),
              reads=[B_xb], writes=[B_col])
        fw.op(v, lambda h: h.tensor_scalar(col[:, 1:2], col[:, 0:1], 1.0 / 1024, None, ALU.mult),
              reads=[B_col], writes=[B_col])
        fw.op(s, lambda h: h.activation(lx[:, :], xb[:, :], AF.Square, accum_out=col[:, 2:3]),
              reads=[B_xb], writes=[B_lx, B_col])
        fw.op(v, lambda h: h.tensor_tensor(col[:, 3:4], col[:, 1:2], col[:, 1:2], ALU.mult),
              reads=[B_col], writes=[B_col])
        fw.op(v, lambda h: h.scalar_tensor_tensor(col[:, 3:4], col[:, 2:3], 1.0 / 1024,
                                                  col[:, 3:4], ALU.mult, ALU.subtract),
              reads=[B_col], writes=[B_col])
        fw.op(s, lambda h: h.activation(col[:, 4:5], col[:, 3:4], AF.Sqrt, bias=epsc[:, 0:1]),
              reads=[B_col, B_eps], writes=[B_col])
        fw.op(v, lambda h: h.reciprocal(col[:, 4:5], col[:, 4:5]),
              reads=[B_col], writes=[B_col])
        if ph == "A":
            fw.op(v, lambda h: h.tensor_scalar(lx[:, :], xb[:, :], col[:, 1:2],
                                               col[:, 4:5], ALU.subtract, op1=ALU.mult),
                  reads=[B_xb, B_col], writes=[B_lx])
        else:
            fw.op(v, lambda h: h.tensor_tensor(col[:, 6:7], col[:, 4:5], col[:, 4:5], ALU.mult),
                  reads=[B_col], writes=[B_col])
            fw.op(v, lambda h: h.tensor_tensor(col[:, 6:7], col[:, 3:4], col[:, 6:7], ALU.mult),
                  reads=[B_col], writes=[B_col])
            fw.op(s, lambda h: h.activation(col[:, 6:7], col[:, 6:7], AF.Sqrt, bias=epsc[:, 1:2]),
                  reads=[B_col, B_eps], writes=[B_col])
            fw.op(v, lambda h: h.reciprocal(col[:, 6:7], col[:, 6:7]),
                  reads=[B_col], writes=[B_col])
            fw.op(v, lambda h: h.tensor_tensor(col[:, 7:8], col[:, 4:5], col[:, 6:7], ALU.mult),
                  reads=[B_col], writes=[B_col])
            fw.op(v, lambda h: h.tensor_scalar(t12[:, :], xb[:, :], col[:, 1:2],
                                               col[:, 7:8], ALU.subtract, op1=ALU.mult),
                  reads=[B_xb, B_col], writes=[B_t12])
            fw.op(v, lambda h: h.tensor_tensor(lx[:, :], t12[:, :], ln1b[:, :], ALU.mult),
                  reads=[B_ln1b, B_t12], writes=[B_lx])

    def transposes(src, B_src, nch, emit_out, base=0, slots=None):
        slots = slots or tslots
        for c in range(nch):
            pv, Bp = slots[c % len(slots)]
            fw.op(t, lambda h, c=c, pv=pv: h.transpose(
                pv, src[:, base + 128 * c:base + 128 * c + 128], ident_bf[:, :]),
                reads=[B_src, B_idbf], writes=[Bp])
            emit_out(c, pv, Bp)

    def rope(src_ap, B_srcs, dst, B_dst, i, nh):
        co = 32 * i
        cosap = sap(cos_tm[:, :], co, [[32 * NT, 128], [0, nh], [1, 32]])
        sinap = sap(sin_tm[:, :], co, [[32 * NT, 128], [0, nh], [1, 32]])
        flen = src_ap.ap[0][0]
        off0 = src_ap.offset
        srcf = sap(src_ap, off0 + 0, [[flen, 128], [64, nh], [1, 32]])
        srcs = sap(src_ap, off0 + 32, [[flen, 128], [64, nh], [1, 32]])
        dlen = dst[:, :].ap[0][0]
        dstf = sap(dst[:, :], 0, [[dlen, 128], [64, nh], [1, 32]])
        dsts = sap(dst[:, :], 32, [[dlen, 128], [64, nh], [1, 32]])
        t1v = sap(t12[:, :], 0, [[1024, 128], [32, nh], [1, 32]])
        t2v = sap(t12[:, :], 512, [[1024, 128], [32, nh], [1, 32]])
        fw.op(v, lambda h: h.tensor_tensor(t1v, srcf, cosap, ALU.mult),
              reads=list(B_srcs) + [B_cos], writes=[B_t12])
        fw.op(v, lambda h: h.tensor_tensor(t2v, srcs, sinap, ALU.mult),
              reads=list(B_srcs) + [B_sin], wtok_only=[B_t12])
        fw.op(v, lambda h: h.tensor_tensor(dstf, t1v, t2v, ALU.subtract),
              reads=[B_t12], writes=[B_dst])
        fw.op(v, lambda h: h.tensor_tensor(t1v, srcs, cosap, ALU.mult),
              reads=list(B_srcs) + [B_cos], writes=[B_t12])
        fw.op(v, lambda h: h.tensor_tensor(t2v, srcf, sinap, ALU.mult),
              reads=list(B_srcs) + [B_cos], wtok_only=[B_t12])
        fw.op(v, lambda h: h.tensor_tensor(dsts, t1v, t2v, ALU.add),
              reads=[B_t12], writes=[B_dst])

    def head_sq(nh, jj, rsrc, B_rsrc, sqf, B_sqf):
        fw.op(s, lambda h, rsrc=rsrc, sqf=sqf: h.activation(
            sqf[:, 0:64 * nh], rsrc[:, 0:64 * nh], AF.Square),
            reads=[B_rsrc], writes=[B_sqf])
        fw.op(v, lambda h, jj=jj, nh=nh, sqf=sqf: h.tensor_reduce(
            nq2[:, nh * jj:nh * jj + nh],
            sap(sqf[:, :], 0, [[sqf[:, :].ap[0][0], 128], [64, nh], [1, 64]]),
            AX.X, ALU.add), reads=[B_sqf], writes=[B_nq2])

    def group_norms(nh, Mkrow, B_Mkrow, running):
        w = 4 * nh
        PX, B_PX = PS[3]
        fw.op(t, lambda h, w=w: h.transpose(PX[0:w, 0:128], nq2[:, 0:w], ident_f[:, :]),
              reads=[B_nq2, B_idf], writes=[B_PX])
        fw.op(v, lambda h, w=w: h.reduce_max(gcol[0:w, 0:1], PX[0:w, 0:128], AX.X),
              reads=[B_PX], writes=[B_gcol])
        fw.op(t, lambda h, w=w: h.transpose(PX[0:1, 128:128 + w], gcol[0:w, 0:1],
                                            ident_f[0:w, 0:w]),
              reads=[B_gcol, B_idf], wtok_only=[B_PX])
        fw.op(v, lambda h, w=w: h.tensor_copy(grow[0:1, 0:w], PX[0:1, 128:128 + w]),
              reads=[B_PX], writes=[B_grow])
        fw.op(v, lambda h, nh=nh: h.tensor_max(m16a[0:1, 0:nh], grow[0:1, 0:nh],
                                               grow[0:1, nh:2 * nh]),
              reads=[B_grow], writes=[B_m16a])
        fw.op(v, lambda h, nh=nh: h.tensor_max(m16b[0:1, 0:nh], grow[0:1, 2 * nh:3 * nh],
                                               grow[0:1, 3 * nh:4 * nh]),
              reads=[B_grow], writes=[B_m16b])
        fw.op(v, lambda h, nh=nh: h.tensor_max(m16a[0:1, 0:nh], m16a[0:1, 0:nh],
                                               m16b[0:1, 0:nh]),
              reads=[B_m16b], writes=[B_m16a])
        if running:
            fw.op(v, lambda h, nh=nh: h.tensor_max(Mkrow[0:1, 0:nh], Mkrow[0:1, 0:nh],
                                                   m16a[0:1, 0:nh]),
                  reads=[B_m16a], writes=[B_Mkrow])

    fw.scope = "pro"
    _iotaF, B_iotaF = sbM("p_iotaF", [128, 128], F32)
    _wrk, B_wrk = sbM("p_wrk", [128, 32], F32)
    _angw, B_ang = sbM("p_ang", [128, 32], F32)
    _wri, B_wri = sbM("p_wri", [128, 32], I32)
    _invfF, B_invfF = sbM("p_invf", [128, 32], F32)
    _rowcol, B_rowcol = sbM("p_rowcol", [128, 1], F32)
    _ang0, B_ang0 = sbM("p_ang0", [128, 32], F32)
    iotaF = _iotaF[:, :]
    wrk = _wrk[:, :]
    angw = _angw[:, :]
    wri = _wri[:, :]
    invf_full = _invfF[:, :]
    rowcol = _rowcol[:, :]
    ang0 = _ang0[:, :]
    import os
    if os.environ.get("SBUF_DEBUG"):
        print("SBUF remaining at peak:", nc.sbuf_bytes_remaining)

    fw.op(g, lambda h: h.memset(epsc[:, 0:1], 1e-5), writes=[B_eps])
    fw.op(g, lambda h: h.memset(epsc[:, 1:2], 1e-6), writes=[B_eps])
    fw.op(g, lambda h: h.iota(rowidx[:, :], [[0, 1]], channel_multiplier=1,
                              allow_small_or_imprecise_dtypes=True), writes=[B_rowidx])
    fw.op(g, lambda h: h.iota(iotaF, [[1, 128]], channel_multiplier=0,
                              allow_small_or_imprecise_dtypes=True), writes=[B_iotaF])
    fw.op(g, lambda h: h.iota(invf_full, [[1, 32]], channel_multiplier=0,
                              allow_small_or_imprecise_dtypes=True), writes=[B_invfF])
    fw.op(g, lambda h: h.memset(ones_bf[:, :], 1.0), writes=[B_onesbf])
    fw.op(g, lambda h: h.memset(ones_f[:, :], 1.0), writes=[B_onesf])
    fw.op(g, lambda h: h.memset(MkAr[:, :], 0.0), writes=[B_MkAr])
    fw.op(g, lambda h: h.memset(MkBr[:, :], 0.0), writes=[B_MkBr])
    fw.op(v, lambda h: h.tensor_scalar(ident_bf[:, :], iotaF, rowidx[:, :], None, ALU.is_equal),
          reads=[B_iotaF, B_rowidx], writes=[B_idbf])
    fw.op(v, lambda h: h.tensor_scalar(ident_f[:, :], iotaF, rowidx[:, :], None, ALU.is_equal),
          reads=[B_iotaF, B_rowidx], writes=[B_idf])
    fw.op(v, lambda h: h.tensor_scalar(mask01[:, :], iotaF, rowidx[:, :], None, ALU.is_ge),
          reads=[B_iotaF, B_rowidx], writes=[B_mask])
    fw.op(s, lambda h: h.activation(invf_full, invf_full, AF.Exp, scale=-LN1E4 / 32.0),
          reads=[B_invfF], writes=[B_invfF])
    for n in ("wq_b", "wk_b", "wv_b", "wo_b"):
        fw.op(g, lambda h, n=n: h.dma_start(brow[n][0],
                                            W[n].rearrange("(o n) -> o n", o=1)),
              writes=[brow[n][1]], dma=ch_cw)
    fw.op(g, lambda h: h.dma_start(ln1b[:, :], sap(W["ln1_w"], 0, [[0, 128], [1, 1024]])),
          writes=[B_ln1b], dma=ch_cw)
    fw.op(g, lambda h: h.dma_start(ln2b[:, :], sap(W["ln2_w"], 0, [[0, 128], [1, 1024]])),
          writes=[B_ln2b], dma=ch_cw)
    fw.op(g, lambda h: h.dma_start(rw_sb[:, :].rearrange("p (c n) -> p c n", n=8),
                                   W["router_w"].rearrange("(c p) n -> p c n", p=128)),
          writes=[B_rw], dma=ch_cw)
    for bi in range(NT):
        fw.op(v, lambda h, bi=bi: h.tensor_scalar(rowcol, rowidx[:, :], 128.0 * bi,
                                                  None, ALU.add),
              reads=[B_rowidx], writes=[B_rowcol])
        fw.op(v, lambda h: h.tensor_scalar(ang0, invf_full, rowcol, None, ALU.mult),
              reads=[B_rowcol, B_invfF], writes=[B_ang0])
        fw.op(v, lambda h: h.tensor_scalar(wrk, ang0, 1.0 / (2 * PI), None, ALU.mult),
              reads=[B_ang0], writes=[B_wrk])
        fw.op(v, lambda h: h.tensor_copy(wri, wrk), reads=[B_wrk], writes=[B_wri])
        fw.op(v, lambda h: h.tensor_copy(wrk, wri), reads=[B_wri], writes=[B_wrk])
        fw.op(v, lambda h: h.scalar_tensor_tensor(angw, wrk, -2 * PI, ang0,
                                                  ALU.mult, ALU.add),
              reads=[B_wrk, B_ang0], writes=[B_ang])
        fw.op(v, lambda h: h.tensor_scalar(wrk, angw, PI, None, ALU.is_gt),
              reads=[B_ang], writes=[B_wrk])
        fw.op(v, lambda h: h.scalar_tensor_tensor(angw, wrk, -2 * PI, angw,
                                                  ALU.mult, ALU.add),
              reads=[B_wrk, B_ang], writes=[B_ang])
        fw.op(v, lambda h: h.tensor_scalar(wrk, angw, -PI, None, ALU.is_lt),
              reads=[B_ang], writes=[B_wrk])
        fw.op(v, lambda h: h.scalar_tensor_tensor(angw, wrk, 2 * PI, angw,
                                                  ALU.mult, ALU.add),
              reads=[B_wrk, B_ang], writes=[B_ang])
        fw.op(s, lambda h, bi=bi: h.activation(sin_tm[:, 32 * bi:32 * bi + 32],
                                               angw, AF.Sin),
              reads=[B_ang], writes=[B_sin])
        fw.op(v, lambda h: h.tensor_scalar(wrk, angw, PI / 2, None, ALU.add),
              reads=[B_ang], writes=[B_wrk])
        fw.op(v, lambda h: h.tensor_scalar(angw, wrk, PI, None, ALU.is_gt),
              reads=[B_wrk], writes=[B_ang])
        fw.op(v, lambda h: h.scalar_tensor_tensor(wrk, angw, -2 * PI, wrk,
                                                  ALU.mult, ALU.add),
              reads=[B_ang, B_wrk], writes=[B_wrk])
        fw.op(s, lambda h, bi=bi: h.activation(cos_tm[:, 32 * bi:32 * bi + 32],
                                               wrk, AF.Sin),
              reads=[B_wrk], writes=[B_cos])

    def prep_tm(ph, gg):
        isA = ph == "A"
        fw.scope = "p" + ph
        for jj in range(NJ):
            i = 4 * gg + jj
            lx, B_lx = lnx4[jj]
            lT, B_lT = lnxT4[jj]
            transposes(lx, B_lx, 8,
                       lambda c, pv, Bp, lT=lT, B_lT=B_lT: fw.op(
                           v, lambda h, c=c, pv=pv: h.tensor_copy(
                               lT[:, 128 * c:128 * c + 128], pv),
                           reads=[Bp], writes=[B_lT]))
        if isA:
            passes = (("wk_w", "wk_b", True), ("wv_w", "wv_b", False))
        else:
            passes = (("fwk", None, True),)
        for (wname, bname, is_k) in passes:
            for c in range(8):
                wsl, B_wsl = wos[c % 2]
                if isA:
                    fw.op(g, lambda h, c=c, wsl=wsl, wname=wname: h.dma_start(
                        wsl[:, :], W[wname][128 * c:128 * c + 128, :]),
                        writes=[B_wsl], dma=ch_wo[c % 2])
                else:
                    fw.op(g, lambda h, c=c, wsl=wsl: h.dma_start(
                        wsl[:, 0:512], W["fwk"][128 * c:128 * c + 128, :]),
                        writes=[B_wsl], dma=ch_wo[c % 2])
                    fw.op(g, lambda h, c=c, wsl=wsl: h.dma_start(
                        wsl[:, 512:1024], W["fwv"][128 * c:128 * c + 128, :]),
                        wtok_only=[B_wsl], dma=ch_wo[c % 2])
                for jj in range(NJ):
                    PP, B_PP = PS[jj]
                    lT, B_lT = lnxT4[jj]
                    def kmm(h, c=c, jj=jj, wsl=wsl, PP=PP, lT=lT, bname=bname, isA=isA):
                        for dh in range(2):
                            first = (c == 0)
                            if first and isA:
                                r = brow[bname][2]
                                h.matmul(PP[:, 512 * dh:512 * dh + 512],
                                         ones_bf[r:r + 1, :],
                                         brow[bname][0][0:1, 512 * dh:512 * dh + 512],
                                         start=True, stop=False)
                                first = False
                            ii = h.matmul(PP[:, 512 * dh:512 * dh + 512],
                                          lT[:, 128 * c:128 * c + 128],
                                          wsl[:, 512 * dh:512 * dh + 512],
                                          start=first, stop=(c == 7))
                        return ii
                    rds = [B_lT, B_wsl] + ([brow[bname][1], B_onesbf] if (c == 0 and isA) else [])
                    if c == 0:
                        fw.op(t, kmm, reads=rds, writes=[B_PP])
                    else:
                        fw.op(t, kmm, reads=rds, wtok_only=[B_PP])
            for jj in range(NJ):
                i = 4 * gg + jj
                PP, B_PP = PS[jj]
                lx, B_lx = lnx4[jj]
                if is_k:
                    nh = 16 if isA else 8
                    rot, B_rot = rot2[jj % 2]
                    rope(PP[:, :], [B_PP], rot, B_rot, i, nh)
                    head_sq(nh, jj, rot, B_rot, lx, B_lx)
                    own = [(sap(psb[jj], 0, [[2048, 128], [1, 128]]), B_PP),
                           (sap(psb[jj], 1024, [[2048, 128], [1, 128]]), B_PP)]
                    if isA:
                        transposes(rot, B_rot, 8,
                                   lambda c, pv, Bp, i=i: fw.op(
                                       s, lambda h, c=c, pv=pv, i=i: h.activation(
                                           KT[:, S * c + 128 * i:S * c + 128 * i + 128], pv,
                                           AF.Copy),
                                       reads=[Bp], writes=[B_KT]),
                                   slots=own)
                    else:
                        fw.op(v, lambda h, jj=jj, i=i, PP=PP: h.tensor_copy(
                            sap(V_tm[:, :], 520 * i, [[1040 * NT, 128], [65, 8], [1, 64]]),
                            sap(PP[:, :], 512, [[1024, 128], [64, 8], [1, 64]])),
                            reads=[B_PP], writes=[B_V])
                        fw.op(v, lambda h, i=i: h.memset(
                            sap(V_tm[:, :], 520 * i + 64, [[1040 * NT, 128], [65, 8], [1, 1]]), 1.0),
                            writes=[B_V])
                        for cc in range(4):
                            pv, Bp = own[cc % 2]
                            fw.op(t, lambda h, cc=cc, pv=pv, rot=rot: h.transpose(
                                pv, rot[:, 128 * cc:128 * cc + 128], ident_bf[:, :]),
                                reads=[B_rot, B_idbf], writes=[Bp])
                            for gg2 in range(2):
                                kvh = 2 * cc + gg2
                                sub = pv[64 * gg2:64 * gg2 + 64, 0:128]
                                for half in range(2):
                                    fw.op(s, lambda h, kvh=kvh, half=half, sub=sub, i=i:
                                          h.activation(
                                              KT[64 * half:64 * half + 64,
                                                 S * kvh + 128 * i:S * kvh + 128 * i + 128],
                                              sub, AF.Copy),
                                          reads=[Bp], writes=[B_KT])
                else:
                    for dh in range(2):
                        fw.op(v, lambda h, dh=dh, i=i, PP=PP: h.tensor_copy(
                            sap(V_tm[:, :], 1040 * i + 65 * 8 * dh, [[1040 * NT, 128], [65, 8], [1, 64]]),
                            sap(PP[:, :], 512 * dh, [[1024, 128], [64, 8], [1, 64]])),
                            reads=[B_PP], writes=[B_V])
                    fw.op(v, lambda h, i=i: h.memset(
                        sap(V_tm[:, :], 1040 * i + 64, [[1040 * NT, 128], [65, 16], [1, 1]]), 1.0),
                        writes=[B_V])
            if is_k:
                group_norms(16 if isA else 8, MkAr if isA else MkBr,
                            B_MkAr if isA else B_MkBr, running=True)

    def qstage(ph, tq):
        isA = ph == "A"
        fw.scope = "q" + ph
        Mkrow, B_Mkrow = (MkAr, B_MkAr) if isA else (MkBr, B_MkBr)
        wqn = "wq_w" if isA else "fwq"
        for c in range(8):
            wsl, B_wsl = wos[c % 2]
            fw.op(g, lambda h, c=c, wsl=wsl, wqn=wqn: h.dma_start(
                wsl[:, :], W[wqn][128 * c:128 * c + 128, :]),
                writes=[B_wsl], dma=ch_wo[c % 2])
            for jj in range(NJ):
                PQ_, B_PQ_ = PS[jj]
                lT, B_lT = lnxT4[jj]
                def qmm(h, c=c, jj=jj, wsl=wsl, PQ_=PQ_, lT=lT):
                    for dh in range(2):
                        first = (c == 0)
                        if first and isA:
                            h.matmul(PQ_[:, 512 * dh:512 * dh + 512], ones_bf[0:1, :],
                                     brow["wq_b"][0][0:1, 512 * dh:512 * dh + 512],
                                     start=True, stop=False)
                            first = False
                        ii = h.matmul(PQ_[:, 512 * dh:512 * dh + 512],
                                      lT[:, 128 * c:128 * c + 128],
                                      wsl[:, 512 * dh:512 * dh + 512],
                                      start=first, stop=(c == 7))
                    return ii
                rds = [B_lT, B_wsl] + ([brow["wq_b"][1], B_onesbf]
                                       if (c == 0 and isA) else [])
                if c == 0:
                    fw.op(t, qmm, reads=rds, writes=[B_PQ_])
                else:
                    fw.op(t, qmm, reads=rds, wtok_only=[B_PQ_])
        for jj in range(NJ):
            j = NJ * tq + jj
            PQ_, B_PQ_ = PS[jj]
            lx, B_lx = lnx4[jj]
            rot, B_rot = rot2[jj % 2]
            rope(PQ_[:, :], [B_PQ_], rot, B_rot, j, 16)
            head_sq(16, jj, rot, B_rot, lx, B_lx)
            pqb = PQ_[:, :].bitcast(BF16)
            tsl_q = [(sap(pqb, 0, [[2048, 128], [1, 128]]), B_PQ_),
                     (sap(pqb, 1024, [[2048, 128], [1, 128]]), B_PQ_)]
            for c in range(8):
                pv, Bp = tsl_q[c % 2]
                fw.op(t, lambda h, c=c, pv=pv, rot=rot: h.transpose(
                    pv, rot[:, 128 * c:128 * c + 128], ident_bf[:, :]),
                    reads=[B_rot, B_idbf], writes=[Bp])
                fw.op(s, lambda h, c=c, pv=pv, jj=jj: h.activation(
                    QT[:, QW * c + 128 * jj:QW * c + 128 * jj + 128], pv, AF.Copy),
                    reads=[Bp], writes=[B_QT])
        group_norms(16, None, None, running=False)
        if isA:
            fw.op(v, lambda h: h.tensor_tensor(kp[:, :],
                                               sap(Mkrow[:, :], 0, [[16, 1], [2, 8]]),
                                               sap(Mkrow[:, :], 1, [[16, 1], [2, 8]]),
                                               ALU.max),
                  reads=[B_Mkrow], writes=[B_kp])
        else:
            fw.op(v, lambda h: h.tensor_copy(kp[:, :], Mkrow[0:1, 0:8]),
                  reads=[B_Mkrow], writes=[B_kp])
        fw.op(v, lambda h: h.tensor_tensor(prow[:, :],
                                           sap(m16a[:, :], 0, [[16, 1], [2, 8]]),
                                           sap(m16a[:, :], 1, [[16, 1], [2, 8]]),
                                           ALU.max),
              reads=[B_m16a], writes=[B_prow])
        fw.op(v, lambda h: h.tensor_tensor(prow[:, :], prow[:, :], kp[:, :], ALU.mult),
              reads=[B_kp, B_prow], writes=[B_prow])
        fw.op(s, lambda h: h.activation(prow[:, :], prow[:, :], AF.Sqrt),
              reads=[B_prow], writes=[B_prow])
        fw.op(v, lambda h: h.tensor_scalar(prow[:, :], prow[:, :], -0.125, None, ALU.mult),
              reads=[B_prow], writes=[B_prow])
        fw.op(t, lambda h: h.matmul(PS[2][0][:, 512:520], ones_f[0:1, :], prow[:, :],
                                    start=True, stop=True),
              reads=[B_onesf, B_prow], writes=[PS[2][1]])
        fw.op(v, lambda h: h.tensor_copy(pairb[:, :], PS[2][0][:, 512:520]),
              reads=[PS[2][1]], writes=[B_pairb])

    def pairs(ph, tq):
        isA = ph == "A"
        fw.scope = "s" + ph
        VST = 1040 if isA else 520
        nkb = NJ * (tq + 1)

        def smm_emit(c, kb):
            PSc, B_PSc = PS[kb % 2]
            qoff = 128 * (kb - NJ * tq) if kb >= NJ * tq else 0
            def smm(h, c=c, kb=kb, qoff=qoff, PSc=PSc):
                h.matmul(PSc[:, qoff:512],
                         KT[0:64, S * c + 128 * kb:S * c + 128 * kb + 128],
                         QT[0:64, QW * c + qoff:QW * c + QW],
                         start=True, stop=True, tile_position=(0, 0))
                return h.matmul(PSc[:, 512 + qoff:1024],
                                KT[64:128, S * c + 128 * kb:S * c + 128 * kb + 128],
                                QT[64:128, QW * c + qoff:QW * c + QW],
                                start=True, stop=True, tile_position=(64, 0))
            fw.op(t, smm, reads=[B_KT, B_QT], writes=[B_PSc])

        def finalize(c):
            PC, B_PC = PS[2 + (c % 2)]
            rrow = (linvb[0:1, :], t12[0:1, 0:512])
            rbuf = (B_linvb, B_t12)
            for half in range(2):
                fw.op(v, lambda h, half=half, PC=PC: h.reciprocal(
                    rrow[half], PC[64:65, 512 * half:512 * half + 512]),
                    reads=[B_PC], writes=[rbuf[half]])
            for half in range(2):
                fw.op(t, lambda h, half=half, PC=PC: h.matmul(
                    PC[64:128, 512 * half:512 * half + 512],
                    ones_f[0:1, 0:64], rrow[half], start=True, stop=True,
                    tile_position=(0, 64)),
                    reads=[B_onesf, rbuf[half]], wtok_only=[B_PC])
            for half in range(2):
                fw.op(v, lambda h, half=half, PC=PC: h.tensor_copy(
                    linvb[:, :], PC[64:128, 512 * half:512 * half + 512]),
                    reads=[B_PC], writes=[B_linvb])
                fw.op(v, lambda h, c=c, half=half, PC=PC: h.tensor_tensor(
                    ctxT[64 * half:64 * half + 64, QW * c:QW * c + QW],
                    PC[0:64, 512 * half:512 * half + 512],
                    linvb[0:64, 0:512], ALU.mult),
                    reads=[B_PC, B_linvb], writes=[B_ctxT])

        smm_emit(0, 0)
        if nkb > 1:
            smm_emit(0, 1)
        for c in range(8):
            PC, B_PC = PS[2 + (c % 2)]
            for kb in range(nkb):
                PSc, B_PSc = PS[kb % 2]
                qoff = 128 * (kb - NJ * tq) if kb >= NJ * tq else 0
                Asb, B_Asb = A_sb[kb % 2]
                if qoff == 0:
                    fw.op(s, lambda h, c=c, PSc=PSc, Asb=Asb: h.activation(
                        Asb[:, :], PSc[:, :], AF.Exp,
                        bias=pairb[:, c:c + 1], scale=0.125),
                        reads=[B_PSc, B_pairb], writes=[B_Asb])
                else:
                    fw.op(s, lambda h, PSc=PSc, Asb=Asb, c=c, qoff=qoff: h.activation(
                        Asb[:, qoff:512], PSc[:, qoff:512], AF.Exp,
                        bias=pairb[:, c:c + 1], scale=0.125),
                        reads=[B_PSc, B_pairb], writes=[B_Asb])
                    fw.op(s, lambda h, PSc=PSc, Asb=Asb, c=c, qoff=qoff: h.activation(
                        Asb[:, 512 + qoff:1024], PSc[:, 512 + qoff:1024], AF.Exp,
                        bias=pairb[:, c:c + 1], scale=0.125),
                        reads=[B_PSc, B_pairb], writes=[B_Asb])
                if kb >= NJ * tq:
                    for half in range(2):
                        o = 512 * half + qoff
                        fw.op(v, lambda h, o=o, Asb=Asb: h.tensor_tensor(
                            Asb[:, o:o + 128], Asb[:, o:o + 128], mask01[:, :], ALU.mult),
                            reads=[B_mask, B_Asb], writes=[B_Asb])
                hA = 65 * (2 * c) if isA else 65 * c
                hB = 65 * (2 * c + 1) if isA else 65 * c
                def avmm(h, kb=kb, qoff=qoff, Asb=Asb, hA=hA, hB=hB, PC=PC, VST=VST,
                         last=(kb == nkb - 1)):
                    h.matmul(PC[0:65, qoff:512],
                             sap(V_tm[:, :], VST * kb + hA, [[1040 * NT, 128], [1, 65]]),
                             Asb[:, qoff:512], start=(kb == 0), stop=last)
                    return h.matmul(PC[0:65, 512 + qoff:1024],
                                    sap(V_tm[:, :], VST * kb + hB, [[1040 * NT, 128], [1, 65]]),
                                    Asb[:, 512 + qoff:1024], start=(kb == 0), stop=last)
                if kb == 0:
                    fw.op(t, avmm, reads=[B_V, B_Asb], writes=[B_PC])
                else:
                    fw.op(t, avmm, reads=[B_V, B_Asb], wtok_only=[B_PC])
                if kb + 2 < nkb:
                    smm_emit(c, kb + 2)
                if kb == 0 and c >= 1:
                    finalize(c - 1)
            if c < 7:
                smm_emit(c + 1, 0)
                if nkb > 1:
                    smm_emit(c + 1, 1)
        finalize(7)

    def wo_evac(ph, tq):
        isA = ph == "A"
        fw.scope = "o" + ph
        won = "wo_w" if isA else "fwo"
        for c in range(8):
            wsl, B_wsl = wos[c % 2]
            fw.op(g, lambda h, c=c, wsl=wsl, won=won: h.dma_start(
                wsl[:, :], W[won][128 * c:128 * c + 128, :]),
                writes=[B_wsl], dma=ch_wo[c % 2])
            for jj in range(NJ):
                PT_, B_PT_ = PS[jj]
                for dh in range(2):
                    def omm(h, c=c, jj=jj, dh=dh, wsl=wsl, PT_=PT_):
                        first = (c == 0)
                        if first and isA:
                            h.matmul(PT_[:, 512 * dh:512 * dh + 512], ones_bf[0:1, :],
                                     brow["wo_b"][0][0:1, 512 * dh:512 * dh + 512],
                                     start=True, stop=False)
                            first = False
                        return h.matmul(PT_[:, 512 * dh:512 * dh + 512],
                                        ctxT[:, QW * c + 128 * jj:QW * c + 128 * jj + 128],
                                        wsl[:, 512 * dh:512 * dh + 512],
                                        start=first, stop=(c == 7))
                    rds = [B_ctxT, B_wsl] + ([brow["wo_b"][1], B_onesbf]
                                             if (c == 0 and isA) else [])
                    if c == 0 and dh == 0:
                        fw.op(t, omm, reads=rds, writes=[B_PT_])
                    else:
                        fw.op(t, omm, reads=rds, wtok_only=[B_PT_])
        p = tq % 2
        for jj in range(NJ):
            j = NJ * tq + jj
            sl = jj % 2
            PT_, B_PT_ = PS[jj]
            xb, B_xb = t12, B_t12
            col, B_col = cols[p][jj]
            src = X if isA else X1S
            B_src = None if isA else B_x1s
            fw.op(sy, lambda h, j=j, xb=xb, src=src: h.dma_start(
                xb[:, :], src[128 * j:128 * j + 128, :]),
                reads=[B_src] if B_src else [],
                writes=[B_xb], dma=ch_xq[sl])
            if isA:
                for dh in range(2):
                    fw.op(v, lambda h, dh=dh, PT_=PT_, xb=xb: h.tensor_tensor(
                        xb[:, 512 * dh:512 * dh + 512],
                        xb[:, 512 * dh:512 * dh + 512],
                        PT_[:, 512 * dh:512 * dh + 512], ALU.add),
                        reads=[B_PT_], writes=[B_xb])
                fw.op(g, lambda h, j=j, xb=xb: h.dma_start(
                    X1S[128 * j:128 * j + 128, :], xb[:, :]),
                    reads=[B_xb], writes=[B_x1s], dma=ch_sx1)
            else:
                y1B, B_y1 = y1B2[jj % 2]
                x2b, B_x2 = x2b2[jj % 2]
                h2b, B_h2 = h2b2[jj % 2]
                fw.op(v, lambda h, xb=xb, col=col, x2b=x2b: h.tensor_scalar(
                    x2b, xb[:, :], col[:, 1:2], col[:, 4:5], ALU.subtract, op1=ALU.mult),
                    reads=[B_xb, B_col], writes=[B_x2])
                for dh in range(2):
                    fw.op(v, lambda h, dh=dh, PT_=PT_, x2b=x2b: h.tensor_tensor(
                        x2b[:, 512 * dh:512 * dh + 512],
                        x2b[:, 512 * dh:512 * dh + 512],
                        PT_[:, 512 * dh:512 * dh + 512], ALU.add),
                        reads=[B_PT_], writes=[B_x2])
                fw.op(v, lambda h, xb=xb, y1B=y1B, x2b=x2b: h.tensor_tensor(
                    y1B, xb[:, :], x2b, ALU.add),
                    reads=[B_xb, B_x2], writes=[B_y1])
                fw.op(g, lambda h, j=j, y1B=y1B: h.dma_start(
                    RS[128 * j:128 * j + 128, :], y1B),
                    reads=[B_y1], writes=[B_rs], dma=ch_sr)
                fw.op(s, lambda h, col=col, h2b=h2b, x2b=x2b: h.activation(
                    h2b, x2b, AF.Square, accum_out=col[:, 6:7]),
                    reads=[B_x2], writes=[B_h2, B_col])
                fw.op(s, lambda h, col=col: h.activation(col[:, 6:7], col[:, 6:7], AF.Sqrt,
                                                         bias=epsc[:, 1:2], scale=1.0 / 1024),
                      reads=[B_col, B_eps], writes=[B_col])
                fw.op(v, lambda h, col=col: h.reciprocal(col[:, 6:7], col[:, 6:7]),
                      reads=[B_col], writes=[B_col])
                fw.op(v, lambda h, col=col, xb=xb, x2b=x2b: h.tensor_scalar(
                    xb[:, :], x2b, col[:, 6:7], None, ALU.mult),
                    reads=[B_x2, B_col], writes=[B_xb])
                fw.op(v, lambda h, xb=xb, h2b=h2b: h.tensor_tensor(
                    h2b, xb[:, :], ln2b[:, :], ALU.mult),
                    reads=[B_ln2b, B_xb], writes=[B_h2])
                fw.op(g, lambda h, j=j, h2b=h2b: h.dma_start(
                    H2S[128 * j:128 * j + 128, :], h2b),
                    reads=[B_h2], writes=[B_h2s], dma=ch_sh2)

    for ph in ("A", "B"):
        for gg in range(NQ):
            if gg == 0:
                fw.scope = "f" + ph
                for jj in range(NJ):
                    front(ph, 0, jj)
            prep_tm(ph, gg)
            qstage(ph, gg)
            if gg + 1 < NQ:
                fw.scope = "f" + ph
                for jj in range(NJ):
                    front(ph, gg + 1, jj)
            pairs(ph, gg)
            wo_evac(ph, gg)
    fw.scope = "mend"
    fw.drain_chans()
    with nc.Block("main") as blkM:
        fw.flush(blkM)

    scM.close()
    fw.new_epoch()

    with contextlib.ExitStack() as pc:
        psA = lambda n, shp, dt: (pc.enter_context(nc.psum_tensor(n, shp, dt)), fw.buf(n))
        sbl = lambda n, shp, dt: (pc.enter_context(nc.sbuf_tensor(n, shp, dt)), fw.buf(n))
        PG = [psA(f"PG{k}", [128, 512], F32) for k in range(2)]
        PU = [psA(f"PU{k}", [128, 512], F32) for k in range(2)]
        PD = [psA(f"PD{k}", [128, 1024], F32) for k in range(2)]
        acc, B_acc = sbl("acc", [128, HNT * 1024], F32)
        h2l, B_h2l = sbl("h2l", [128, HTT * 4096], BF16)
        h2st = [sbl(f"h2st{k}", [128, 1024], BF16) for k in range(2)]
        wcol, B_wcol = sbl("wcol", [128, 8 * HNT], F32)
        e8, B_e8 = sbl("e8", [128, 8], F32)
        w8, B_w8 = sbl("w8", [128, 8], F32)
        rcol, B_rcol = sbl("rcol", [128, 6], F32)
        wg_b = [sbl(f"wg{k}", [128, 1024], BF16) for k in range(2 * MB)]
        wu_b = [sbl(f"wu{k}", [128, 1024], BF16) for k in range(2 * MB)]
        wd_b = [sbl(f"wd{k}", [128, 1024], BF16) for k in range(MB)]
        silu_b = [sbl(f"silu{k}", [128, 512], BF16) for k in range(2)]
        actp, B_actp = sbl("actp", [128, MB * 512], BF16)
        pdb = [PD[k][0][:, :].bitcast(BF16) for k in range(2)]
        mslots = [(sap(pdb[0], 0, [[2048, 128], [1, 128]]), PD[0][1]),
                  (sap(pdb[1], 0, [[2048, 128], [1, 128]]), PD[1][1]),
                  (sap(pdb[0], 1024, [[2048, 128], [1, 128]]), PD[0][1]),
                  (sap(pdb[1], 1024, [[2048, 128], [1, 128]]), PD[1][1])]

        fw.scope = "moe"
        for hh in range(CH):
            t0 = hh * HT
            fw.op(sy, lambda h, t0=t0: h.dma_start(
                acc[:, :].rearrange("p (i n) -> p i n", n=1024),
                sap(RS, t0 * 1024, [[1024, 128], [131072, HNT], [1, 1024]])),
                reads=[B_rs], writes=[B_acc], dma=ch_acc)
            for it in range(HNT):
                tt, sub = it // 4, it % 4
                hb, B_hb = h2st[it % 2]
                fw.op(sy, lambda h, t0=t0, it=it, hb=hb: h.dma_start(
                    hb[:, :], H2S[t0 + 128 * it:t0 + 128 * it + 128, :]),
                    reads=[B_h2s], writes=[B_hb], dma=ch_h2l[it % 2])
                for c in range(8):
                    pv, Bp = mslots[c % 4]
                    fw.op(t, lambda h, c=c, pv=pv, hb=hb: h.transpose(
                        pv, hb[:, 128 * c:128 * c + 128], ident_bf[:, :]),
                        reads=[B_hb, B_idbf], writes=[Bp])
                    fw.op(v, lambda h, c=c, pv=pv, tt=tt, sub=sub: h.tensor_copy(
                        h2l[:, 4096 * tt + 512 * c + 128 * sub:
                            4096 * tt + 512 * c + 128 * sub + 128], pv),
                        reads=[Bp], writes=[B_h2l])
            for it in range(HNT):
                tt, sub = it // 4, it % 4
                pd, B_pd = PD[it % 2]
                def rmm(h, tt=tt, sub=sub, pd=pd):
                    for cc in range(8):
                        ii = h.matmul(pd[:, 512:520],
                                      h2l[:, 4096 * tt + 512 * cc + 128 * sub:
                                          4096 * tt + 512 * cc + 128 * sub + 128],
                                      rw_sb[:, 8 * cc:8 * cc + 8],
                                      start=(cc == 0), stop=(cc == 7))
                    return ii
                fw.op(t, rmm, reads=[B_h2l, B_rw], writes=[B_pd])
                fw.op(v, lambda h, pd=pd: h.reduce_max(rcol[:, 0:1], pd[:, 512:520], AX.X),
                      reads=[B_pd], writes=[B_rcol])
                fw.op(v, lambda h: h.tensor_scalar(rcol[:, 1:2], rcol[:, 0:1], -1.0, None, ALU.mult),
                      reads=[B_rcol], writes=[B_rcol])
                fw.op(s, lambda h, pd=pd: h.activation(e8[:, :], pd[:, 512:520], AF.Exp,
                                                       bias=rcol[:, 1:2]),
                      reads=[B_pd, B_rcol], writes=[B_e8])
                fw.op(v, lambda h: h.reduce_max(rcol[:, 2:3], e8[:, :], AX.X),
                      reads=[B_e8], writes=[B_rcol])
                fw.op(v, lambda h: h.tensor_scalar(w8[:, :], e8[:, :], rcol[:, 2:3], None, ALU.is_equal),
                      reads=[B_e8, B_rcol], writes=[B_w8])
                fw.op(v, lambda h: h.scalar_tensor_tensor(w8[:, :], w8[:, :], -2.0, e8[:, :],
                                                          ALU.mult, ALU.add),
                      reads=[B_w8, B_e8], writes=[B_w8])
                fw.op(v, lambda h: h.reduce_max(rcol[:, 3:4], w8[:, :], AX.X),
                      reads=[B_w8], writes=[B_rcol])
                fw.op(v, lambda h: h.tensor_tensor(rcol[:, 4:5], rcol[:, 2:3], rcol[:, 3:4], ALU.add),
                      reads=[B_rcol], writes=[B_rcol])
                fw.op(v, lambda h: h.reciprocal(rcol[:, 4:5], rcol[:, 4:5]),
                      reads=[B_rcol], writes=[B_rcol])
                fw.op(v, lambda h: h.tensor_scalar(w8[:, :], e8[:, :], rcol[:, 3:4], None, ALU.is_ge),
                      reads=[B_e8, B_rcol], writes=[B_w8])
                fw.op(v, lambda h: h.tensor_tensor(w8[:, :], w8[:, :], e8[:, :], ALU.mult),
                      reads=[B_e8, B_w8], writes=[B_w8])
                fw.op(v, lambda h, it=it: h.tensor_scalar(
                    wcol[:, 8 * it:8 * it + 8], w8[:, :], rcol[:, 4:5], None, ALU.mult),
                    reads=[B_rcol, B_w8], writes=[B_wcol])
            for e in range(8):
                for b in range(NB):
                    par = (e * NB + b) % 2
                    for mi in range(MB):
                        mc = MB * b + mi
                        sl = par * MB + mi
                        fw.op(g, lambda h, e=e, mc=mc, sl=sl: h.dma_start(
                            wg_b[sl][0][:, :].rearrange("p (c n) -> p c n", n=128),
                            sap(W["w_gate"], e * 1024 * 3584 + 128 * mc,
                                [[3584, 128], [128 * 3584, 8], [1, 128]])),
                            writes=[wg_b[sl][1]], dma=ch_wg[sl])
                        fw.op(g, lambda h, e=e, mc=mc, sl=sl: h.dma_start(
                            wu_b[sl][0][:, :].rearrange("p (c n) -> p c n", n=128),
                            sap(W["w_up"], e * 1024 * 3584 + 128 * mc,
                                [[3584, 128], [128 * 3584, 8], [1, 128]])),
                            writes=[wu_b[sl][1]], dma=ch_wu[sl])
                        fw.op(g, lambda h, e=e, mc=mc, mi=mi: h.dma_start(
                            wd_b[mi][0][:, :],
                            sap(W["w_down"], e * 3584 * 1024 + 128 * mc * 1024,
                                [[1024, 128], [1, 1024]])),
                            writes=[wd_b[mi][1]], dma=ch_wdd[par * MB + mi])
                    for tt in range(HTT):
                        for mi in range(MB):
                            sl = par * MB + mi
                            pg, B_pg = PG[mi % 2]
                            pu, B_pu = PU[mi % 2]
                            def gmm(h, tt=tt, sl=sl, pg=pg):
                                for kc in range(8):
                                    ii = h.matmul(pg[:, :], wg_b[sl][0][:, 128 * kc:128 * kc + 128],
                                                  h2l[:, 4096 * tt + 512 * kc:4096 * tt + 512 * kc + 512],
                                                  start=(kc == 0), stop=(kc == 7))
                                return ii
                            fw.op(t, gmm, reads=[wg_b[sl][1], B_h2l], writes=[B_pg])
                            def umm(h, tt=tt, sl=sl, pu=pu):
                                for kc in range(8):
                                    ii = h.matmul(pu[:, :], wu_b[sl][0][:, 128 * kc:128 * kc + 128],
                                                  h2l[:, 4096 * tt + 512 * kc:4096 * tt + 512 * kc + 512],
                                                  start=(kc == 0), stop=(kc == 7))
                                return ii
                            fw.op(t, umm, reads=[wu_b[sl][1], B_h2l], writes=[B_pu])
                            sb_, B_sb_ = silu_b[mi % 2]
                            fw.op(s, lambda h, pg=pg, sb_=sb_: h.activation(sb_[:, :], pg[:, :], AF.Silu),
                                  reads=[B_pg], writes=[B_sb_])
                            fw.op(v, lambda h, mi=mi, pu=pu, sb_=sb_: h.tensor_tensor(
                                actp[:, 512 * mi:512 * mi + 512], sb_[:, :], pu[:, :], ALU.mult),
                                reads=[B_sb_, B_pu], writes=[B_actp])
                        for jj in range(4):
                            pd, B_pd = PD[jj % 2]
                            for mi in range(MB):
                                def dmm(h, jj=jj, pd=pd, mi=mi):
                                    h.matmul(pd[:, 0:512],
                                             actp[:, 512 * mi + 128 * jj:512 * mi + 128 * jj + 128],
                                             wd_b[mi][0][:, 0:512],
                                             start=(mi == 0), stop=(mi == MB - 1))
                                    return h.matmul(pd[:, 512:1024],
                                                    actp[:, 512 * mi + 128 * jj:512 * mi + 128 * jj + 128],
                                                    wd_b[mi][0][:, 512:1024],
                                                    start=(mi == 0), stop=(mi == MB - 1))
                                if mi == 0:
                                    fw.op(t, dmm, reads=[B_actp, wd_b[mi][1]], writes=[B_pd])
                                else:
                                    fw.op(t, dmm, reads=[B_actp, wd_b[mi][1]], wtok_only=[B_pd])
                            it = 4 * tt + jj
                            fw.op(v, lambda h, it=it, pd=pd, e=e: h.scalar_tensor_tensor(
                                acc[:, 1024 * it:1024 * it + 1024], pd[:, :],
                                wcol[:, 8 * it + e:8 * it + e + 1],
                                acc[:, 1024 * it:1024 * it + 1024],
                                ALU.mult, ALU.add),
                                reads=[B_pd, B_wcol, B_acc], writes=[B_acc])
            fw.op(sy, lambda h, t0=t0: h.dma_start(
                sap(OUT, t0 * 1024, [[1024, 128], [131072, HNT], [1, 1024]]),
                acc[:, :].rearrange("p (i n) -> p i n", n=1024)),
                reads=[B_acc], dma=ch_out)
        fw.drain_chans()
        with nc.Block("moe") as blk:
            fw.flush(blk)

    ctx.close()
    return nc


_NC_CACHE = {}


def kernel(**inputs):
    x = np.ascontiguousarray(inputs["L0_attn_x"], dtype=np.float32)
    B, S, D = x.shape
    if S not in _NC_CACHE:
        _NC_CACHE[S] = build_program(S)
    nc = _NC_CACHE[S]
    wmap = {
        "wq_w": inputs["L0_attn_wq_w"], "wq_b": inputs["L0_attn_wq_b"],
        "wk_w": inputs["L0_attn_wk_w"], "wk_b": inputs["L0_attn_wk_b"],
        "wv_w": inputs["L0_attn_wv_w"], "wv_b": inputs["L0_attn_wv_b"],
        "wo_w": inputs["L0_attn_wo_w"], "wo_b": inputs["L0_attn_wo_b"],
        "ln1_w": inputs["L0_ffn_ln1_w"], "ln2_w": inputs["L0_ffn_ln2_w"],
        "fwq": inputs["L0_ffn_wq"], "fwk": inputs["L0_ffn_wk"],
        "fwv": inputs["L0_ffn_wv"], "fwo": inputs["L0_ffn_wo"],
        "router_w": inputs["L0_ffn_router_w"], "w_gate": inputs["L0_ffn_w_gate"],
        "w_up": inputs["L0_ffn_w_up"], "w_down": inputs["L0_ffn_w_down"],
    }
    wmap = {k: np.ascontiguousarray(v, dtype=np.float32) for k, v in wmap.items()}
    in_maps = [dict(x=np.ascontiguousarray(x[i]), **wmap) for i in range(B)]
    res = run_bass_kernel_spmd(nc, in_maps, core_ids=list(range(B)))
    return np.stack([np.asarray(res.results[i]["out"]) for i in range(B)], axis=0)
